# version 01
# baseline (speedup 1.0000x reference)
import contextlib
import numpy as np
import concourse.bass as bass
import concourse.mybir as mybir
from concourse.bass_utils import run_bass_kernel_spmd

F32 = mybir.dt.float32
F32R = mybir.dt.float32r
BF16 = mybir.dt.bfloat16
FP8 = mybir.dt.float8e4
AF = mybir.ActivationFunctionType
ALU = mybir.AluOpType
AX = mybir.AxisListType

NCORES = 8
T = 1024
D = 2048
H = 16
CB = 128
NCH = T // CB
FF = 8192
KC = D // 128
TC = T // 128
EPS = 1e-5

SB_BASE = 16896
SB_TAB = SB_BASE + 0
SB_SBF = SB_BASE + 16384
SB_SCR = SB_BASE + 20480
SB_STRA = SB_BASE + 28672
SB_STRB = SB_BASE + 36864
SB_BIGA = SB_BASE + 69632
SB_BIGB = SB_BASE + 102400
SB_BIGC = SB_BASE + 135168
SB_BIGD = SB_BASE + 167936


class Prog:
    ENGS = ("sync", "scalar", "vector", "tensor", "gpsimd")

    def __init__(self, nc, es):
        self.nc = nc
        self.streams = {k: [] for k in self.ENGS}
        self.sems = {}
        self.ctr = {}
        for n in ("LD", "LG", "LS", "MM", "VE", "SE", "GE", "CC", "ST"):
            self.sems[n] = es.enter_context(nc.semaphore(n))
            self.ctr[n] = 0

    def op(self, eng, fn, wait=(), inc=None, incn=1):
        post = None
        if inc is not None:
            self.ctr[inc] += incn
            post = self.ctr[inc]
        sems = self.sems

        def run(e, fn=fn, wait=tuple(wait), inc=inc, incn=incn):
            for (n, v) in wait:
                if v > 0:
                    e.wait_ge(sems[n], v)
            r = fn(e)
            if inc is not None:
                if r is None:
                    r = e.nop()
                r.then_inc(sems[inc], incn)
        self.streams[eng].append(run)
        return post

    def raw(self, eng, fn):
        self.streams[eng].append(fn)


def build_program(upto=10, debug=(), subph=99):
    import os as _os
    V4 = int(_os.environ.get("V4", "9"))
    nc = bass.Bass("TRN2", target_bir_lowering=False)

    xin = nc.dram_tensor("xin", [T, D], F32, kind="ExternalInput").ap()
    wqkv = nc.dram_tensor("wqkv", [D, 3 * D], F32, kind="ExternalInput").ap()
    bq_cols = nc.dram_tensor("bq_cols", [128, 32], F32, kind="ExternalInput").ap()
    bq_vrow = nc.dram_tensor("bq_vrow", [1, D], F32, kind="ExternalInput").ap()
    wout = nc.dram_tensor("wout", [D, D], F32, kind="ExternalInput").ap()
    bo_row = nc.dram_tensor("bo_row", [1, D], F32, kind="ExternalInput").ap()
    slopes = nc.dram_tensor("slopes", [128, H], F32, kind="ExternalInput").ap()
    wfc_sh = nc.dram_tensor("wfc_sh", [FF // 8, D], F32, kind="ExternalInput").ap()
    wpr_sh = nc.dram_tensor("wpr_sh", [D, FF // 8], F32, kind="ExternalInput").ap()
    identb = nc.dram_tensor("identb", [128, 128], BF16, kind="ExternalInput").ap()
    ident8 = nc.dram_tensor("ident8", [128, 128], FP8, kind="ExternalInput").ap()
    bciota = nc.dram_tensor("bciota", [128, 128], F32, kind="ExternalInput").ap()
    difft = nc.dram_tensor("difft", [128, 128], F32, kind="ExternalInput").ap()
    iotac = nc.dram_tensor("iotac", [128, 1], F32, kind="ExternalInput").ap()
    dvecbc = nc.dram_tensor("dvecbc", [128, 8], F32, kind="ExternalInput").ap()

    yout = nc.dram_tensor("yout", [T, D], F32, kind="ExternalOutput").ap()

    agt_in = nc.dram_tensor("agt_in", [128, H * 128], BF16).ap()
    agt_out = nc.dram_tensor("agt_out", [NCORES * 128, H * 128], BF16).ap()
    arw_in = nc.dram_tensor("arw_in", [1, 128], F32).ap()
    arw_out = nc.dram_tensor("arw_out", [1, 128], F32).ap()
    arx_in = nc.dram_tensor("arx_in", [1, 128], F32).ap()
    arx_out = nc.dram_tensor("arx_out", [1, 128], F32).ap()
    arh_in = nc.dram_tensor("arh_in", [1, 128], F32).ap()
    arh_out = nc.dram_tensor("arh_out", [1, 128], F32).ap()
    w8f_in = nc.dram_tensor("w8f_in", [D, FF // 8], FP8).ap()
    w8f_out = nc.dram_tensor("w8f_out", [NCORES * D, FF // 8], FP8).ap()
    w8p_in = nc.dram_tensor("w8p_in", [FF // 8, D], FP8).ap()
    w8p_out = nc.dram_tensor("w8p_out", [NCORES * (FF // 8), D], FP8).ap()
    gdram = nc.dram_tensor("gdram", [FF // 128, 128, T], BF16).ap()

    def sb(name, shape, dt, off):
        return nc.alloc_sbuf_tensor_at(name, shape, dt, offset=off).ap()

    dq_t = sb("dq_t", [128, H * 128], BF16, SB_TAB + 0)
    din_t = sb("din_t", [128, H * 128], BF16, SB_TAB + 4096)
    bq_vr = sb("bq_vr", [1, D], BF16, SB_TAB + 8192)
    bo_r = sb("bo_r", [1, D], BF16, SB_TAB + 12288)
    kd_c = sb("kd_c", [128, H], F32, SB_SCR + 0)
    bd_c = sb("bd_c", [128, H], F32, SB_SCR + 64)
    sl_c = sb("sl_c", [128, H], F32, SB_SCR + 128)
    bqc = sb("bqc", [128, 32], F32, SB_SCR + 192)
    scal = sb("scal", [128, 16], F32, SB_SCR + 320)
    s11 = sb("s11", [1, 32], F32, SB_SCR + 384)
    bcw = sb("bcw", [128, 128], F32, SB_SCR + 512)
    idb = sb("idb", [128, 128], BF16, SB_SCR + 1024)
    id8 = sb("id8", [128, 128], FP8, SB_SCR + 1280)
    onesb = sb("onesb", [1, 512], BF16, SB_SCR + 1408)
    onesr_f = sb("onesr", [1, 128], F32, SB_SCR + 2432)
    onesr = sb("onesr_r", [1, 128], F32R, SB_SCR + 5760)
    s11r = sb("s11r", [1, 8], F32R, SB_SCR + 6272)
    bcio = sb("bcio", [128, 128], F32, SB_SCR + 2944)
    difft_s = sb("difft_s", [128, 128], F32, SB_SCR + 3456)
    iotac_s = sb("iotac_s", [128, 1], F32, SB_SCR + 3968)
    dvec_s = sb("dvec_s", [128, 8], F32, SB_SCR + 4000)
    mcol = sb("mcol", [128, 1], F32, SB_SCR + 4032)
    bnst = sb("bnst", [128, 24], F32, SB_SCR + 4064)
    bnag = sb("bnag", [128, 2], F32, SB_SCR + 4160)
    rstd = sb("rstd", [128, 1], F32, SB_SCR + 4192)
    qd_s = sb("qd_s", [128, 2 * 128], BF16, SB_SCR + 4224)
    st_s = sb("st_s", [128, 2 * 128], BF16, SB_SCR + 4736)
    tabtmp = sb("tabtmp", [128, 128], F32, SB_SCR + 5248)
    qtm8 = sb("qtm8", [128, D], FP8, SB_STRB + 24576)

    sbf = sb("sbf", [128, H * 128], BF16, SB_SBF)

    wqslab = [sb(f"wqslab{i}", [128, KC * 128], BF16, SB_STRA + i * 4096) for i in range(2)]
    w8psl = [sb(f"w8psl{i}", [128, 8 * 512], FP8, SB_STRA + i * 4096) for i in range(2)]
    xres = [sb(f"xres{i}", [128, 512], F32, SB_STRA + i * 2048) for i in range(2)]

    xtm = sb("xtm", [128, TC * D], BF16, SB_STRB)
    vslab = [sb(f"vslab{i}", [128, KC * 512], BF16, SB_STRB + i * 16384) for i in range(2)]
    wsh_f32 = sb("wsh_f32", [128, D], F32, SB_STRB + 0)
    wsh_f8 = sb("wsh_f8", [128, D], FP8, SB_STRB + 8192)
    wsh_stg = sb("wsh_stg", [128, 16 * 1024], FP8, SB_STRB + 10752)
    tjin = sb("tjin", [128, H * 128], BF16, SB_STRB + 27136)
    sini = sb("sini", [128, H * 128], F32, SB_STRB + 0)
    attnt = sb("attnt", [128, H * 1024], BF16, SB_STRB)
    w8slab = [sb(f"w8slab{i}", [128, KC * 512], FP8, SB_STRB + i * 8192) for i in range(2)]
    rstg = [sb(f"rstg{i}", [128, 512], BF16, SB_STRB + 16384 + i * 1024) for i in range(2)]
    lnscr2 = sb("lnscr2", [128, D], F32, SB_STRB + 0)
    g8 = sb("g8", [128, 64 * 256], FP8, SB_STRB + 16384)

    qfm = sb("qfm", [128, H * 1024], BF16, SB_BIGA)
    woslab = [sb(f"woslab{i}", [128, KC * 512], BF16, SB_BIGA + i * 16384) for i in range(2)]
    x8fm = sb("x8fm", [128, KC * T], FP8, SB_BIGA)
    rload = sb("rload", [128, 32 * 256], BF16, SB_BIGA + 16384)
    kfm = sb("kfm", [128, H * 1024], BF16, SB_BIGB)
    x2t = [sb(f"x2t{i}", [128, D], F32, SB_BIGB + 16384 + i * 8192) for i in range(2)]
    vtm = sb("vtm", [128, H * 1024], BF16, SB_BIGC)
    xfm = sb("xfm", [128, KC * T], BF16, SB_BIGD)
    kdt = sb("kdt", [128, H * 1024], BF16, SB_BIGD)
    x1 = sb("x1", [128, TC * D], F32, SB_BIGC)

    pf = [nc.alloc_psum_tensor(f"pf{i}", [128, 512], F32).ap() for i in range(6)]
    ptr = nc.alloc_psum_tensor("ptr", [128, 1024], BF16).ap()
    paux = nc.alloc_psum_tensor("paux", [128, 512], F32).ap()
    ptr8 = ptr.bitcast(FP8)
    ptr2 = paux.bitcast(BF16)
    ptr28 = paux.bitcast(FP8)

    es = contextlib.ExitStack()
    P = Prog(nc, es)
    opn = P.op
    RG = [list(range(NCORES))]

    dbg_list = []

    def dbg(name, apget, shape, dt):
        if name in debug:
            dap = nc.dram_tensor("dbg_" + name, shape, dt, kind="ExternalOutput").ap()
            dbg_list.append((dap, apget))

    for dst, src in [(idb, identb), (id8, ident8), (bcio, bciota),
                     (difft_s, difft), (iotac_s, iotac), (dvec_s, dvecbc),
                     (sl_c, slopes), (bqc, bq_cols)]:
        opn("sync", lambda e, d=dst, s=src: e.dma_start(out=d[:, :], in_=s[:, :]),
            inc="LD", incn=16)
    consts_ld = P.ctr["LD"]

    opn("gpsimd", lambda e: e.dma_start(out=bq_vr[:, :], in_=bq_vrow[:, :]),
        inc="LG", incn=16)
    opn("gpsimd", lambda e: e.dma_start(out=bo_r[:, :], in_=bo_row[:, :]),
        inc="LG", incn=16)
    for tc in range(TC):
        opn("gpsimd", lambda e, tc=tc: e.dma_start(
            out=xtm[:, tc * D:(tc + 1) * D], in_=xin[tc * 128:(tc + 1) * 128, :]),
            inc="LG", incn=16)
    x_ld = P.ctr["LG"]

    first = True
    for h in range(H):
        w = [("LD", consts_ld)] if first else [("SE", P.ctr["SE"])]
        ve = opn("vector", lambda e, h=h: e.tensor_scalar(
            tabtmp[:, :], bcio[:, :], sl_c[:, h:h + 1], -1.0, ALU.mult, ALU.mult),
            wait=w, inc="VE")
        opn("scalar", lambda e, h=h: e.activation(
            dq_t[:, h * 128:(h + 1) * 128], tabtmp[:, :], AF.Exp),
            wait=[("VE", ve)], inc="SE")
        ve = opn("vector", lambda e, h=h: e.tensor_scalar(
            tabtmp[:, :], difft_s[:, :], sl_c[:, h:h + 1], -1.0, ALU.mult, ALU.mult),
            wait=[("SE", P.ctr["SE"])], inc="VE")
        ge = opn("gpsimd", lambda e: e.affine_select(
            tabtmp[:, :], tabtmp[:, :], pattern=[[1, 128]], compare_op=ALU.is_ge,
            fill=-1e5, base=0, channel_multiplier=-1),
            wait=[("VE", ve)], inc="GE")
        opn("scalar", lambda e, h=h: e.activation(
            din_t[:, h * 128:(h + 1) * 128], tabtmp[:, :], AF.Exp),
            wait=[("GE", ge)], inc="SE")
        first = False

    def cols_v(e):
        e.memset(onesb[:, :], 1.0)
        e.memset(onesr_f[:, :], 1.0)
        for h in range(H):
            e.tensor_scalar(kd_c[:, h:h + 1], iotac_s[:, :], sl_c[:, h:h + 1],
                            None, ALU.mult)
        e.tensor_scalar(bd_c[:, :], sl_c[:, :], -128.0, None, ALU.mult)
        r = None
        for h in range(H):
            r = e.tensor_scalar(bcw[:, h * 8:(h + 1) * 8], dvec_s[:, :],
                                sl_c[:, h:h + 1], -1.0, ALU.mult, ALU.mult)
        return r
    ve = opn("vector", cols_v, wait=[("SE", P.ctr["SE"])], inc="VE")

    def cols_s(e):
        e.activation(kd_c[:, :], kd_c[:, :], AF.Exp)
        e.activation(bd_c[:, :], bd_c[:, :], AF.Exp)
        return e.activation(bcw[:, :], bcw[:, :], AF.Exp)
    tabs_se = opn("scalar", cols_s, wait=[("VE", ve)], inc="SE")

    xtr_ve = []
    n = 0
    for tc in range(TC):
        for k in range(KC):
            half = n % 2
            w = [("LG", x_ld), ("LD", consts_ld)] if n == 0 else []
            if n >= 2:
                w = [("VE", xtr_ve[n - 2])]
            mmv = opn("tensor", lambda e, tc=tc, k=k, half=half: e.transpose(
                (ptr if half == 0 else ptr2)[:, 0:128],
                xtm[:, tc * D + k * 128: tc * D + (k + 1) * 128], idb[:, :]),
                wait=w, inc="MM")
            if n % 8 == 7:
                P.raw("tensor", lambda e: e.drain())
            xtr_ve.append(opn("vector", lambda e, tc=tc, k=k, half=half: e.tensor_copy(
                xfm[:, k * T + tc * 128: k * T + (tc + 1) * 128],
                (ptr if half == 0 else ptr2)[:, 0:128]),
                wait=[("MM", mmv)], inc="VE"))
            n += 1
    xfm_ve = P.ctr["VE"]
    dbg("xfm", lambda: xfm[:, :], [128, KC * T], BF16)

    if upto >= 1:
        qk_mm = []
        qk_ve = []
        qk_ld = []

        def reg_qk_ld(i):
            w = []
            if i >= 2:
                w = [("MM", qk_mm[(i - 2) * 2 + 1])]
            qk_ld.append(opn("gpsimd", lambda e, i=i: e.dma_start(
                out=wqslab[i % 2][:, :].rearrange("p (kc fc) -> p kc fc", kc=KC),
                in_=wqkv.rearrange("(kc p) f -> p kc f", p=128)[:, :, i * 128:(i + 1) * 128]),
                wait=w, inc="LG", incn=16))

        reg_qk_ld(0)
        reg_qk_ld(1)
        gidx = 0
        for i in range(32):
            for th in range(2):
                bank = gidx % 4
                w = [("LG", qk_ld[i])]
                if gidx == 0:
                    w.append(("VE", xfm_ve))
                if gidx >= 4:
                    w.append(("VE", qk_ve[gidx - 4]))

                def mmf(e, i=i, th=th, bank=bank):
                    r = None
                    for k in range(KC):
                        r = e.matmul(pf[bank][:, :],
                                     wqslab[i % 2][:, k * 128:(k + 1) * 128],
                                     xfm[:, k * T + th * 512: k * T + th * 512 + 512],
                                     start=(k == 0), stop=(k == KC - 1))
                    return r
                qk_mm.append(opn("tensor", mmf, wait=w, inc="MM"))

                def evf(e, i=i, th=th, bank=bank):
                    dst = qfm if i < 16 else kfm
                    h = i % 16
                    return e.tensor_scalar(
                        dst[:, h * T + th * 512: h * T + th * 512 + 512],
                        pf[bank][:, :], bqc[:, i:i + 1], None, ALU.add)
                qk_ve.append(opn("vector", evf, wait=[("MM", qk_mm[-1])], inc="VE"))
                gidx += 1
            if i + 2 < 32:
                reg_qk_ld(i + 2)

        v_ld = []
        v_mm = []
        v_ve = []

        def reg_v_ld(fg):
            w = []
            if fg == 0:
                w = [("VE", xfm_ve)]
            if fg >= 2:
                w = [("MM", v_mm[(fg - 2) * TC + TC - 1])]
            v_ld.append(opn("gpsimd", lambda e, fg=fg: e.dma_start(
                out=vslab[fg % 2][:, :].rearrange("p (kc fc) -> p kc fc", kc=KC),
                in_=wqkv.rearrange("(kc p) f -> p kc f", p=128)[
                    :, :, 2 * D + fg * 512: 2 * D + (fg + 1) * 512]),
                wait=w, inc="LG", incn=16))

        reg_v_ld(0)
        reg_v_ld(1)
        gidx = 0
        for fg in range(4):
            for tc in range(TC):
                bank = gidx % 4
                w = [("LG", v_ld[fg]), ("SE", tabs_se)]
                if gidx == 0:
                    w.append(("VE", qk_ve[-1]))
                if gidx >= 4:
                    w.append(("VE", v_ve[gidx - 4]))

                def mmv(e, fg=fg, tc=tc, bank=bank):
                    for k in range(KC):
                        e.matmul(pf[bank][:, :],
                                 xfm[:, k * T + tc * 128:k * T + (tc + 1) * 128],
                                 vslab[fg % 2][:, k * 512:(k + 1) * 512],
                                 start=(k == 0), stop=False)
                    return e.matmul(pf[bank][:, :], onesb[0:1, 0:128],
                                    bq_vr[0:1, fg * 512:(fg + 1) * 512],
                                    start=False, stop=True)
                v_mm.append(opn("tensor", mmv, wait=w, inc="MM"))

                def evv(e, fg=fg, tc=tc, bank=bank):
                    src = pf[bank][:, :].rearrange("p (j e) -> p j e", j=4)
                    dst = vtm[:, :].rearrange("p (h c e) -> p h c e", h=H, c=NCH)[
                        :, fg * 4:fg * 4 + 4, tc, :]
                    return e.tensor_copy(dst, src)
                v_ve.append(opn("vector", evv, wait=[("MM", v_mm[-1])], inc="VE"))
                gidx += 1
            if fg + 2 < 4:
                reg_v_ld(fg + 2)

        ktr_ve = []
        n = 0
        for h in range(H):
            for c in range(NCH):
                half = n % 2
                w = []
                if n >= 2:
                    w = [("VE", ktr_ve[n - 2])]
                mmv = opn("tensor", lambda e, h=h, c=c, half=half: e.transpose(
                    (ptr if half == 0 else ptr2)[:, 0:128],
                    kfm[:, h * T + c * 128: h * T + (c + 1) * 128], idb[:, :]),
                    wait=w, inc="MM")
                if n % 8 == 7:
                    P.raw("tensor", lambda e: e.drain())
                ktr_ve.append(opn("vector", lambda e, h=h, c=c, half=half:
                                  e.tensor_scalar(
                                      kdt[:, (h * NCH + c) * 128:(h * NCH + c + 1) * 128],
                                      (ptr if half == 0 else ptr2)[:, 0:128],
                                      kd_c[:, h:h + 1], None, ALU.mult),
                                  wait=[("MM", mmv)], inc="VE"))
                n += 1
        kdt_ve = P.ctr["VE"]
        last_v_mm = v_mm[-1]
        dbg("qfm", lambda: qfm[:, :], [128, H * 1024], BF16)
        dbg("kfm", lambda: kfm[:, :], [128, H * 1024], BF16)
        dbg("vtm", lambda: vtm[:, :], [128, H * 1024], BF16)
        dbg("kdt", lambda: kdt[:, :], [128, H * 1024], BF16)

    if upto >= 2:
        sz_ve = opn("vector", lambda e: e.memset(sbf[:, :], 0.0), inc="VE")
        p1_mm = []
        p1_ve = []
        nidx = 0
        for h in range(H):
            for c in range(NCH):
                bank = 4 + (nidx % 2)
                w = [("VE", kdt_ve)] if nidx == 0 else []
                if nidx >= 2:
                    w.append(("VE", p1_ve[nidx - 2]))

                def smm(e, h=h, c=c, bank=bank):
                    i = h * NCH + c
                    return e.matmul(pf[bank][:, 0:128],
                                    kdt[:, i * 128:(i + 1) * 128],
                                    vtm[:, i * 128:(i + 1) * 128],
                                    start=True, stop=True)
                p1_mm.append(opn("tensor", smm, wait=w, inc="MM"))
                if nidx % 4 == 3:
                    P.raw("tensor", lambda e: e.drain())

                def sstt(e, h=h, bank=bank):
                    return e.scalar_tensor_tensor(
                        sbf[:, h * 128:(h + 1) * 128], sbf[:, h * 128:(h + 1) * 128],
                        bd_c[:, h:h + 1], pf[bank][:, 0:128], ALU.mult, ALU.add)
                p1_ve.append(opn("vector", sstt,
                                 wait=[("MM", p1_mm[-1]), ("VE", sz_ve)], inc="VE"))
                nidx += 1
        p1_done_ve = P.ctr["VE"]

        if subph < 2:
            upto = 0
        texp = None
        if subph >= 2:
            texp = opn("sync", lambda e: e.dma_start(out=agt_in[:, :], in_=sbf[:, :]),
                       wait=[("VE", p1_done_ve)], inc="LD", incn=16)
        agT = None
        if subph >= 2:
            agT = opn("gpsimd", lambda e: e.collective_compute(
                "AllGather", ALU.bypass, replica_groups=RG,
                ins=[agt_in.opt()], outs=[agt_out.opt()]),
                wait=[("LD", texp)], inc="CC")

        if subph < 3:
            upto = 0
        wsh_ld = []
        if subph >= 3:
            wsh_ve = []
            for i in range(24):
                w = [("MM", last_v_mm)] if i == 0 else [("VE", wsh_ve[i - 1])]

                def lsh(e, i=i):
                    if i < 8:
                        return e.dma_start(out=wsh_f32[:, :],
                                           in_=wfc_sh[i * 128:(i + 1) * 128, :])
                    j = i - 8
                    return e.dma_start(out=wsh_f32[:, 0:1024],
                                       in_=wpr_sh[j * 128:(j + 1) * 128, :])
                wsh_ld.append(opn("scalar", lsh, wait=w, inc="LS", incn=16))

                def msh(e, i=i):
                    width = D if i < 8 else 1024
                    col = 8 if i < 8 else 9
                    e.tensor_reduce(mcol[:, :], wsh_f32[:, 0:width], AX.X, ALU.max,
                                    apply_absolute_value=True)
                    e.drain()
                    if i == 0 or i == 8:
                        r = e.tensor_copy(scal[:, col:col + 1], mcol[:, :])
                    else:
                        r = e.tensor_tensor(scal[:, col:col + 1],
                                            scal[:, col:col + 1], mcol[:, :],
                                            ALU.max)
                    e.drain()
                    return r
                wsh_ve.append(opn("vector", msh, wait=[("LS", wsh_ld[-1])], inc="VE"))

            def gmax_f(e):
                e.tensor_reduce(scal[0:1, 8:9], scal[:, 8:9], AX.C, ALU.max)
                return e.tensor_reduce(scal[0:1, 9:10], scal[:, 9:10], AX.C, ALU.max)
            gmax = opn("gpsimd", gmax_f, wait=[("VE", wsh_ve[-1])], inc="GE")

            def stgw_f(e):
                e.dma_start(out=arw_in[0:1, 0:1], in_=scal[0:1, 8:9]).then_inc(
                    P.sems["LG"], 16)
                return e.dma_start(out=arw_in[0:1, 1:2], in_=scal[0:1, 9:10])
            opn("gpsimd", stgw_f, wait=[("GE", gmax)], inc="LG", incn=16)
            P.ctr["LG"] += 16
            stgw = P.ctr["LG"]
            arw = opn("gpsimd", lambda e: e.collective_compute(
                "AllReduce", ALU.max, replica_groups=RG,
                ins=[arw_in.opt()], outs=[arw_out.opt()]),
                wait=[("LG", stgw)], inc="CC")

            def ldw_f(e):
                e.dma_start(out=scal[:, 14:15],
                            in_=arw_out[0:1, 0:1].broadcast_to((128, 1))).then_inc(
                    P.sems["LG"], 16)
                return e.dma_start(out=scal[:, 15:16],
                                   in_=arw_out[0:1, 1:2].broadcast_to((128, 1)))
            opn("gpsimd", ldw_f, wait=[("CC", arw)], inc="LG", incn=16)
            P.ctr["LG"] += 16
            ldw = P.ctr["LG"]

            def wscal(e):
                e.tensor_scalar(scal[:, 5:7], scal[:, 14:16], 1.0 / 448.0, 1e-6,
                                ALU.mult, ALU.max)
                e.drain()
                e.reciprocal(scal[:, 0:2], scal[:, 5:7])
                e.drain()
                r = e.tensor_scalar(scal[:, 0:2], scal[:, 0:2], 0.5, None, ALU.mult)
                e.drain()
                return r
            bc_ve = opn("vector", wscal, wait=[("LG", ldw)], inc="VE")

            if subph < 6:
                upto = 0
            if subph >= 4:
                for i in range(8):
                    w = [("VE", bc_ve)] if i == 0 else [("VE", P.ctr["VE"])]
                    ldq = opn("scalar", lambda e, i=i: e.dma_start(
                        out=wsh_f32[:, :], in_=wfc_sh[i * 128:(i + 1) * 128, :]),
                        wait=w, inc="LS", incn=16)
                    qv = opn("vector", lambda e: e.tensor_scalar(
                        wsh_f8[:, :], wsh_f32[:, :], scal[:, 0:1], None, ALU.mult),
                        wait=[("LS", ldq)], inc="VE")
                    prev_tr = []
                    for k in range(KC):
                        half = k % 2
                        w2 = [("VE", qv)]
                        if k >= 2:
                            w2.append(("VE", prev_tr[k - 2]))
                        tmm = opn("tensor", lambda e, k=k, half=half: e.transpose(
                            (ptr8 if half == 0 else ptr28)[:, 0:256:2],
                            wsh_f8[:, k * 128:(k + 1) * 128], id8[:, :]),
                            wait=w2, inc="MM")
                        if k % 8 == 7:
                            P.raw("tensor", lambda e: e.drain())
                        prev_tr.append(opn("vector", lambda e, k=k, half=half, i=i:
                                           e.tensor_copy(
                                               wsh_stg[:, k * 1024 + i * 128:
                                                       k * 1024 + (i + 1) * 128],
                                               (ptr8 if half == 0 else ptr28)[:, 0:256:2]),
                                           wait=[("MM", tmm)], inc="VE"))
                stg_f = opn("sync", lambda e: e.dma_start(
                    out=w8f_in.rearrange("(kc p) u -> p kc u", p=128),
                    in_=wsh_stg[:, :].rearrange("p (kc u) -> p kc u", kc=KC)),
                    wait=[("VE", P.ctr["VE"])], inc="LD", incn=16)
                agF = opn("gpsimd", lambda e: e.collective_compute(
                    "AllGather", ALU.bypass, replica_groups=RG,
                    ins=[w8f_in.opt()], outs=[w8f_out.opt()]),
                    wait=[("LD", stg_f), ("CC", agT)], inc="CC")

            if subph >= 5:
                for i in range(16):
                    w = [("LD", stg_f)] if i == 0 else [("VE", P.ctr["VE"])]
                    ldq = opn("scalar", lambda e, i=i: e.dma_start(
                        out=wsh_f32[:, 0:1024], in_=wpr_sh[i * 128:(i + 1) * 128, :]),
                        wait=w, inc="LS", incn=16)
                    qv = opn("vector", lambda e: e.tensor_scalar(
                        wsh_f8[:, 0:1024], wsh_f32[:, 0:1024], scal[:, 1:2], None, ALU.mult),
                        wait=[("LS", ldq)], inc="VE")
                    prev_tr = []
                    for k in range(8):
                        half = k % 2
                        w2 = [("VE", qv)]
                        if k >= 2:
                            w2.append(("VE", prev_tr[k - 2]))
                        tmm = opn("tensor", lambda e, k=k, half=half: e.transpose(
                            (ptr8 if half == 0 else ptr28)[:, 0:256:2],
                            wsh_f8[:, k * 128:(k + 1) * 128], id8[:, :]),
                            wait=w2, inc="MM")
                        prev_tr.append(opn("vector", lambda e, k=k, half=half, i=i:
                                           e.tensor_copy(
                                               wsh_stg[:, k * 2048 + i * 128:
                                                       k * 2048 + (i + 1) * 128],
                                               (ptr8 if half == 0 else ptr28)[:, 0:256:2]),
                                           wait=[("MM", tmm)], inc="VE"))
                    P.raw("tensor", lambda e: e.drain())
                stg_p = opn("sync", lambda e: e.dma_start(
                    out=w8p_in.rearrange("(kc p) o -> p kc o", p=128),
                    in_=wsh_stg[:, :].rearrange("p (kc o) -> p kc o", kc=8)),
                    wait=[("VE", P.ctr["VE"])], inc="LD", incn=16)
                agP = opn("gpsimd", lambda e: e.collective_compute(
                    "AllGather", ALU.bypass, replica_groups=RG,
                    ins=[w8p_in.opt()], outs=[w8p_out.opt()]),
                    wait=[("LD", stg_p), ("CC", agF)], inc="CC")

            if subph >= 6:
                comb_last = None
                for r in range(NCORES):
                    wl = [("CC", agT)]
                    if r == 0:
                        wl.append(("VE", P.ctr["VE"]))
                    else:
                        wl.append(("VE", comb_last))
                    lt = opn("sync", lambda e, r=r: e.dma_start(
                        out=tjin[:, :], in_=agt_out[r * 128:(r + 1) * 128, :]),
                        wait=wl, inc="LD", incn=16)
                    for h in range(H):
                        def cmb(e, r=r, h=h):
                            if r == 0:
                                return e.tensor_scalar(
                                    sini[:, h * 128:(h + 1) * 128],
                                    tjin[:, h * 128:(h + 1) * 128],
                                    bcw[:, h * 8 + r:h * 8 + r + 1], None, ALU.mult)
                            return e.scalar_tensor_tensor(
                                sini[:, h * 128:(h + 1) * 128],
                                tjin[:, h * 128:(h + 1) * 128],
                                bcw[:, h * 8 + r:h * 8 + r + 1],
                                sini[:, h * 128:(h + 1) * 128], ALU.mult, ALU.add)
                        comb_last = opn("vector", cmb,
                                        wait=[("LD", lt)] if h == 0 else [], inc="VE")
                sinit_ve = opn("vector", lambda e: e.tensor_copy(sbf[:, :], sini[:, :]),
                               inc="VE")
                dbg("sinit", lambda: sbf[:, :], [128, H * 128], BF16)
    dbg("bcw", lambda: bcw[:, :], [128, 128], F32)
    dbg("kdc", lambda: kd_c[:, :], [128, H], F32)

    if upto >= 3:
        p2_sco, p2_stv, p2_qdv, p2_omm, p2_umm, p2_oev, p2_stt = ([] for _ in range(7))
        nidx = 0
        for h in range(H):
            for c in range(NCH):
                p = nidx % 2
                A, B, Cb = pf[0 + p], pf[2 + p], pf[4 + p]
                i = h * NCH + c
                w = [("VE", sinit_ve)] if nidx == 0 else []
                if nidx >= 2:
                    w.append(("VE", p2_stv[nidx - 2]))
                p2_sco.append(opn("tensor", lambda e, i=i, A=A: e.matmul(
                    A[:, 0:128], kfm[:, i * 128:(i + 1) * 128],
                    qfm[:, i * 128:(i + 1) * 128], start=True, stop=True),
                    wait=w, inc="MM"))

                w2 = [("MM", p2_sco[-1])]
                if nidx >= 2:
                    w2.append(("MM", p2_omm[nidx - 2]))
                p2_stv.append(opn("vector", lambda e, h=h, p=p, A=A:
                                  e.tensor_tensor(st_s[:, p * 128:(p + 1) * 128],
                                                  A[:, 0:128],
                                                  din_t[:, h * 128:(h + 1) * 128],
                                                  ALU.mult), wait=w2, inc="VE"))
                w3 = []
                if nidx >= 2:
                    w3.append(("MM", p2_omm[nidx - 2]))
                p2_qdv.append(opn("vector", lambda e, i=i, h=h, p=p:
                                  e.tensor_tensor(qd_s[:, p * 128:(p + 1) * 128],
                                                  qfm[:, i * 128:(i + 1) * 128],
                                                  dq_t[:, h * 128:(h + 1) * 128],
                                                  ALU.mult), wait=w3, inc="VE"))
                w4 = [("VE", p2_qdv[-1])]
                if nidx >= 2:
                    w4.append(("SE", p2_oev[nidx - 2]))
                if nidx >= 1:
                    w4.append(("VE", p2_stt[nidx - 1]))

                def mo(e, i=i, h=h, p=p, B=B):
                    e.matmul(B[:, 0:128], vtm[:, i * 128:(i + 1) * 128],
                             st_s[:, p * 128:(p + 1) * 128], start=True, stop=False)
                    return e.matmul(B[:, 0:128], sbf[:, h * 128:(h + 1) * 128],
                                    qd_s[:, p * 128:(p + 1) * 128],
                                    start=False, stop=True)
                p2_omm.append(opn("tensor", mo, wait=w4, inc="MM"))

                w5 = []
                if nidx >= 2:
                    w5.append(("VE", p2_stt[nidx - 2]))
                p2_umm.append(opn("tensor", lambda e, i=i, Cb=Cb: e.matmul(
                    Cb[:, 0:128], kdt[:, i * 128:(i + 1) * 128],
                    vtm[:, i * 128:(i + 1) * 128], start=True, stop=True),
                    wait=w5, inc="MM"))
                if nidx % 2 == 1:
                    P.raw("tensor", lambda e: e.drain())
                p2_oev.append(opn("scalar", lambda e, i=i, B=B: e.activation(
                    attnt[:, i * 128:(i + 1) * 128], B[:, 0:128], AF.Copy),
                    wait=[("MM", p2_omm[-1])], inc="SE"))
                p2_stt.append(opn("vector", lambda e, h=h, Cb=Cb:
                                  e.scalar_tensor_tensor(
                                      sbf[:, h * 128:(h + 1) * 128],
                                      sbf[:, h * 128:(h + 1) * 128],
                                      bd_c[:, h:h + 1], Cb[:, 0:128],
                                      ALU.mult, ALU.add),
                                  wait=[("MM", p2_umm[-1])], inc="VE"))
                nidx += 1
        pass2_se = P.ctr["SE"]
        pass2_ve = P.ctr["VE"]
        dbg("attnt", lambda: attnt[:, :], [128, H * 1024], BF16)

    if upto >= 4:
        op_mm = []
        op_ve = []
        wo_ld = []
        xr_ld = []

        def reg_wo_ld(og):
            w = [("VE", pass2_ve), ("SE", pass2_se)]
            if og >= 2:
                w = [("MM", op_mm[(og - 2) * TC + TC - 1])]
            wo_ld.append(opn("gpsimd", lambda e, og=og: e.dma_start(
                out=woslab[og % 2][:, :].rearrange("p (kc oc) -> p kc oc", kc=KC),
                in_=wout.rearrange("(kc p) o -> p kc o", p=128)[
                    :, :, og * 512:(og + 1) * 512]),
                wait=w, inc="LG", incn=16))

        reg_wo_ld(0)
        reg_wo_ld(1)
        gidx = 0
        for og in range(4):
            for tc in range(TC):
                bank = gidx % 4
                wxl = []
                if gidx >= 2:
                    wxl = [("VE", op_ve[gidx - 2])]
                elif og == 0 and tc == 0:
                    wxl = [("SE", pass2_se)]
                xr_ld.append(opn("sync", lambda e, og=og, tc=tc, par=gidx % 2:
                                 e.dma_start(out=xres[par][:, :],
                                             in_=xin[tc * 128:(tc + 1) * 128,
                                                     og * 512:(og + 1) * 512]),
                                 wait=wxl, inc="LD", incn=16))

                w = [("LG", wo_ld[og]), ("LD", xr_ld[-1])]
                if gidx == 0:
                    w += [("VE", pass2_ve), ("SE", pass2_se)]
                if gidx >= 4:
                    w.append(("VE", op_ve[gidx - 4]))

                def mop(e, tc=tc, og=og, bank=bank):
                    for k in range(KC):
                        e.matmul(pf[bank][:, :],
                                 attnt[:, k * T + tc * 128:k * T + (tc + 1) * 128],
                                 woslab[og % 2][:, k * 512:(k + 1) * 512],
                                 start=(k == 0), stop=False)
                    return e.matmul(pf[bank][:, :], onesb[0:1, 0:128],
                                    bo_r[0:1, og * 512:(og + 1) * 512],
                                    start=False, stop=True)
                op_mm.append(opn("tensor", mop, wait=w, inc="MM"))

                def vev(e, tc=tc, og=og, bank=bank, par=gidx % 2):
                    return e.tensor_tensor(
                        x1[:, tc * D + og * 512: tc * D + (og + 1) * 512],
                        pf[bank][:, :], xres[par][:, :], ALU.add)
                op_ve.append(opn("vector", vev, wait=[("MM", op_mm[-1])], inc="VE"))
                gidx += 1
            if og + 2 < 4:
                reg_wo_ld(og + 2)
        opj_ve = P.ctr["VE"]

        ln_last = None
        for tc in range(TC):
            def ln1(e, tc=tc):
                xt = x1[:, tc * D:(tc + 1) * D]
                e.tensor_reduce(bnag[:, 0:1], xt, AX.X, ALU.add)
                e.drain()
                e.tensor_scalar(bnag[:, 0:1], bnag[:, 0:1], 1.0 / D, None, ALU.mult)
                e.drain()
                e.tensor_scalar(x2t[0][:, :], xt, bnag[:, 0:1], None, ALU.subtract)
                e.tensor_tensor(x2t[0][:, :], x2t[0][:, :], x2t[0][:, :], ALU.mult)
                e.tensor_reduce(bnag[:, 1:2], x2t[0][:, :], AX.X, ALU.add)
                e.drain()
                r = e.tensor_scalar(rstd[:, :], bnag[:, 1:2], 1.0 / D, EPS,
                                    ALU.mult, ALU.add)
                e.drain()
                return r
            wv = [("VE", opj_ve)] if tc == 0 else [("VE", ln_last)]
            lv = opn("vector", ln1, wait=wv, inc="VE")
            sq = opn("scalar", lambda e: e.activation(rstd[:, :], rstd[:, :], AF.Sqrt),
                     wait=[("VE", lv)], inc="SE")
            rv = opn("vector", lambda e: (e.reciprocal(rstd[:, :], rstd[:, :]),
                                          e.drain())[0],
                     wait=[("SE", sq)], inc="VE")

            def nrm(e, tc=tc):
                e.drain()
                e.tensor_scalar(x1[:, tc * D:(tc + 1) * D],
                                x1[:, tc * D:(tc + 1) * D],
                                bnag[:, 0:1], rstd[:, :], ALU.subtract, ALU.mult)
                e.tensor_reduce(mcol[:, :], x1[:, tc * D:(tc + 1) * D], AX.X,
                                ALU.max, apply_absolute_value=True)
                e.drain()
                if tc == 0:
                    r = e.tensor_copy(scal[:, 10:11], mcol[:, :])
                else:
                    r = e.tensor_tensor(scal[:, 10:11], scal[:, 10:11], mcol[:, :],
                                        ALU.max)
                e.drain()
                return r
            ln_last = opn("vector", nrm, wait=[("VE", rv)], inc="VE")
        ln1_ve = P.ctr["VE"]
        dbg("x1", lambda: x1[:, :], [128, TC * D], F32)

        gx = opn("gpsimd", lambda e: e.tensor_reduce(
            scal[0:1, 10:11], scal[:, 10:11], AX.C, ALU.max),
            wait=[("VE", ln1_ve)], inc="GE")
        sgx = opn("gpsimd", lambda e: e.dma_start(out=arx_in[0:1, 0:1],
                                                  in_=scal[0:1, 10:11]),
                  wait=[("GE", gx)], inc="LG", incn=16)
        arx = opn("gpsimd", lambda e: e.collective_compute(
            "AllReduce", ALU.max, replica_groups=RG,
            ins=[arx_in.opt()], outs=[arx_out.opt()]),
            wait=[("LG", sgx), ("CC", agP)], inc="CC")
        ldx = opn("gpsimd", lambda e: e.dma_start(
            out=scal[:, 14:15], in_=arx_out[0:1, 0:1].broadcast_to((128, 1))),
                  wait=[("CC", arx)], inc="LG", incn=16)

        def xscal(e):
            e.tensor_scalar(scal[:, 7:8], scal[:, 14:15], 1.0 / 448.0, 1e-6,
                            ALU.mult, ALU.max)
            e.drain()
            e.reciprocal(scal[:, 2:3], scal[:, 7:8])
            e.drain()
            e.tensor_scalar(scal[:, 2:3], scal[:, 2:3], 0.5, None, ALU.mult)
            e.tensor_tensor(scal[:, 13:14], scal[:, 7:8], scal[:, 5:6], ALU.mult)
            e.drain()
            r = e.tensor_scalar(scal[:, 13:14], scal[:, 13:14], 4.0, None,
                                ALU.mult)
            e.drain()
            return r
        bcx_ve = opn("vector", xscal, wait=[("LG", ldx)], inc="VE")

    if upto >= 5:
        x8_ve = []
        n = 0
        for tc in range(TC):
            wq = [("VE", bcx_ve)] if tc == 0 else [("VE", P.ctr["VE"])]
            qv = opn("vector", lambda e, tc=tc: e.tensor_scalar(
                qtm8[:, :], x1[:, tc * D:(tc + 1) * D], scal[:, 2:3], None, ALU.mult),
                wait=wq, inc="VE")
            for k in range(KC):
                half = n % 2
                w = [("VE", qv)]
                if n >= 2:
                    w.append(("VE", x8_ve[n - 2]))
                tmm = opn("tensor", lambda e, k=k, half=half: e.transpose(
                    (ptr8 if half == 0 else ptr28)[:, 0:256:2],
                    qtm8[:, k * 128:(k + 1) * 128], id8[:, :]),
                    wait=w, inc="MM")
                if n % 8 == 7:
                    P.raw("tensor", lambda e: e.drain())
                x8_ve.append(opn("vector", lambda e, k=k, tc=tc, half=half:
                                 e.tensor_copy(
                                     x8fm[:, k * T + tc * 128:k * T + (tc + 1) * 128],
                                     (ptr8 if half == 0 else ptr28)[:, 0:256:2]),
                                 wait=[("MM", tmm)], inc="VE"))
                n += 1
        x8_done = P.ctr["VE"]
        dbg("x8fm", lambda: x8fm[:, :], [128, KC * T], FP8)

    if upto >= 6:
        fc_ld, fc_mm, fc_se, fc_st, fc_vmax = [], [], [], [], []

        def reg_fc_ld(ug):
            w = [("CC", agF)]
            if ug == 0:
                w += [("MM", op_mm[-1]), ("SE", pass2_se)]
            if ug >= 2:
                w.append(("MM", fc_mm[(ug - 2) * 8 + 7]))

            def lfc(e, ug=ug):
                r, lh = ug // 2, (ug % 2) * 512
                src = w8f_out.rearrange("(r kc p) u -> p r kc u", r=NCORES, kc=KC)[
                    :, r, :, lh:lh + 512]
                return e.dma_start(
                    out=w8slab[ug % 2][:, :].rearrange("p (kc u) -> p kc u", kc=KC),
                    in_=src)
            fc_ld.append(opn("sync", lfc, wait=w, inc="LD", incn=16))

        reg_fc_ld(0)
        reg_fc_ld(1)
        for ug in range(16):
            for uc4 in range(4):
                for th in range(2):
                    gi = len(fc_mm)
                    bank = gi % 4
                    w2 = [("LD", fc_ld[ug]), ("VE", x8_done)]
                    if gi >= 4:
                        w2.append(("SE", fc_se[gi - 4]))

                    def mfc(e, ug=ug, uc4=uc4, th=th, bank=bank):
                        r = None
                        for k in range(KC):
                            r = e.matmul(pf[bank][:, :],
                                         w8slab[ug % 2][:, k * 512 + uc4 * 128:
                                                        k * 512 + (uc4 + 1) * 128],
                                         x8fm[:, k * T + th * 512:
                                              k * T + th * 512 + 512],
                                         start=(k == 0), stop=(k == KC - 1))
                        return r
                    fc_mm.append(opn("tensor", mfc, wait=w2, inc="MM"))

                    w3 = [("MM", fc_mm[-1])]
                    if gi >= 2:
                        w3 += [("ST", fc_st[gi - 2]), ("VE", fc_vmax[gi - 2])]
                    fc_se.append(opn("scalar", lambda e, bank=bank, gi=gi:
                                     e.activation(rstg[gi % 2][:, :], pf[bank][:, :],
                                                  AF.Relu), wait=w3, inc="SE"))

                    def vmx(e, gi=gi):
                        e.tensor_reduce(mcol[:, :], rstg[gi % 2][:, :], AX.X,
                                        ALU.max, apply_absolute_value=True)
                        e.drain()
                        if gi == 0:
                            r = e.tensor_copy(scal[:, 11:12], mcol[:, :])
                        else:
                            r = e.tensor_tensor(scal[:, 11:12], scal[:, 11:12],
                                                mcol[:, :], ALU.max)
                        e.drain()
                        return r
                    fc_vmax.append(opn("vector", vmx, wait=[("SE", fc_se[-1])],
                                       inc="VE"))
                    uc = ug * 4 + uc4
                    fc_st.append(opn("sync", lambda e, uc=uc, th=th, gi=gi:
                                     e.dma_start(out=gdram[uc, :, th * 512:(th + 1) * 512],
                                                 in_=rstg[gi % 2][:, :]),
                                     wait=[("SE", fc_se[-1])], inc="ST", incn=16))
            if ug + 2 < 16:
                reg_fc_ld(ug + 2)
        fc_done_st = P.ctr["ST"]
        fc_done_mm = fc_mm[-1]

        gh = opn("gpsimd", lambda e: e.tensor_reduce(
            scal[0:1, 11:12], scal[:, 11:12], AX.C, ALU.max),
            wait=[("VE", P.ctr["VE"])], inc="GE")
        sgh = opn("gpsimd", lambda e: e.dma_start(out=arh_in[0:1, 0:1],
                                                  in_=scal[0:1, 11:12]),
                  wait=[("GE", gh)], inc="LG", incn=16)
        arh = opn("gpsimd", lambda e: e.collective_compute(
            "AllReduce", ALU.max, replica_groups=RG,
            ins=[arh_in.opt()], outs=[arh_out.opt()]),
            wait=[("LG", sgh)], inc="CC")
        ldh = opn("gpsimd", lambda e: e.dma_start(
            out=scal[:, 14:15], in_=arh_out[0:1, 0:1].broadcast_to((128, 1))),
            wait=[("CC", arh)], inc="LG", incn=16)

        def hscal(e):
            e.tensor_tensor(scal[:, 15:16], scal[:, 13:14], scal[:, 14:15], ALU.mult)
            e.drain()
            e.tensor_tensor(scal[:, 15:16], scal[:, 15:16], scal[:, 15:16], ALU.mult)
            e.drain()
            e.tensor_scalar(scal[:, 12:13], scal[:, 15:16], 1.0 / 448.0, 1e-6,
                            ALU.mult, ALU.max)
            e.tensor_tensor(scal[:, 15:16], scal[:, 13:14], scal[:, 13:14], ALU.mult)
            e.drain()
            e.reciprocal(scal[:, 3:4], scal[:, 12:13])
            e.drain()
            e.tensor_tensor(scal[:, 3:4], scal[:, 3:4], scal[:, 15:16], ALU.mult)
            e.drain()
            e.tensor_scalar(scal[:, 3:4], scal[:, 3:4], 0.5, None, ALU.mult)
            e.tensor_tensor(scal[:, 4:5], scal[:, 12:13], scal[:, 6:7], ALU.mult)
            e.drain()
            r = e.tensor_scalar(scal[:, 4:5], scal[:, 4:5], 4.0, None, ALU.mult)
            e.drain()
            return r
        bch_ve = opn("vector", hscal, wait=[("LG", ldh)], inc="VE")
        if "rg" in debug:
            rg_out = nc.dram_tensor("dbg_rg", [4 * 128, T], BF16,
                                    kind="ExternalOutput").ap()
            for uu, uc in enumerate((0, 1, 31, 63)):
                opn("sync", lambda e, uu=uu, uc=uc: e.dma_start(
                    out=rg_out[uu * 128:(uu + 1) * 128, :], in_=gdram[uc, :, :]),
                    wait=[("ST", fc_done_st)], inc="ST", incn=16)
        dbg("scal", lambda: scal[:, :], [128, 16], F32)

    if upto >= 7:
        prev_pair_mm = None
        for pr in range(4):
            gq_ve = []
            for hfl in range(2):
                lds = []
                for uo in range(32):
                    uc = hfl * 32 + uo
                    w = []
                    if pr == 0 and hfl == 0 and uo == 0:
                        w = [("ST", fc_done_st), ("VE", bch_ve), ("MM", fc_done_mm)]
                    elif uo == 0:
                        if hfl == 1:
                            w = [("VE", gq_ve[-1])]
                        elif prev_pair_mm is not None:
                            w = [("MM", prev_pair_mm)]
                    lds.append(opn("sync", lambda e, uc=uc, pr=pr, uo=uo: e.dma_start(
                        out=rload[:, uo * 256:(uo + 1) * 256],
                        in_=gdram[uc, :, pr * 256:(pr + 1) * 256]),
                        wait=w, inc="LD", incn=16))

                def gq(e, hfl=hfl):
                    r = None
                    for uo in range(32):
                        uc = hfl * 32 + uo
                        r = e.scalar_tensor_tensor(
                            g8[:, uc * 256:(uc + 1) * 256],
                            rload[:, uo * 256:(uo + 1) * 256], scal[:, 3:4],
                            rload[:, uo * 256:(uo + 1) * 256], ALU.mult, ALU.mult)
                    return r
                wq = [("LD", lds[-1])]
                if pr > 0 and hfl == 0 and prev_pair_mm is not None:
                    wq.append(("MM", prev_pair_mm))
                gq_ve.append(opn("vector", gq, wait=wq, inc="VE"))

            pr_mm = []
            pr_ev = []
            for og in range(4):
                for us in range(8):
                    wl = [("VE", gq_ve[-1])]
                    if len(pr_mm) >= 2:
                        wl = [("MM", pr_mm[-2])]
                    sli = opn("sync", lambda e, og=og, us=us: e.dma_start(
                        out=w8psl[us % 2][:, :].rearrange("p (u o) -> p u o", u=8),
                        in_=w8p_out.rearrange("(u p) o -> p u o", p=128)[
                            :, us * 8:(us + 1) * 8, og * 512:(og + 1) * 512]),
                        wait=wl, inc="LD", incn=16)

                    wm = [("LD", sli), ("VE", gq_ve[-1])]
                    if og >= 2 and us == 0:
                        wm.append(("VE", pr_ev[(og - 2) * 2 + 1]))

                    def mpr(e, og=og, us=us):
                        r = None
                        for j in range(8):
                            uc = us * 8 + j
                            for half in range(2):
                                r = e.matmul(pf[(og % 2) * 2 + half][:, :],
                                             g8[:, uc * 256 + half * 128:
                                                uc * 256 + half * 128 + 128],
                                             w8psl[us % 2][:, j * 512:(j + 1) * 512],
                                             start=(us == 0 and j == 0),
                                             stop=(us == 7 and j == 7))
                        return r
                    pr_mm.append(opn("tensor", mpr, wait=wm, inc="MM"))
                for half in range(2):
                    tc = pr * 2 + half

                    def pev(e, og=og, half=half, tc=tc):
                        return e.scalar_tensor_tensor(
                            x2t[half][:, og * 512:(og + 1) * 512],
                            pf[(og % 2) * 2 + half][:, :], scal[:, 4:5],
                            x1[:, tc * D + og * 512: tc * D + (og + 1) * 512],
                            ALU.mult, ALU.add)
                    pr_ev.append(opn("vector", pev, wait=[("MM", pr_mm[-1])],
                                     inc="VE"))
            prev_pair_mm = pr_mm[-1]

            for half in range(2):
                tc = pr * 2 + half

                def ln2(e, half=half):
                    xt = x2t[half][:, :]
                    e.tensor_reduce(bnag[:, 0:1], xt, AX.X, ALU.add)
                    e.drain()
                    e.tensor_scalar(bnag[:, 0:1], bnag[:, 0:1], 1.0 / D, None,
                                    ALU.mult)
                    e.drain()
                    e.tensor_scalar(lnscr2[:, :], xt, bnag[:, 0:1], None,
                                    ALU.subtract)
                    e.tensor_tensor(lnscr2[:, :], lnscr2[:, :], lnscr2[:, :],
                                    ALU.mult)
                    e.tensor_reduce(bnag[:, 1:2], lnscr2[:, :], AX.X, ALU.add)
                    e.drain()
                    r = e.tensor_scalar(rstd[:, :], bnag[:, 1:2], 1.0 / D, EPS,
                                        ALU.mult, ALU.add)
                    e.drain()
                    return r
                lv = opn("vector", ln2, wait=[("VE", pr_ev[-1])], inc="VE")
                sq2 = opn("scalar", lambda e: e.activation(rstd[:, :], rstd[:, :],
                                                           AF.Sqrt),
                          wait=[("VE", lv)], inc="SE")
                rv2 = opn("vector", lambda e: (e.reciprocal(rstd[:, :], rstd[:, :]),
                                               e.drain())[0],
                          wait=[("SE", sq2)], inc="VE")
                nv = opn("vector", lambda e, half=half: e.tensor_scalar(
                    x2t[half][:, :], x2t[half][:, :], bnag[:, 0:1], rstd[:, :],
                    ALU.subtract, ALU.mult), wait=[("VE", rv2)], inc="VE")
                opn("sync", lambda e, tc=tc, half=half: e.dma_start(
                    out=yout[tc * 128:(tc + 1) * 128, :], in_=x2t[half][:, :]),
                    wait=[("VE", nv)], inc="ST", incn=16)

    P.raw("sync", lambda e, v=P.ctr["VE"]: e.wait_ge(P.sems["VE"], v))
    P.raw("sync", lambda e, v=P.ctr["SE"]: e.wait_ge(P.sems["SE"], v))
    P.raw("sync", lambda e, v=P.ctr["MM"]: e.wait_ge(P.sems["MM"], v))
    P.raw("sync", lambda e, v=P.ctr["GE"]: e.wait_ge(P.sems["GE"], v))

    def fin(e):
        for (dap, apget) in dbg_list:
            e.dma_start(out=dap[:, :], in_=apget()).then_inc(P.sems["ST"], 16)
            P.ctr["ST"] += 16
        if upto < 7:
            e.dma_start(out=yout[0:128, :], in_=x1[:, 0:D]).then_inc(P.sems["ST"], 16)
            P.ctr["ST"] += 16
        e.wait_ge(P.sems["ST"], P.ctr["ST"])
    P.raw("sync", fin)

    with nc.Block() as block:
        @block.sync
        def _(e):
            for f in P.streams["sync"]:
                f(e)

        @block.gpsimd
        def _(e):
            for f in P.streams["gpsimd"]:
                f(e)

        @block.vector
        def _(e):
            for f in P.streams["vector"]:
                f(e)

        @block.scalar
        def _(e):
            for f in P.streams["scalar"]:
                f(e)

        @block.tensor
        def _(e):
            for f in P.streams["tensor"]:
                f(e)

    es.close()
    return nc


def make_consts(core_id):
    import ml_dtypes
    i = np.arange(128, dtype=np.float32)
    return {
        "identb": np.eye(128).astype(ml_dtypes.bfloat16),
        "ident8": np.eye(128).astype(ml_dtypes.float8_e4m3fn),
        "bciota": np.ascontiguousarray(np.tile(i[None, :], (128, 1))),
        "difft": np.ascontiguousarray(i[None, :] - i[:, None]),
        "iotac": np.ascontiguousarray((i - 128.0)[:, None]),
        "dvecbc": np.ascontiguousarray(np.tile(
            np.where(np.arange(8) < core_id,
                     1024.0 * (core_id - 1 - np.arange(8)),
                     1e9).astype(np.float32)[None, :], (128, 1))),
    }


def kernel(L0_attn_x, L0_attn_Wqkv, L0_attn_bqkv, L0_attn_Wout, L0_attn_bout,
           L0_attn_slopes, L0_ffn_w_fc, L0_ffn_w_proj):
    x = np.ascontiguousarray(np.asarray(L0_attn_x, dtype=np.float32).reshape(8192, 2048))
    wqkv = np.ascontiguousarray(np.asarray(L0_attn_Wqkv, dtype=np.float32))
    bq = np.asarray(L0_attn_bqkv, dtype=np.float32)
    wout_a = np.ascontiguousarray(np.asarray(L0_attn_Wout, dtype=np.float32))
    bo = np.asarray(L0_attn_bout, dtype=np.float32)
    wfc = np.ascontiguousarray(np.asarray(L0_ffn_w_fc, dtype=np.float32))
    wpr = np.ascontiguousarray(np.asarray(L0_ffn_w_proj, dtype=np.float32))
    slopes_t = np.ascontiguousarray(
        np.tile(np.asarray(L0_attn_slopes, dtype=np.float32)[None, :], (128, 1)))
    bq_cols = np.ascontiguousarray(bq[:4096].reshape(32, 128).T)
    nc = build_program(upto=7)
    in_maps = []
    for c in range(NCORES):
        m = dict(
            xin=np.ascontiguousarray(x[c * T:(c + 1) * T]),
            wqkv=wqkv,
            bq_cols=bq_cols,
            bq_vrow=np.ascontiguousarray(bq[4096:].reshape(1, 2048)),
            wout=wout_a,
            bo_row=np.ascontiguousarray(bo.reshape(1, 2048)),
            slopes=slopes_t,
            wfc_sh=np.ascontiguousarray(wfc[c * (FF // 8):(c + 1) * (FF // 8)]),
            wpr_sh=np.ascontiguousarray(wpr[:, c * (FF // 8):(c + 1) * (FF // 8)]),
        )
        m.update(make_consts(c))
        in_maps.append(m)
    res = run_bass_kernel_spmd(nc, in_maps, core_ids=list(range(NCORES)))
    out = np.concatenate([r["yout"] for r in res.results], axis=0)
    return np.ascontiguousarray(out.reshape(1, 8192, 2048))


# version 02
# speedup vs baseline: 1.0312x; 1.0312x over previous
import contextlib
import numpy as np
import concourse.bass as bass
import concourse.mybir as mybir
from concourse.bass_utils import run_bass_kernel_spmd

F32 = mybir.dt.float32
F32R = mybir.dt.float32r
BF16 = mybir.dt.bfloat16
FP8 = mybir.dt.float8e4
AF = mybir.ActivationFunctionType
ALU = mybir.AluOpType
AX = mybir.AxisListType

NCORES = 8
T = 1024
D = 2048
H = 16
CB = 128
NCH = T // CB
FF = 8192
KC = D // 128
TC = T // 128
EPS = 1e-5

SB_BASE = 16896
SB_TAB = SB_BASE + 0
SB_SBF = SB_BASE + 16384
SB_SCR = SB_BASE + 20480
SB_STRA = SB_BASE + 28672
SB_STRB = SB_BASE + 36864
SB_BIGA = SB_BASE + 69632
SB_BIGB = SB_BASE + 102400
SB_BIGC = SB_BASE + 135168
SB_BIGD = SB_BASE + 167936


class Prog:
    ENGS = ("sync", "scalar", "vector", "tensor", "gpsimd")

    def __init__(self, nc, es):
        self.nc = nc
        self.streams = {k: [] for k in self.ENGS}
        self.sems = {}
        self.ctr = {}
        for n in ("LD", "LG", "LS", "MM", "VE", "SE", "GE", "CC", "ST"):
            self.sems[n] = es.enter_context(nc.semaphore(n))
            self.ctr[n] = 0

    def op(self, eng, fn, wait=(), inc=None, incn=1):
        post = None
        if inc is not None:
            self.ctr[inc] += incn
            post = self.ctr[inc]
        sems = self.sems

        def run(e, fn=fn, wait=tuple(wait), inc=inc, incn=incn):
            for (n, v) in wait:
                if v > 0:
                    e.wait_ge(sems[n], v)
            r = fn(e)
            if inc is not None:
                if r is None:
                    r = e.nop()
                r.then_inc(sems[inc], incn)
        self.streams[eng].append(run)
        return post

    def raw(self, eng, fn):
        self.streams[eng].append(fn)


def build_program(upto=10, debug=(), subph=99):
    import os as _os
    V4 = int(_os.environ.get("V4", "9"))
    nc = bass.Bass("TRN2", target_bir_lowering=False)

    xin = nc.dram_tensor("xin", [T, D], F32, kind="ExternalInput").ap()
    wqkv = nc.dram_tensor("wqkv", [D, 3 * D], F32, kind="ExternalInput").ap()
    bq_cols = nc.dram_tensor("bq_cols", [128, 32], F32, kind="ExternalInput").ap()
    bq_vrow = nc.dram_tensor("bq_vrow", [1, D], F32, kind="ExternalInput").ap()
    wout = nc.dram_tensor("wout", [D, D], F32, kind="ExternalInput").ap()
    bo_row = nc.dram_tensor("bo_row", [1, D], F32, kind="ExternalInput").ap()
    slopes = nc.dram_tensor("slopes", [128, H], F32, kind="ExternalInput").ap()
    wfc_sh = nc.dram_tensor("wfc_sh", [FF // 8, D], F32, kind="ExternalInput").ap()
    wpr_sh = nc.dram_tensor("wpr_sh", [D, FF // 8], F32, kind="ExternalInput").ap()
    identb = nc.dram_tensor("identb", [128, 128], BF16, kind="ExternalInput").ap()
    ident8 = nc.dram_tensor("ident8", [128, 128], FP8, kind="ExternalInput").ap()
    bciota = nc.dram_tensor("bciota", [128, 128], F32, kind="ExternalInput").ap()
    difft = nc.dram_tensor("difft", [128, 128], F32, kind="ExternalInput").ap()
    iotac = nc.dram_tensor("iotac", [128, 1], F32, kind="ExternalInput").ap()
    dvecbc = nc.dram_tensor("dvecbc", [128, 8], F32, kind="ExternalInput").ap()

    yout = nc.dram_tensor("yout", [T, D], F32, kind="ExternalOutput").ap()

    agt_in = nc.dram_tensor("agt_in", [128, H * 128], BF16).ap()
    agt_out = nc.dram_tensor("agt_out", [NCORES * 128, H * 128], BF16).ap()
    arw_in = nc.dram_tensor("arw_in", [1, 128], F32).ap()
    arw_out = nc.dram_tensor("arw_out", [1, 128], F32).ap()
    arx_in = nc.dram_tensor("arx_in", [1, 128], F32).ap()
    arx_out = nc.dram_tensor("arx_out", [1, 128], F32).ap()
    arh_in = nc.dram_tensor("arh_in", [1, 128], F32).ap()
    arh_out = nc.dram_tensor("arh_out", [1, 128], F32).ap()
    w8f_in = nc.dram_tensor("w8f_in", [D, FF // 8], FP8).ap()
    w8f_out = nc.dram_tensor("w8f_out", [NCORES * D, FF // 8], FP8).ap()
    w8p_in = nc.dram_tensor("w8p_in", [FF // 8, D], FP8).ap()
    w8p_out = nc.dram_tensor("w8p_out", [NCORES * (FF // 8), D], FP8).ap()
    gdram = nc.dram_tensor("gdram", [FF // 128, 128, T], BF16).ap()

    def sb(name, shape, dt, off):
        return nc.alloc_sbuf_tensor_at(name, shape, dt, offset=off).ap()

    dq_t = sb("dq_t", [128, H * 128], BF16, SB_TAB + 0)
    din_t = sb("din_t", [128, H * 128], BF16, SB_TAB + 4096)
    bq_vr = sb("bq_vr", [1, D], BF16, SB_TAB + 8192)
    bo_r = sb("bo_r", [1, D], BF16, SB_TAB + 12288)
    kd_c = sb("kd_c", [128, H], F32, SB_SCR + 0)
    bd_c = sb("bd_c", [128, H], F32, SB_SCR + 64)
    sl_c = sb("sl_c", [128, H], F32, SB_SCR + 128)
    bqc = sb("bqc", [128, 32], F32, SB_SCR + 192)
    scal = sb("scal", [128, 16], F32, SB_SCR + 320)
    s11 = sb("s11", [1, 32], F32, SB_SCR + 384)
    bcw = sb("bcw", [128, 128], F32, SB_SCR + 512)
    idb = sb("idb", [128, 128], BF16, SB_SCR + 1024)
    id8 = sb("id8", [128, 128], FP8, SB_SCR + 1280)
    onesb = sb("onesb", [1, 512], BF16, SB_SCR + 1408)
    onesr_f = sb("onesr", [1, 128], F32, SB_SCR + 2432)
    onesr = sb("onesr_r", [1, 128], F32R, SB_SCR + 5760)
    s11r = sb("s11r", [1, 8], F32R, SB_SCR + 6272)
    bcio = sb("bcio", [128, 128], F32, SB_SCR + 2944)
    difft_s = sb("difft_s", [128, 128], F32, SB_SCR + 3456)
    iotac_s = sb("iotac_s", [128, 1], F32, SB_SCR + 3968)
    dvec_s = sb("dvec_s", [128, 8], F32, SB_SCR + 4000)
    mcol = sb("mcol", [128, 1], F32, SB_SCR + 4032)
    bnst = sb("bnst", [128, 24], F32, SB_SCR + 4064)
    bnag = sb("bnag", [128, 2], F32, SB_SCR + 4160)
    rstd = sb("rstd", [128, 1], F32, SB_SCR + 4192)
    qd_s = sb("qd_s", [128, 2 * 128], BF16, SB_SCR + 4224)
    st_s = sb("st_s", [128, 2 * 128], BF16, SB_SCR + 4736)
    tabtmp = sb("tabtmp", [128, 128], F32, SB_SCR + 5248)
    qtm8 = sb("qtm8", [128, D], FP8, SB_STRB + 24576)

    sbf = sb("sbf", [128, H * 128], BF16, SB_SBF)

    wqslab = [sb(f"wqslab{i}", [128, KC * 128], BF16, SB_STRA + i * 4096) for i in range(2)]
    w8psl = [sb(f"w8psl{i}", [128, 8 * 512], FP8, SB_STRA + i * 4096) for i in range(2)]
    xres = [sb(f"xres{i}", [128, 512], F32, SB_STRA + i * 2048) for i in range(2)]

    xtm = sb("xtm", [128, TC * D], BF16, SB_STRB)
    vslab = [sb(f"vslab{i}", [128, KC * 512], BF16, SB_STRB + i * 16384) for i in range(2)]
    wsh_f32 = sb("wsh_f32", [128, D], F32, SB_STRB + 0)
    wsh_f32b = sb("wsh_f32b", [128, D], F32, SB_STRB + 8192)
    wsh_f8 = sb("wsh_f8", [128, D], FP8, SB_STRB + 8192)
    wsh_stg = sb("wsh_stg", [128, 16 * 1024], FP8, SB_STRB + 10752)
    tjin = sb("tjin", [128, H * 128], BF16, SB_STRB + 27136)
    sini = sb("sini", [128, H * 128], F32, SB_STRB + 0)
    attnt = sb("attnt", [128, H * 1024], BF16, SB_STRB)
    w8slab = [sb(f"w8slab{i}", [128, KC * 512], FP8, SB_STRB + i * 8192) for i in range(2)]
    rstg = [sb(f"rstg{i}", [128, 512], BF16, SB_STRB + 16384 + i * 1024) for i in range(2)]
    lnscr2 = sb("lnscr2", [128, D], F32, SB_BIGA + 0)
    g8ab = [sb("g8a", [128, 64 * 256], FP8, SB_STRB + 16384),
            sb("g8b", [128, 64 * 256], FP8, SB_STRB + 0)]

    qfm = sb("qfm", [128, H * 1024], BF16, SB_BIGA)
    woslab = [sb(f"woslab{i}", [128, KC * 512], BF16, SB_BIGA + i * 16384) for i in range(2)]
    x8fm = sb("x8fm", [128, KC * T], FP8, SB_BIGA)
    rload = sb("rload", [128, 32 * 256], BF16, SB_BIGA + 16384)
    kfm = sb("kfm", [128, H * 1024], BF16, SB_BIGB)
    x2t = [sb(f"x2t{i}", [128, D], F32, SB_BIGB + 16384 + i * 8192) for i in range(2)]
    vtm = sb("vtm", [128, H * 1024], BF16, SB_BIGC)
    xfm = sb("xfm", [128, KC * T], BF16, SB_BIGD)
    kdt = sb("kdt", [128, H * 1024], BF16, SB_BIGD)
    x1 = sb("x1", [128, TC * D], F32, SB_BIGC)

    pf = [nc.alloc_psum_tensor(f"pf{i}", [128, 512], F32).ap() for i in range(6)]
    ptr = nc.alloc_psum_tensor("ptr", [128, 1024], BF16).ap()
    paux = nc.alloc_psum_tensor("paux", [128, 512], F32).ap()
    ptr8 = ptr.bitcast(FP8)
    ptr2 = paux.bitcast(BF16)
    ptr28 = paux.bitcast(FP8)

    es = contextlib.ExitStack()
    P = Prog(nc, es)
    opn = P.op
    RG = [list(range(NCORES))]

    dbg_list = []

    def dbg(name, apget, shape, dt):
        if name in debug:
            dap = nc.dram_tensor("dbg_" + name, shape, dt, kind="ExternalOutput").ap()
            dbg_list.append((dap, apget))

    for dst, src in [(idb, identb), (id8, ident8), (bcio, bciota),
                     (difft_s, difft), (iotac_s, iotac), (dvec_s, dvecbc),
                     (sl_c, slopes), (bqc, bq_cols)]:
        opn("sync", lambda e, d=dst, s=src: e.dma_start(out=d[:, :], in_=s[:, :]),
            inc="LD", incn=16)
    consts_ld = P.ctr["LD"]

    opn("gpsimd", lambda e: e.dma_start(out=bq_vr[:, :], in_=bq_vrow[:, :]),
        inc="LG", incn=16)
    opn("gpsimd", lambda e: e.dma_start(out=bo_r[:, :], in_=bo_row[:, :]),
        inc="LG", incn=16)
    for tc in range(TC):
        opn("gpsimd", lambda e, tc=tc: e.dma_start(
            out=xtm[:, tc * D:(tc + 1) * D], in_=xin[tc * 128:(tc + 1) * 128, :]),
            inc="LG", incn=16)
    x_ld = P.ctr["LG"]

    first = True
    for h in range(H):
        w = [("LD", consts_ld)] if first else [("SE", P.ctr["SE"])]
        ve = opn("vector", lambda e, h=h: e.tensor_scalar(
            tabtmp[:, :], bcio[:, :], sl_c[:, h:h + 1], -1.0, ALU.mult, ALU.mult),
            wait=w, inc="VE")
        opn("scalar", lambda e, h=h: e.activation(
            dq_t[:, h * 128:(h + 1) * 128], tabtmp[:, :], AF.Exp),
            wait=[("VE", ve)], inc="SE")
        ve = opn("vector", lambda e, h=h: e.tensor_scalar(
            tabtmp[:, :], difft_s[:, :], sl_c[:, h:h + 1], -1.0, ALU.mult, ALU.mult),
            wait=[("SE", P.ctr["SE"])], inc="VE")
        ge = opn("gpsimd", lambda e: e.affine_select(
            tabtmp[:, :], tabtmp[:, :], pattern=[[1, 128]], compare_op=ALU.is_ge,
            fill=-1e5, base=0, channel_multiplier=-1),
            wait=[("VE", ve)], inc="GE")
        opn("scalar", lambda e, h=h: e.activation(
            din_t[:, h * 128:(h + 1) * 128], tabtmp[:, :], AF.Exp),
            wait=[("GE", ge)], inc="SE")
        first = False

    def cols_v(e):
        e.memset(onesb[:, :], 1.0)
        e.memset(onesr_f[:, :], 1.0)
        for h in range(H):
            e.tensor_scalar(kd_c[:, h:h + 1], iotac_s[:, :], sl_c[:, h:h + 1],
                            None, ALU.mult)
        e.tensor_scalar(bd_c[:, :], sl_c[:, :], -128.0, None, ALU.mult)
        r = None
        for h in range(H):
            r = e.tensor_scalar(bcw[:, h * 8:(h + 1) * 8], dvec_s[:, :],
                                sl_c[:, h:h + 1], -1.0, ALU.mult, ALU.mult)
        return r
    ve = opn("vector", cols_v, wait=[("SE", P.ctr["SE"])], inc="VE")

    def cols_s(e):
        e.activation(kd_c[:, :], kd_c[:, :], AF.Exp)
        e.activation(bd_c[:, :], bd_c[:, :], AF.Exp)
        return e.activation(bcw[:, :], bcw[:, :], AF.Exp)
    tabs_se = opn("scalar", cols_s, wait=[("VE", ve)], inc="SE")

    xtr_ve = []
    n = 0
    for tc in range(TC):
        for k in range(KC):
            half = n % 2
            w = [("LG", x_ld), ("LD", consts_ld)] if n == 0 else []
            if n >= 2:
                w = [("VE", xtr_ve[n - 2])]
            mmv = opn("tensor", lambda e, tc=tc, k=k, half=half: e.transpose(
                (ptr if half == 0 else ptr2)[:, 0:128],
                xtm[:, tc * D + k * 128: tc * D + (k + 1) * 128], idb[:, :]),
                wait=w, inc="MM")
            if n % 8 == 7:
                P.raw("tensor", lambda e: e.drain())
            xtr_ve.append(opn("vector", lambda e, tc=tc, k=k, half=half: e.tensor_copy(
                xfm[:, k * T + tc * 128: k * T + (tc + 1) * 128],
                (ptr if half == 0 else ptr2)[:, 0:128]),
                wait=[("MM", mmv)], inc="VE"))
            n += 1
    xfm_ve = P.ctr["VE"]
    dbg("xfm", lambda: xfm[:, :], [128, KC * T], BF16)

    if upto >= 1:
        qk_mm = []
        qk_ve = []
        qk_ld = []

        def reg_qk_ld(i):
            w = []
            if i >= 2:
                w = [("MM", qk_mm[(i - 2) * 2 + 1])]
            qk_ld.append(opn("gpsimd", lambda e, i=i: e.dma_start(
                out=wqslab[i % 2][:, :].rearrange("p (kc fc) -> p kc fc", kc=KC),
                in_=wqkv.rearrange("(kc p) f -> p kc f", p=128)[:, :, i * 128:(i + 1) * 128]),
                wait=w, inc="LG", incn=16))

        reg_qk_ld(0)
        reg_qk_ld(1)
        gidx = 0
        for i in range(32):
            for th in range(2):
                bank = gidx % 4
                w = [("LG", qk_ld[i])]
                if gidx == 0:
                    w.append(("VE", xfm_ve))
                if gidx >= 4:
                    w.append(("VE", qk_ve[gidx - 4]))

                def mmf(e, i=i, th=th, bank=bank):
                    r = None
                    for k in range(KC):
                        r = e.matmul(pf[bank][:, :],
                                     wqslab[i % 2][:, k * 128:(k + 1) * 128],
                                     xfm[:, k * T + th * 512: k * T + th * 512 + 512],
                                     start=(k == 0), stop=(k == KC - 1))
                    return r
                qk_mm.append(opn("tensor", mmf, wait=w, inc="MM"))

                def evf(e, i=i, th=th, bank=bank):
                    dst = qfm if i < 16 else kfm
                    h = i % 16
                    return e.tensor_scalar(
                        dst[:, h * T + th * 512: h * T + th * 512 + 512],
                        pf[bank][:, :], bqc[:, i:i + 1], None, ALU.add)
                qk_ve.append(opn("vector", evf, wait=[("MM", qk_mm[-1])], inc="VE"))
                gidx += 1
            if i + 2 < 32:
                reg_qk_ld(i + 2)

        v_ld = []
        v_mm = []
        v_ve = []

        def reg_v_ld(fg):
            w = []
            if fg == 0:
                w = [("VE", xfm_ve)]
            if fg >= 2:
                w = [("MM", v_mm[(fg - 2) * TC + TC - 1])]
            v_ld.append(opn("gpsimd", lambda e, fg=fg: e.dma_start(
                out=vslab[fg % 2][:, :].rearrange("p (kc fc) -> p kc fc", kc=KC),
                in_=wqkv.rearrange("(kc p) f -> p kc f", p=128)[
                    :, :, 2 * D + fg * 512: 2 * D + (fg + 1) * 512]),
                wait=w, inc="LG", incn=16))

        reg_v_ld(0)
        reg_v_ld(1)
        gidx = 0
        for fg in range(4):
            for tc in range(TC):
                bank = gidx % 4
                w = [("LG", v_ld[fg]), ("SE", tabs_se)]
                if gidx == 0:
                    w.append(("VE", qk_ve[-1]))
                if gidx >= 4:
                    w.append(("VE", v_ve[gidx - 4]))

                def mmv(e, fg=fg, tc=tc, bank=bank):
                    for k in range(KC):
                        e.matmul(pf[bank][:, :],
                                 xfm[:, k * T + tc * 128:k * T + (tc + 1) * 128],
                                 vslab[fg % 2][:, k * 512:(k + 1) * 512],
                                 start=(k == 0), stop=False)
                    return e.matmul(pf[bank][:, :], onesb[0:1, 0:128],
                                    bq_vr[0:1, fg * 512:(fg + 1) * 512],
                                    start=False, stop=True)
                v_mm.append(opn("tensor", mmv, wait=w, inc="MM"))

                def evv(e, fg=fg, tc=tc, bank=bank):
                    src = pf[bank][:, :].rearrange("p (j e) -> p j e", j=4)
                    dst = vtm[:, :].rearrange("p (h c e) -> p h c e", h=H, c=NCH)[
                        :, fg * 4:fg * 4 + 4, tc, :]
                    return e.tensor_copy(dst, src)
                v_ve.append(opn("vector", evv, wait=[("MM", v_mm[-1])], inc="VE"))
                gidx += 1
            if fg + 2 < 4:
                reg_v_ld(fg + 2)

        ktr_ve = []
        n = 0
        for h in range(H):
            for c in range(NCH):
                half = n % 2
                w = []
                if n >= 2:
                    w = [("VE", ktr_ve[n - 2])]
                mmv = opn("tensor", lambda e, h=h, c=c, half=half: e.transpose(
                    (ptr if half == 0 else ptr2)[:, 0:128],
                    kfm[:, h * T + c * 128: h * T + (c + 1) * 128], idb[:, :]),
                    wait=w, inc="MM")
                if n % 8 == 7:
                    P.raw("tensor", lambda e: e.drain())
                ktr_ve.append(opn("vector", lambda e, h=h, c=c, half=half:
                                  e.tensor_scalar(
                                      kdt[:, (h * NCH + c) * 128:(h * NCH + c + 1) * 128],
                                      (ptr if half == 0 else ptr2)[:, 0:128],
                                      kd_c[:, h:h + 1], None, ALU.mult),
                                  wait=[("MM", mmv)], inc="VE"))
                n += 1
        kdt_ve = P.ctr["VE"]
        last_v_mm = v_mm[-1]
        dbg("qfm", lambda: qfm[:, :], [128, H * 1024], BF16)
        dbg("kfm", lambda: kfm[:, :], [128, H * 1024], BF16)
        dbg("vtm", lambda: vtm[:, :], [128, H * 1024], BF16)
        dbg("kdt", lambda: kdt[:, :], [128, H * 1024], BF16)

    if upto >= 2:
        sz_ve = opn("vector", lambda e: e.memset(sbf[:, :], 0.0), inc="VE")
        p1_mm = []
        p1_ve = []
        nidx = 0
        for h in range(H):
            for c in range(NCH):
                bank = 4 + (nidx % 2)
                w = [("VE", kdt_ve)] if nidx == 0 else []
                if nidx >= 2:
                    w.append(("VE", p1_ve[nidx - 2]))

                def smm(e, h=h, c=c, bank=bank):
                    i = h * NCH + c
                    return e.matmul(pf[bank][:, 0:128],
                                    kdt[:, i * 128:(i + 1) * 128],
                                    vtm[:, i * 128:(i + 1) * 128],
                                    start=True, stop=True)
                p1_mm.append(opn("tensor", smm, wait=w, inc="MM"))
                if nidx % 4 == 3:
                    P.raw("tensor", lambda e: e.drain())

                def sstt(e, h=h, bank=bank):
                    return e.scalar_tensor_tensor(
                        sbf[:, h * 128:(h + 1) * 128], sbf[:, h * 128:(h + 1) * 128],
                        bd_c[:, h:h + 1], pf[bank][:, 0:128], ALU.mult, ALU.add)
                p1_ve.append(opn("vector", sstt,
                                 wait=[("MM", p1_mm[-1]), ("VE", sz_ve)], inc="VE"))
                nidx += 1
        p1_done_ve = P.ctr["VE"]

        if subph < 2:
            upto = 0
        texp = None
        if subph >= 2:
            texp = opn("sync", lambda e: e.dma_start(out=agt_in[:, :], in_=sbf[:, :]),
                       wait=[("VE", p1_done_ve)], inc="LD", incn=16)
        agT = None
        if subph >= 2:
            agT = opn("gpsimd", lambda e: e.collective_compute(
                "AllGather", ALU.bypass, replica_groups=RG,
                ins=[agt_in.opt()], outs=[agt_out.opt()]),
                wait=[("LD", texp)], inc="CC")

        if subph < 3:
            upto = 0
        wsh_ld = []
        if subph >= 3:
            wsh_ve = []
            for i in range(24):
                if i < 2:
                    w = [("MM", last_v_mm)]
                else:
                    w = [("VE", wsh_ve[i - 2])]

                def lsh(e, i=i, buf=(wsh_f32, wsh_f32b)[i % 2]):
                    if i < 8:
                        return e.dma_start(out=buf[:, :],
                                           in_=wfc_sh[i * 128:(i + 1) * 128, :])
                    j = i - 8
                    return e.dma_start(out=buf[:, 0:1024],
                                       in_=wpr_sh[j * 128:(j + 1) * 128, :])
                wsh_ld.append(opn("scalar", lsh, wait=w, inc="LS", incn=16))

                def msh(e, i=i, buf=(wsh_f32, wsh_f32b)[i % 2]):
                    width = D if i < 8 else 1024
                    col = 8 if i < 8 else 9
                    e.tensor_reduce(mcol[:, :], buf[:, 0:width], AX.X, ALU.max,
                                    apply_absolute_value=True)
                    e.drain()
                    if i == 0 or i == 8:
                        r = e.tensor_copy(scal[:, col:col + 1], mcol[:, :])
                    else:
                        r = e.tensor_tensor(scal[:, col:col + 1],
                                            scal[:, col:col + 1], mcol[:, :],
                                            ALU.max)
                    e.drain()
                    return r
                wsh_ve.append(opn("vector", msh, wait=[("LS", wsh_ld[-1])], inc="VE"))

            def gmax_f(e):
                e.tensor_reduce(scal[0:1, 8:9], scal[:, 8:9], AX.C, ALU.max)
                return e.tensor_reduce(scal[0:1, 9:10], scal[:, 9:10], AX.C, ALU.max)
            gmax = opn("gpsimd", gmax_f, wait=[("VE", wsh_ve[-1])], inc="GE")

            def stgw_f(e):
                e.dma_start(out=arw_in[0:1, 0:1], in_=scal[0:1, 8:9]).then_inc(
                    P.sems["LG"], 16)
                return e.dma_start(out=arw_in[0:1, 1:2], in_=scal[0:1, 9:10])
            opn("gpsimd", stgw_f, wait=[("GE", gmax)], inc="LG", incn=16)
            P.ctr["LG"] += 16
            stgw = P.ctr["LG"]
            arw = opn("gpsimd", lambda e: e.collective_compute(
                "AllReduce", ALU.max, replica_groups=RG,
                ins=[arw_in.opt()], outs=[arw_out.opt()]),
                wait=[("LG", stgw)], inc="CC")

            def ldw_f(e):
                e.dma_start(out=scal[:, 14:15],
                            in_=arw_out[0:1, 0:1].broadcast_to((128, 1))).then_inc(
                    P.sems["LG"], 16)
                return e.dma_start(out=scal[:, 15:16],
                                   in_=arw_out[0:1, 1:2].broadcast_to((128, 1)))
            opn("gpsimd", ldw_f, wait=[("CC", arw)], inc="LG", incn=16)
            P.ctr["LG"] += 16
            ldw = P.ctr["LG"]

            def wscal(e):
                e.tensor_scalar(scal[:, 5:7], scal[:, 14:16], 1.0 / 448.0, 1e-6,
                                ALU.mult, ALU.max)
                e.drain()
                e.reciprocal(scal[:, 0:2], scal[:, 5:7])
                e.drain()
                r = e.tensor_scalar(scal[:, 0:2], scal[:, 0:2], 0.5, None, ALU.mult)
                e.drain()
                return r
            bc_ve = opn("vector", wscal, wait=[("LG", ldw)], inc="VE")

            if subph < 6:
                upto = 0
            if subph >= 4:
                for i in range(8):
                    w = [("VE", bc_ve)] if i == 0 else [("VE", P.ctr["VE"])]
                    ldq = opn("scalar", lambda e, i=i: e.dma_start(
                        out=wsh_f32[:, :], in_=wfc_sh[i * 128:(i + 1) * 128, :]),
                        wait=w, inc="LS", incn=16)
                    qv = opn("vector", lambda e: e.tensor_scalar(
                        wsh_f8[:, :], wsh_f32[:, :], scal[:, 0:1], None, ALU.mult),
                        wait=[("LS", ldq)], inc="VE")
                    prev_tr = []
                    for k in range(KC):
                        half = k % 2
                        w2 = [("VE", qv)]
                        if k >= 2:
                            w2.append(("VE", prev_tr[k - 2]))
                        tmm = opn("tensor", lambda e, k=k, half=half: e.transpose(
                            (ptr8 if half == 0 else ptr28)[:, 0:256:2],
                            wsh_f8[:, k * 128:(k + 1) * 128], id8[:, :]),
                            wait=w2, inc="MM")
                        if k % 8 == 7:
                            P.raw("tensor", lambda e: e.drain())
                        prev_tr.append(opn("vector", lambda e, k=k, half=half, i=i:
                                           e.tensor_copy(
                                               wsh_stg[:, k * 1024 + i * 128:
                                                       k * 1024 + (i + 1) * 128],
                                               (ptr8 if half == 0 else ptr28)[:, 0:256:2]),
                                           wait=[("MM", tmm)], inc="VE"))
                stg_f = opn("sync", lambda e: e.dma_start(
                    out=w8f_in.rearrange("(kc p) u -> p kc u", p=128),
                    in_=wsh_stg[:, :].rearrange("p (kc u) -> p kc u", kc=KC)),
                    wait=[("VE", P.ctr["VE"])], inc="LD", incn=16)
                agF = opn("gpsimd", lambda e: e.collective_compute(
                    "AllGather", ALU.bypass, replica_groups=RG,
                    ins=[w8f_in.opt()], outs=[w8f_out.opt()]),
                    wait=[("LD", stg_f), ("CC", agT)], inc="CC")

            if subph >= 5:
                for i in range(16):
                    w = [("LD", stg_f)] if i == 0 else [("VE", P.ctr["VE"])]
                    ldq = opn("scalar", lambda e, i=i: e.dma_start(
                        out=wsh_f32[:, 0:1024], in_=wpr_sh[i * 128:(i + 1) * 128, :]),
                        wait=w, inc="LS", incn=16)
                    qv = opn("vector", lambda e: e.tensor_scalar(
                        wsh_f8[:, 0:1024], wsh_f32[:, 0:1024], scal[:, 1:2], None, ALU.mult),
                        wait=[("LS", ldq)], inc="VE")
                    prev_tr = []
                    for k in range(8):
                        half = k % 2
                        w2 = [("VE", qv)]
                        if k >= 2:
                            w2.append(("VE", prev_tr[k - 2]))
                        tmm = opn("tensor", lambda e, k=k, half=half: e.transpose(
                            (ptr8 if half == 0 else ptr28)[:, 0:256:2],
                            wsh_f8[:, k * 128:(k + 1) * 128], id8[:, :]),
                            wait=w2, inc="MM")
                        prev_tr.append(opn("vector", lambda e, k=k, half=half, i=i:
                                           e.tensor_copy(
                                               wsh_stg[:, k * 2048 + i * 128:
                                                       k * 2048 + (i + 1) * 128],
                                               (ptr8 if half == 0 else ptr28)[:, 0:256:2]),
                                           wait=[("MM", tmm)], inc="VE"))
                    P.raw("tensor", lambda e: e.drain())
                stg_p = opn("sync", lambda e: e.dma_start(
                    out=w8p_in.rearrange("(kc p) o -> p kc o", p=128),
                    in_=wsh_stg[:, :].rearrange("p (kc o) -> p kc o", kc=8)),
                    wait=[("VE", P.ctr["VE"])], inc="LD", incn=16)
                agP = opn("gpsimd", lambda e: e.collective_compute(
                    "AllGather", ALU.bypass, replica_groups=RG,
                    ins=[w8p_in.opt()], outs=[w8p_out.opt()]),
                    wait=[("LD", stg_p), ("CC", agF)], inc="CC")

            if subph >= 6:
                comb_last = None
                for r in range(NCORES):
                    wl = [("CC", agT)]
                    if r == 0:
                        wl.append(("VE", P.ctr["VE"]))
                    else:
                        wl.append(("VE", comb_last))
                    lt = opn("sync", lambda e, r=r: e.dma_start(
                        out=tjin[:, :], in_=agt_out[r * 128:(r + 1) * 128, :]),
                        wait=wl, inc="LD", incn=16)
                    for h in range(H):
                        def cmb(e, r=r, h=h):
                            if r == 0:
                                return e.tensor_scalar(
                                    sini[:, h * 128:(h + 1) * 128],
                                    tjin[:, h * 128:(h + 1) * 128],
                                    bcw[:, h * 8 + r:h * 8 + r + 1], None, ALU.mult)
                            return e.scalar_tensor_tensor(
                                sini[:, h * 128:(h + 1) * 128],
                                tjin[:, h * 128:(h + 1) * 128],
                                bcw[:, h * 8 + r:h * 8 + r + 1],
                                sini[:, h * 128:(h + 1) * 128], ALU.mult, ALU.add)
                        comb_last = opn("vector", cmb,
                                        wait=[("LD", lt)] if h == 0 else [], inc="VE")
                sinit_ve = opn("vector", lambda e: e.tensor_copy(sbf[:, :], sini[:, :]),
                               inc="VE")
                dbg("sinit", lambda: sbf[:, :], [128, H * 128], BF16)
    dbg("bcw", lambda: bcw[:, :], [128, 128], F32)
    dbg("kdc", lambda: kd_c[:, :], [128, H], F32)

    if upto >= 3:
        p2_sco, p2_stv, p2_qdv, p2_omm, p2_umm, p2_oev, p2_stt = ([] for _ in range(7))
        nidx = 0
        for h in range(H):
            for c in range(NCH):
                p = nidx % 2
                A, B, Cb = pf[0 + p], pf[2 + p], pf[4 + p]
                i = h * NCH + c
                w = [("VE", sinit_ve)] if nidx == 0 else []
                if nidx >= 2:
                    w.append(("VE", p2_stv[nidx - 2]))
                p2_sco.append(opn("tensor", lambda e, i=i, A=A: e.matmul(
                    A[:, 0:128], kfm[:, i * 128:(i + 1) * 128],
                    qfm[:, i * 128:(i + 1) * 128], start=True, stop=True),
                    wait=w, inc="MM"))

                w2 = [("MM", p2_sco[-1])]
                if nidx >= 2:
                    w2.append(("MM", p2_omm[nidx - 2]))
                p2_stv.append(opn("vector", lambda e, h=h, p=p, A=A:
                                  e.tensor_tensor(st_s[:, p * 128:(p + 1) * 128],
                                                  A[:, 0:128],
                                                  din_t[:, h * 128:(h + 1) * 128],
                                                  ALU.mult), wait=w2, inc="VE"))
                w3 = []
                if nidx >= 2:
                    w3.append(("MM", p2_omm[nidx - 2]))
                p2_qdv.append(opn("vector", lambda e, i=i, h=h, p=p:
                                  e.tensor_tensor(qd_s[:, p * 128:(p + 1) * 128],
                                                  qfm[:, i * 128:(i + 1) * 128],
                                                  dq_t[:, h * 128:(h + 1) * 128],
                                                  ALU.mult), wait=w3, inc="VE"))
                w4 = [("VE", p2_qdv[-1])]
                if nidx >= 2:
                    w4.append(("SE", p2_oev[nidx - 2]))
                if nidx >= 1:
                    w4.append(("VE", p2_stt[nidx - 1]))

                def mo(e, i=i, h=h, p=p, B=B):
                    e.matmul(B[:, 0:128], vtm[:, i * 128:(i + 1) * 128],
                             st_s[:, p * 128:(p + 1) * 128], start=True, stop=False)
                    return e.matmul(B[:, 0:128], sbf[:, h * 128:(h + 1) * 128],
                                    qd_s[:, p * 128:(p + 1) * 128],
                                    start=False, stop=True)
                p2_omm.append(opn("tensor", mo, wait=w4, inc="MM"))

                w5 = []
                if nidx >= 2:
                    w5.append(("VE", p2_stt[nidx - 2]))
                p2_umm.append(opn("tensor", lambda e, i=i, Cb=Cb: e.matmul(
                    Cb[:, 0:128], kdt[:, i * 128:(i + 1) * 128],
                    vtm[:, i * 128:(i + 1) * 128], start=True, stop=True),
                    wait=w5, inc="MM"))
                if nidx % 2 == 1:
                    P.raw("tensor", lambda e: e.drain())
                p2_oev.append(opn("scalar", lambda e, i=i, B=B: e.activation(
                    attnt[:, i * 128:(i + 1) * 128], B[:, 0:128], AF.Copy),
                    wait=[("MM", p2_omm[-1])], inc="SE"))
                p2_stt.append(opn("vector", lambda e, h=h, Cb=Cb:
                                  e.scalar_tensor_tensor(
                                      sbf[:, h * 128:(h + 1) * 128],
                                      sbf[:, h * 128:(h + 1) * 128],
                                      bd_c[:, h:h + 1], Cb[:, 0:128],
                                      ALU.mult, ALU.add),
                                  wait=[("MM", p2_umm[-1])], inc="VE"))
                nidx += 1
        pass2_se = P.ctr["SE"]
        pass2_ve = P.ctr["VE"]
        dbg("attnt", lambda: attnt[:, :], [128, H * 1024], BF16)

    if upto >= 4:
        op_mm = []
        op_ve = []
        wo_ld = []
        xr_ld = []

        def reg_wo_ld(og):
            w = [("VE", pass2_ve), ("SE", pass2_se)]
            if og >= 2:
                w = [("MM", op_mm[(og - 2) * TC + TC - 1])]
            wo_ld.append(opn("gpsimd", lambda e, og=og: e.dma_start(
                out=woslab[og % 2][:, :].rearrange("p (kc oc) -> p kc oc", kc=KC),
                in_=wout.rearrange("(kc p) o -> p kc o", p=128)[
                    :, :, og * 512:(og + 1) * 512]),
                wait=w, inc="LG", incn=16))

        reg_wo_ld(0)
        reg_wo_ld(1)
        gidx = 0
        for og in range(4):
            for tc in range(TC):
                bank = gidx % 4
                wxl = []
                if gidx >= 2:
                    wxl = [("VE", op_ve[gidx - 2])]
                elif og == 0 and tc == 0:
                    wxl = [("SE", pass2_se)]
                xr_ld.append(opn("sync", lambda e, og=og, tc=tc, par=gidx % 2:
                                 e.dma_start(out=xres[par][:, :],
                                             in_=xin[tc * 128:(tc + 1) * 128,
                                                     og * 512:(og + 1) * 512]),
                                 wait=wxl, inc="LD", incn=16))

                w = [("LG", wo_ld[og]), ("LD", xr_ld[-1])]
                if gidx == 0:
                    w += [("VE", pass2_ve), ("SE", pass2_se)]
                if gidx >= 4:
                    w.append(("VE", op_ve[gidx - 4]))

                def mop(e, tc=tc, og=og, bank=bank):
                    for k in range(KC):
                        e.matmul(pf[bank][:, :],
                                 attnt[:, k * T + tc * 128:k * T + (tc + 1) * 128],
                                 woslab[og % 2][:, k * 512:(k + 1) * 512],
                                 start=(k == 0), stop=False)
                    return e.matmul(pf[bank][:, :], onesb[0:1, 0:128],
                                    bo_r[0:1, og * 512:(og + 1) * 512],
                                    start=False, stop=True)
                op_mm.append(opn("tensor", mop, wait=w, inc="MM"))

                def vev(e, tc=tc, og=og, bank=bank, par=gidx % 2):
                    return e.tensor_tensor(
                        x1[:, tc * D + og * 512: tc * D + (og + 1) * 512],
                        pf[bank][:, :], xres[par][:, :], ALU.add)
                op_ve.append(opn("vector", vev, wait=[("MM", op_mm[-1])], inc="VE"))
                gidx += 1
            if og + 2 < 4:
                reg_wo_ld(og + 2)
        opj_ve = P.ctr["VE"]

        ln_last = None
        for tc in range(TC):
            def ln1(e, tc=tc):
                xt = x1[:, tc * D:(tc + 1) * D]
                e.tensor_reduce(bnag[:, 0:1], xt, AX.X, ALU.add)
                e.drain()
                e.tensor_scalar(bnag[:, 0:1], bnag[:, 0:1], 1.0 / D, None, ALU.mult)
                e.drain()
                e.tensor_scalar(x2t[0][:, :], xt, bnag[:, 0:1], None, ALU.subtract)
                e.tensor_tensor(x2t[0][:, :], x2t[0][:, :], x2t[0][:, :], ALU.mult)
                e.tensor_reduce(bnag[:, 1:2], x2t[0][:, :], AX.X, ALU.add)
                e.drain()
                r = e.tensor_scalar(rstd[:, :], bnag[:, 1:2], 1.0 / D, EPS,
                                    ALU.mult, ALU.add)
                e.drain()
                return r
            wv = [("VE", opj_ve)] if tc == 0 else [("VE", ln_last)]
            lv = opn("vector", ln1, wait=wv, inc="VE")
            sq = opn("scalar", lambda e: e.activation(rstd[:, :], rstd[:, :], AF.Sqrt),
                     wait=[("VE", lv)], inc="SE")
            rv = opn("vector", lambda e: (e.reciprocal(rstd[:, :], rstd[:, :]),
                                          e.drain())[0],
                     wait=[("SE", sq)], inc="VE")

            def nrm(e, tc=tc):
                e.drain()
                e.tensor_scalar(x1[:, tc * D:(tc + 1) * D],
                                x1[:, tc * D:(tc + 1) * D],
                                bnag[:, 0:1], rstd[:, :], ALU.subtract, ALU.mult)
                e.tensor_reduce(mcol[:, :], x1[:, tc * D:(tc + 1) * D], AX.X,
                                ALU.max, apply_absolute_value=True)
                e.drain()
                if tc == 0:
                    r = e.tensor_copy(scal[:, 10:11], mcol[:, :])
                else:
                    r = e.tensor_tensor(scal[:, 10:11], scal[:, 10:11], mcol[:, :],
                                        ALU.max)
                e.drain()
                return r
            ln_last = opn("vector", nrm, wait=[("VE", rv)], inc="VE")
        ln1_ve = P.ctr["VE"]
        dbg("x1", lambda: x1[:, :], [128, TC * D], F32)

        gx = opn("gpsimd", lambda e: e.tensor_reduce(
            scal[0:1, 10:11], scal[:, 10:11], AX.C, ALU.max),
            wait=[("VE", ln1_ve)], inc="GE")
        sgx = opn("gpsimd", lambda e: e.dma_start(out=arx_in[0:1, 0:1],
                                                  in_=scal[0:1, 10:11]),
                  wait=[("GE", gx)], inc="LG", incn=16)
        arx = opn("gpsimd", lambda e: e.collective_compute(
            "AllReduce", ALU.max, replica_groups=RG,
            ins=[arx_in.opt()], outs=[arx_out.opt()]),
            wait=[("LG", sgx), ("CC", agP)], inc="CC")
        ldx = opn("gpsimd", lambda e: e.dma_start(
            out=scal[:, 14:15], in_=arx_out[0:1, 0:1].broadcast_to((128, 1))),
                  wait=[("CC", arx)], inc="LG", incn=16)

        def xscal(e):
            e.tensor_scalar(scal[:, 7:8], scal[:, 14:15], 1.0 / 448.0, 1e-6,
                            ALU.mult, ALU.max)
            e.drain()
            e.reciprocal(scal[:, 2:3], scal[:, 7:8])
            e.drain()
            e.tensor_scalar(scal[:, 2:3], scal[:, 2:3], 0.5, None, ALU.mult)
            e.tensor_tensor(scal[:, 13:14], scal[:, 7:8], scal[:, 5:6], ALU.mult)
            e.drain()
            r = e.tensor_scalar(scal[:, 13:14], scal[:, 13:14], 4.0, None,
                                ALU.mult)
            e.drain()
            return r
        bcx_ve = opn("vector", xscal, wait=[("LG", ldx)], inc="VE")

    if upto >= 5:
        x8_ve = []
        n = 0
        for tc in range(TC):
            wq = [("VE", bcx_ve)] if tc == 0 else [("VE", P.ctr["VE"])]
            qv = opn("vector", lambda e, tc=tc: e.tensor_scalar(
                qtm8[:, :], x1[:, tc * D:(tc + 1) * D], scal[:, 2:3], None, ALU.mult),
                wait=wq, inc="VE")
            for k in range(KC):
                half = n % 2
                w = [("VE", qv)]
                if n >= 2:
                    w.append(("VE", x8_ve[n - 2]))
                tmm = opn("tensor", lambda e, k=k, half=half: e.transpose(
                    (ptr8 if half == 0 else ptr28)[:, 0:256:2],
                    qtm8[:, k * 128:(k + 1) * 128], id8[:, :]),
                    wait=w, inc="MM")
                if n % 8 == 7:
                    P.raw("tensor", lambda e: e.drain())
                x8_ve.append(opn("vector", lambda e, k=k, tc=tc, half=half:
                                 e.tensor_copy(
                                     x8fm[:, k * T + tc * 128:k * T + (tc + 1) * 128],
                                     (ptr8 if half == 0 else ptr28)[:, 0:256:2]),
                                 wait=[("MM", tmm)], inc="VE"))
                n += 1
        x8_done = P.ctr["VE"]
        dbg("x8fm", lambda: x8fm[:, :], [128, KC * T], FP8)

    if upto >= 6:
        fc_ld, fc_mm, fc_se, fc_st, fc_vmax = [], [], [], [], []

        def reg_fc_ld(ug):
            w = [("CC", agF)]
            if ug == 0:
                w += [("MM", op_mm[-1]), ("SE", pass2_se)]
            if ug >= 2:
                w.append(("MM", fc_mm[(ug - 2) * 8 + 7]))

            def lfc(e, ug=ug):
                r, lh = ug // 2, (ug % 2) * 512
                src = w8f_out.rearrange("(r kc p) u -> p r kc u", r=NCORES, kc=KC)[
                    :, r, :, lh:lh + 512]
                return e.dma_start(
                    out=w8slab[ug % 2][:, :].rearrange("p (kc u) -> p kc u", kc=KC),
                    in_=src)
            fc_ld.append(opn("sync", lfc, wait=w, inc="LD", incn=16))

        reg_fc_ld(0)
        reg_fc_ld(1)
        for ug in range(16):
            for uc4 in range(4):
                for th in range(2):
                    gi = len(fc_mm)
                    bank = gi % 4
                    w2 = [("LD", fc_ld[ug]), ("VE", x8_done)]
                    if gi >= 4:
                        w2.append(("SE", fc_se[gi - 4]))

                    def mfc(e, ug=ug, uc4=uc4, th=th, bank=bank):
                        r = None
                        for k in range(KC):
                            r = e.matmul(pf[bank][:, :],
                                         w8slab[ug % 2][:, k * 512 + uc4 * 128:
                                                        k * 512 + (uc4 + 1) * 128],
                                         x8fm[:, k * T + th * 512:
                                              k * T + th * 512 + 512],
                                         start=(k == 0), stop=(k == KC - 1))
                        return r
                    fc_mm.append(opn("tensor", mfc, wait=w2, inc="MM"))

                    w3 = [("MM", fc_mm[-1])]
                    if gi >= 2:
                        w3 += [("ST", fc_st[gi - 2]), ("VE", fc_vmax[gi - 2])]
                    fc_se.append(opn("scalar", lambda e, bank=bank, gi=gi:
                                     e.activation(rstg[gi % 2][:, :], pf[bank][:, :],
                                                  AF.Relu), wait=w3, inc="SE"))

                    def vmx(e, gi=gi):
                        e.tensor_reduce(mcol[:, :], rstg[gi % 2][:, :], AX.X,
                                        ALU.max, apply_absolute_value=True)
                        e.drain()
                        if gi == 0:
                            r = e.tensor_copy(scal[:, 11:12], mcol[:, :])
                        else:
                            r = e.tensor_tensor(scal[:, 11:12], scal[:, 11:12],
                                                mcol[:, :], ALU.max)
                        e.drain()
                        return r
                    fc_vmax.append(opn("vector", vmx, wait=[("SE", fc_se[-1])],
                                       inc="VE"))
                    uc = ug * 4 + uc4
                    fc_st.append(opn("sync", lambda e, uc=uc, th=th, gi=gi:
                                     e.dma_start(out=gdram[uc, :, th * 512:(th + 1) * 512],
                                                 in_=rstg[gi % 2][:, :]),
                                     wait=[("SE", fc_se[-1])], inc="ST", incn=16))
            if ug + 2 < 16:
                reg_fc_ld(ug + 2)
        fc_done_st = P.ctr["ST"]
        fc_done_mm = fc_mm[-1]

        gh = opn("gpsimd", lambda e: e.tensor_reduce(
            scal[0:1, 11:12], scal[:, 11:12], AX.C, ALU.max),
            wait=[("VE", P.ctr["VE"])], inc="GE")
        sgh = opn("gpsimd", lambda e: e.dma_start(out=arh_in[0:1, 0:1],
                                                  in_=scal[0:1, 11:12]),
                  wait=[("GE", gh)], inc="LG", incn=16)
        arh = opn("gpsimd", lambda e: e.collective_compute(
            "AllReduce", ALU.max, replica_groups=RG,
            ins=[arh_in.opt()], outs=[arh_out.opt()]),
            wait=[("LG", sgh)], inc="CC")
        ldh = opn("gpsimd", lambda e: e.dma_start(
            out=scal[:, 14:15], in_=arh_out[0:1, 0:1].broadcast_to((128, 1))),
            wait=[("CC", arh)], inc="LG", incn=16)

        def hscal(e):
            e.tensor_tensor(scal[:, 15:16], scal[:, 13:14], scal[:, 14:15], ALU.mult)
            e.drain()
            e.tensor_tensor(scal[:, 15:16], scal[:, 15:16], scal[:, 15:16], ALU.mult)
            e.drain()
            e.tensor_scalar(scal[:, 12:13], scal[:, 15:16], 1.0 / 448.0, 1e-6,
                            ALU.mult, ALU.max)
            e.tensor_tensor(scal[:, 15:16], scal[:, 13:14], scal[:, 13:14], ALU.mult)
            e.drain()
            e.reciprocal(scal[:, 3:4], scal[:, 12:13])
            e.drain()
            e.tensor_tensor(scal[:, 3:4], scal[:, 3:4], scal[:, 15:16], ALU.mult)
            e.drain()
            e.tensor_scalar(scal[:, 3:4], scal[:, 3:4], 0.5, None, ALU.mult)
            e.tensor_tensor(scal[:, 4:5], scal[:, 12:13], scal[:, 6:7], ALU.mult)
            e.drain()
            r = e.tensor_scalar(scal[:, 4:5], scal[:, 4:5], 4.0, None, ALU.mult)
            e.drain()
            return r
        bch_ve = opn("vector", hscal, wait=[("LG", ldh)], inc="VE")
        if "rg" in debug:
            rg_out = nc.dram_tensor("dbg_rg", [4 * 128, T], BF16,
                                    kind="ExternalOutput").ap()
            for uu, uc in enumerate((0, 1, 31, 63)):
                opn("sync", lambda e, uu=uu, uc=uc: e.dma_start(
                    out=rg_out[uu * 128:(uu + 1) * 128, :], in_=gdram[uc, :, :]),
                    wait=[("ST", fc_done_st)], inc="ST", incn=16)
        dbg("scal", lambda: scal[:, :], [128, 16], F32)

    if upto >= 7:
        prev_pair_mm = None
        prev2_pair_mm = None
        prev_gq_last = None
        prev_out_st = [None, None]
        for pr in range(4):
            gcur = g8ab[pr % 2]
            gq_ve = []
            for hfl in range(2):
                lds = []
                for uo in range(32):
                    uc = hfl * 32 + uo
                    w = []
                    if pr == 0 and hfl == 0 and uo == 0:
                        w = [("ST", fc_done_st), ("VE", bch_ve), ("MM", fc_done_mm)]
                    elif uo == 0:
                        if hfl == 1:
                            w = [("VE", gq_ve[-1])]
                        elif prev_gq_last is not None:
                            w = [("VE", prev_gq_last)]
                    lds.append(opn("sync", lambda e, uc=uc, pr=pr, uo=uo: e.dma_start(
                        out=rload[:, uo * 256:(uo + 1) * 256],
                        in_=gdram[uc, :, pr * 256:(pr + 1) * 256]),
                        wait=w, inc="LD", incn=16))

                def gq(e, hfl=hfl, gcur=gcur):
                    r = None
                    for uo in range(32):
                        uc = hfl * 32 + uo
                        r = e.scalar_tensor_tensor(
                            gcur[:, uc * 256:(uc + 1) * 256],
                            rload[:, uo * 256:(uo + 1) * 256], scal[:, 3:4],
                            rload[:, uo * 256:(uo + 1) * 256], ALU.mult, ALU.mult)
                    return r
                wq = [("LD", lds[-1])]
                if hfl == 0 and prev2_pair_mm is not None:
                    wq.append(("MM", prev2_pair_mm))
                gq_ve.append(opn("vector", gq, wait=wq, inc="VE"))
            prev_gq_last = gq_ve[-1]

            pr_mm = []
            pr_ev = []
            for og in range(4):
                for us in range(8):
                    if len(pr_mm) >= 2:
                        wl = [("MM", pr_mm[-2])]
                    elif prev_pair_mm is not None:
                        wl = [("MM", prev_pair_mm)]
                    else:
                        wl = [("VE", gq_ve[-1])]
                    sli = opn("sync", lambda e, og=og, us=us: e.dma_start(
                        out=w8psl[us % 2][:, :].rearrange("p (u o) -> p u o", u=8),
                        in_=w8p_out.rearrange("(u p) o -> p u o", p=128)[
                            :, us * 8:(us + 1) * 8, og * 512:(og + 1) * 512]),
                        wait=wl, inc="LD", incn=16)

                    wm = [("LD", sli), ("VE", gq_ve[-1])]
                    if og >= 2 and us == 0:
                        wm.append(("VE", pr_ev[(og - 2) * 2 + 1]))

                    def mpr(e, og=og, us=us, gcur=gcur):
                        r = None
                        for j in range(8):
                            uc = us * 8 + j
                            for half in range(2):
                                r = e.matmul(pf[(og % 2) * 2 + half][:, :],
                                             gcur[:, uc * 256 + half * 128:
                                                  uc * 256 + half * 128 + 128],
                                             w8psl[us % 2][:, j * 512:(j + 1) * 512],
                                             start=(us == 0 and j == 0),
                                             stop=(us == 7 and j == 7))
                        return r
                    pr_mm.append(opn("tensor", mpr, wait=wm, inc="MM"))
                for half in range(2):
                    tc = pr * 2 + half

                    def pev(e, og=og, half=half, tc=tc):
                        return e.scalar_tensor_tensor(
                            x2t[half][:, og * 512:(og + 1) * 512],
                            pf[(og % 2) * 2 + half][:, :], scal[:, 4:5],
                            x1[:, tc * D + og * 512: tc * D + (og + 1) * 512],
                            ALU.mult, ALU.add)
                    wpe = [("MM", pr_mm[-1])]
                    if og == 0 and prev_out_st[half] is not None:
                        wpe.append(("ST", prev_out_st[half]))
                    pr_ev.append(opn("vector", pev, wait=wpe, inc="VE"))
            prev2_pair_mm = prev_pair_mm
            prev_pair_mm = pr_mm[-1]

            for half in range(2):
                tc = pr * 2 + half

                def ln2(e, half=half):
                    xt = x2t[half][:, :]
                    e.tensor_reduce(bnag[:, 0:1], xt, AX.X, ALU.add)
                    e.drain()
                    e.tensor_scalar(bnag[:, 0:1], bnag[:, 0:1], 1.0 / D, None,
                                    ALU.mult)
                    e.drain()
                    e.tensor_scalar(lnscr2[:, :], xt, bnag[:, 0:1], None,
                                    ALU.subtract)
                    e.tensor_tensor(lnscr2[:, :], lnscr2[:, :], lnscr2[:, :],
                                    ALU.mult)
                    e.tensor_reduce(bnag[:, 1:2], lnscr2[:, :], AX.X, ALU.add)
                    e.drain()
                    r = e.tensor_scalar(rstd[:, :], bnag[:, 1:2], 1.0 / D, EPS,
                                        ALU.mult, ALU.add)
                    e.drain()
                    return r
                lv = opn("vector", ln2, wait=[("VE", pr_ev[-1])], inc="VE")
                sq2 = opn("scalar", lambda e: e.activation(rstd[:, :], rstd[:, :],
                                                           AF.Sqrt),
                          wait=[("VE", lv)], inc="SE")
                rv2 = opn("vector", lambda e: (e.reciprocal(rstd[:, :], rstd[:, :]),
                                               e.drain())[0],
                          wait=[("SE", sq2)], inc="VE")
                nv = opn("vector", lambda e, half=half: e.tensor_scalar(
                    x2t[half][:, :], x2t[half][:, :], bnag[:, 0:1], rstd[:, :],
                    ALU.subtract, ALU.mult), wait=[("VE", rv2)], inc="VE")
                prev_out_st[half] = opn("sync", lambda e, tc=tc, half=half:
                    e.dma_start(out=yout[tc * 128:(tc + 1) * 128, :],
                                in_=x2t[half][:, :]),
                    wait=[("VE", nv)], inc="ST", incn=16)

    P.raw("sync", lambda e, v=P.ctr["VE"]: e.wait_ge(P.sems["VE"], v))
    P.raw("sync", lambda e, v=P.ctr["SE"]: e.wait_ge(P.sems["SE"], v))
    P.raw("sync", lambda e, v=P.ctr["MM"]: e.wait_ge(P.sems["MM"], v))
    P.raw("sync", lambda e, v=P.ctr["GE"]: e.wait_ge(P.sems["GE"], v))

    def fin(e):
        for (dap, apget) in dbg_list:
            e.dma_start(out=dap[:, :], in_=apget()).then_inc(P.sems["ST"], 16)
            P.ctr["ST"] += 16
        if upto < 7:
            e.dma_start(out=yout[0:128, :], in_=x1[:, 0:D]).then_inc(P.sems["ST"], 16)
            P.ctr["ST"] += 16
        e.wait_ge(P.sems["ST"], P.ctr["ST"])
    P.raw("sync", fin)

    with nc.Block() as block:
        @block.sync
        def _(e):
            for f in P.streams["sync"]:
                f(e)

        @block.gpsimd
        def _(e):
            for f in P.streams["gpsimd"]:
                f(e)

        @block.vector
        def _(e):
            for f in P.streams["vector"]:
                f(e)

        @block.scalar
        def _(e):
            for f in P.streams["scalar"]:
                f(e)

        @block.tensor
        def _(e):
            for f in P.streams["tensor"]:
                f(e)

    es.close()
    return nc


def make_consts(core_id):
    import ml_dtypes
    i = np.arange(128, dtype=np.float32)
    return {
        "identb": np.eye(128).astype(ml_dtypes.bfloat16),
        "ident8": np.eye(128).astype(ml_dtypes.float8_e4m3fn),
        "bciota": np.ascontiguousarray(np.tile(i[None, :], (128, 1))),
        "difft": np.ascontiguousarray(i[None, :] - i[:, None]),
        "iotac": np.ascontiguousarray((i - 128.0)[:, None]),
        "dvecbc": np.ascontiguousarray(np.tile(
            np.where(np.arange(8) < core_id,
                     1024.0 * (core_id - 1 - np.arange(8)),
                     1e9).astype(np.float32)[None, :], (128, 1))),
    }


def kernel(L0_attn_x, L0_attn_Wqkv, L0_attn_bqkv, L0_attn_Wout, L0_attn_bout,
           L0_attn_slopes, L0_ffn_w_fc, L0_ffn_w_proj):
    x = np.ascontiguousarray(np.asarray(L0_attn_x, dtype=np.float32).reshape(8192, 2048))
    wqkv = np.ascontiguousarray(np.asarray(L0_attn_Wqkv, dtype=np.float32))
    bq = np.asarray(L0_attn_bqkv, dtype=np.float32)
    wout_a = np.ascontiguousarray(np.asarray(L0_attn_Wout, dtype=np.float32))
    bo = np.asarray(L0_attn_bout, dtype=np.float32)
    wfc = np.ascontiguousarray(np.asarray(L0_ffn_w_fc, dtype=np.float32))
    wpr = np.ascontiguousarray(np.asarray(L0_ffn_w_proj, dtype=np.float32))
    slopes_t = np.ascontiguousarray(
        np.tile(np.asarray(L0_attn_slopes, dtype=np.float32)[None, :], (128, 1)))
    bq_cols = np.ascontiguousarray(bq[:4096].reshape(32, 128).T)
    nc = build_program(upto=7)
    in_maps = []
    for c in range(NCORES):
        m = dict(
            xin=np.ascontiguousarray(x[c * T:(c + 1) * T]),
            wqkv=wqkv,
            bq_cols=bq_cols,
            bq_vrow=np.ascontiguousarray(bq[4096:].reshape(1, 2048)),
            wout=wout_a,
            bo_row=np.ascontiguousarray(bo.reshape(1, 2048)),
            slopes=slopes_t,
            wfc_sh=np.ascontiguousarray(wfc[c * (FF // 8):(c + 1) * (FF // 8)]),
            wpr_sh=np.ascontiguousarray(wpr[:, c * (FF // 8):(c + 1) * (FF // 8)]),
        )
        m.update(make_consts(c))
        in_maps.append(m)
    res = run_bass_kernel_spmd(nc, in_maps, core_ids=list(range(NCORES)))
    out = np.concatenate([r["yout"] for r in res.results], axis=0)
    return np.ascontiguousarray(out.reshape(1, 8192, 2048))


# version 03
# speedup vs baseline: 1.0648x; 1.0326x over previous
import contextlib
import numpy as np
import concourse.bass as bass
import concourse.mybir as mybir
from concourse.bass_utils import run_bass_kernel_spmd

F32 = mybir.dt.float32
F32R = mybir.dt.float32r
BF16 = mybir.dt.bfloat16
FP8 = mybir.dt.float8e4
AF = mybir.ActivationFunctionType
ALU = mybir.AluOpType
AX = mybir.AxisListType

NCORES = 8
T = 1024
D = 2048
H = 16
CB = 128
NCH = T // CB
FF = 8192
KC = D // 128
TC = T // 128
EPS = 1e-5

SB_BASE = 16896
SB_TAB = SB_BASE + 0
SB_SBF = SB_BASE + 16384
SB_SCR = SB_BASE + 20480
SB_STRA = SB_BASE + 28672
SB_STRB = SB_BASE + 36864
SB_BIGA = SB_BASE + 69632
SB_BIGB = SB_BASE + 102400
SB_BIGC = SB_BASE + 135168
SB_BIGD = SB_BASE + 167936


class Prog:
    ENGS = ("sync", "scalar", "vector", "tensor", "gpsimd")

    def __init__(self, nc, es):
        self.nc = nc
        self.streams = {k: [] for k in self.ENGS}
        self.sems = {}
        self.ctr = {}
        for n in ("LD", "LG", "LS", "MM", "VE", "SE", "GE", "CC", "ST"):
            self.sems[n] = es.enter_context(nc.semaphore(n))
            self.ctr[n] = 0

    def op(self, eng, fn, wait=(), inc=None, incn=1):
        post = None
        if inc is not None:
            self.ctr[inc] += incn
            post = self.ctr[inc]
        sems = self.sems

        def run(e, fn=fn, wait=tuple(wait), inc=inc, incn=incn):
            for (n, v) in wait:
                if v > 0:
                    e.wait_ge(sems[n], v)
            r = fn(e)
            if inc is not None:
                if r is None:
                    r = e.nop()
                r.then_inc(sems[inc], incn)
        self.streams[eng].append(run)
        return post

    def raw(self, eng, fn):
        self.streams[eng].append(fn)


def build_program(upto=10, debug=(), subph=99):
    import os as _os
    V4 = int(_os.environ.get("V4", "9"))
    nc = bass.Bass("TRN2", target_bir_lowering=False)

    xin = nc.dram_tensor("xin", [T, D], F32, kind="ExternalInput").ap()
    wqkv = nc.dram_tensor("wqkv", [D, 3 * D], F32, kind="ExternalInput").ap()
    bq_cols = nc.dram_tensor("bq_cols", [128, 32], F32, kind="ExternalInput").ap()
    bq_vrow = nc.dram_tensor("bq_vrow", [1, D], F32, kind="ExternalInput").ap()
    wout = nc.dram_tensor("wout", [D, D], F32, kind="ExternalInput").ap()
    bo_row = nc.dram_tensor("bo_row", [1, D], F32, kind="ExternalInput").ap()
    slopes = nc.dram_tensor("slopes", [128, H], F32, kind="ExternalInput").ap()
    wfc_sh = nc.dram_tensor("wfc_sh", [FF // 8, D], F32, kind="ExternalInput").ap()
    wpr_sh = nc.dram_tensor("wpr_sh", [D, FF // 8], F32, kind="ExternalInput").ap()
    identb = nc.dram_tensor("identb", [128, 128], BF16, kind="ExternalInput").ap()
    ident8 = nc.dram_tensor("ident8", [128, 128], FP8, kind="ExternalInput").ap()
    bciota = nc.dram_tensor("bciota", [128, 128], F32, kind="ExternalInput").ap()
    difft = nc.dram_tensor("difft", [128, 128], F32, kind="ExternalInput").ap()
    iotac = nc.dram_tensor("iotac", [128, 1], F32, kind="ExternalInput").ap()
    dvecbc = nc.dram_tensor("dvecbc", [128, 8], F32, kind="ExternalInput").ap()

    yout = nc.dram_tensor("yout", [T, D], F32, kind="ExternalOutput").ap()

    agt_in = nc.dram_tensor("agt_in", [128, H * 128], BF16).ap()
    agt_out = nc.dram_tensor("agt_out", [NCORES * 128, H * 128], BF16).ap()
    arw_in = nc.dram_tensor("arw_in", [1, 128], F32).ap()
    arw_out = nc.dram_tensor("arw_out", [1, 128], F32).ap()
    arx_in = nc.dram_tensor("arx_in", [1, 128], F32).ap()
    arx_out = nc.dram_tensor("arx_out", [1, 128], F32).ap()
    arh_in = nc.dram_tensor("arh_in", [1, 128], F32).ap()
    arh_out = nc.dram_tensor("arh_out", [1, 128], F32).ap()
    w8f_in = nc.dram_tensor("w8f_in", [D, FF // 8], FP8).ap()
    w8f_out = nc.dram_tensor("w8f_out", [NCORES * D, FF // 8], FP8).ap()
    w8p_in = nc.dram_tensor("w8p_in", [FF // 8, D], FP8).ap()
    w8p_out = nc.dram_tensor("w8p_out", [NCORES * (FF // 8), D], FP8).ap()
    gdram = nc.dram_tensor("gdram", [FF // 128, 128, T], BF16).ap()

    def sb(name, shape, dt, off):
        return nc.alloc_sbuf_tensor_at(name, shape, dt, offset=off).ap()

    dq_t = sb("dq_t", [128, H * 128], BF16, SB_TAB + 0)
    din_t = sb("din_t", [128, H * 128], BF16, SB_TAB + 4096)
    bq_vr = sb("bq_vr", [1, D], BF16, SB_TAB + 8192)
    bo_r = sb("bo_r", [1, D], BF16, SB_TAB + 12288)
    kd_c = sb("kd_c", [128, H], F32, SB_SCR + 0)
    bd_c = sb("bd_c", [128, H], F32, SB_SCR + 64)
    sl_c = sb("sl_c", [128, H], F32, SB_SCR + 128)
    bqc = sb("bqc", [128, 32], F32, SB_SCR + 192)
    scal = sb("scal", [128, 16], F32, SB_SCR + 320)
    s11 = sb("s11", [1, 32], F32, SB_SCR + 384)
    bcw = sb("bcw", [128, 128], F32, SB_SCR + 512)
    idb = sb("idb", [128, 128], BF16, SB_SCR + 1024)
    id8 = sb("id8", [128, 128], FP8, SB_SCR + 1280)
    onesb = sb("onesb", [1, 512], BF16, SB_SCR + 1408)
    onesr_f = sb("onesr", [1, 128], F32, SB_SCR + 2432)
    onesr = sb("onesr_r", [1, 128], F32R, SB_SCR + 5760)
    s11r = sb("s11r", [1, 8], F32R, SB_SCR + 6272)
    bcio = sb("bcio", [128, 128], F32, SB_SCR + 2944)
    difft_s = sb("difft_s", [128, 128], F32, SB_SCR + 3456)
    iotac_s = sb("iotac_s", [128, 1], F32, SB_SCR + 3968)
    dvec_s = sb("dvec_s", [128, 8], F32, SB_SCR + 4000)
    mcol = sb("mcol", [128, 1], F32, SB_SCR + 4032)
    bnst = sb("bnst", [128, 24], F32, SB_SCR + 4064)
    bnag = sb("bnag", [128, 2], F32, SB_SCR + 4160)
    rstd = sb("rstd", [128, 1], F32, SB_SCR + 4192)
    qd_s = sb("qd_s", [128, 2 * 128], BF16, SB_SCR + 4224)
    st_s = sb("st_s", [128, 2 * 128], BF16, SB_SCR + 4736)
    tabtmp = sb("tabtmp", [128, 128], F32, SB_SCR + 5248)
    qtm8 = sb("qtm8", [128, D], FP8, SB_STRB + 24576)

    sbf = sb("sbf", [128, H * 128], BF16, SB_SBF)

    wqslab = [sb(f"wqslab{i}", [128, KC * 128], BF16, SB_STRA + i * 4096) for i in range(2)]
    w8psl = [sb(f"w8psl{i}", [128, 8 * 512], FP8, SB_STRA + i * 4096) for i in range(2)]
    xres = [sb(f"xres{i}", [128, 512], F32, SB_STRA + i * 2048) for i in range(2)]

    xtm = sb("xtm", [128, TC * D], BF16, SB_STRB)
    vslab = [sb(f"vslab{i}", [128, KC * 512], BF16, SB_STRB + i * 16384) for i in range(2)]
    wsh_f32 = sb("wsh_f32", [128, D], F32, SB_STRB + 0)
    wsh_f32b = sb("wsh_f32b", [128, D], F32, SB_STRB + 8192)
    wsh_f8 = sb("wsh_f8", [128, D], FP8, SB_STRB + 8192)
    wsh_stg = sb("wsh_stg", [128, 16 * 1024], FP8, SB_STRB + 10752)
    tjin = sb("tjin", [128, H * 128], BF16, SB_STRB + 27136)
    sini = sb("sini", [128, H * 128], F32, SB_STRB + 0)
    attnt = sb("attnt", [128, H * 1024], BF16, SB_STRB)
    w8slab = [sb(f"w8slab{i}", [128, KC * 512], FP8, SB_STRB + i * 8192) for i in range(2)]
    rstg = [sb(f"rstg{i}", [128, 512], BF16, SB_STRB + 16384 + i * 1024) for i in range(2)]
    lnscr2 = sb("lnscr2", [128, D], F32, SB_BIGA + 0)
    g8ab = [sb("g8a", [128, 64 * 256], FP8, SB_STRB + 16384),
            sb("g8b", [128, 64 * 256], FP8, SB_STRB + 0)]

    qfm = sb("qfm", [128, H * 1024], BF16, SB_BIGA)
    woslab = [sb(f"woslab{i}", [128, KC * 512], BF16, SB_BIGA + i * 16384) for i in range(2)]
    x8fm = sb("x8fm", [128, KC * T], FP8, SB_BIGA)
    rload = sb("rload", [128, 32 * 256], BF16, SB_BIGA + 16384)
    kfm = sb("kfm", [128, H * 1024], BF16, SB_BIGB)
    x2t = [sb(f"x2t{i}", [128, D], F32, SB_BIGB + 16384 + i * 8192) for i in range(2)]
    vtm = sb("vtm", [128, H * 1024], BF16, SB_BIGC)
    xfm = sb("xfm", [128, KC * T], BF16, SB_BIGD)
    kdt = sb("kdt", [128, H * 1024], BF16, SB_BIGD)
    x1 = sb("x1", [128, TC * D], F32, SB_BIGC)

    pf = [nc.alloc_psum_tensor(f"pf{i}", [128, 512], F32).ap() for i in range(6)]
    ptr = nc.alloc_psum_tensor("ptr", [128, 1024], BF16).ap()
    paux = nc.alloc_psum_tensor("paux", [128, 512], F32).ap()
    ptr8 = ptr.bitcast(FP8)
    ptr2 = paux.bitcast(BF16)
    ptr28 = paux.bitcast(FP8)

    es = contextlib.ExitStack()
    P = Prog(nc, es)
    opn = P.op
    RG = [list(range(NCORES))]

    dbg_list = []

    def dbg(name, apget, shape, dt):
        if name in debug:
            dap = nc.dram_tensor("dbg_" + name, shape, dt, kind="ExternalOutput").ap()
            dbg_list.append((dap, apget))

    for dst, src in [(idb, identb), (id8, ident8), (bcio, bciota),
                     (difft_s, difft), (iotac_s, iotac), (dvec_s, dvecbc),
                     (sl_c, slopes), (bqc, bq_cols)]:
        opn("sync", lambda e, d=dst, s=src: e.dma_start(out=d[:, :], in_=s[:, :]),
            inc="LD", incn=16)
    consts_ld = P.ctr["LD"]

    opn("gpsimd", lambda e: e.dma_start(out=bq_vr[:, :], in_=bq_vrow[:, :]),
        inc="LG", incn=16)
    opn("gpsimd", lambda e: e.dma_start(out=bo_r[:, :], in_=bo_row[:, :]),
        inc="LG", incn=16)
    for tc in range(TC):
        opn("gpsimd", lambda e, tc=tc: e.dma_start(
            out=xtm[:, tc * D:(tc + 1) * D], in_=xin[tc * 128:(tc + 1) * 128, :]),
            inc="LG", incn=16)
    x_ld = P.ctr["LG"]

    first = True
    for h in range(H):
        w = [("LD", consts_ld)] if first else [("SE", P.ctr["SE"])]
        ve = opn("vector", lambda e, h=h: e.tensor_scalar(
            tabtmp[:, :], bcio[:, :], sl_c[:, h:h + 1], -1.0, ALU.mult, ALU.mult),
            wait=w, inc="VE")
        opn("scalar", lambda e, h=h: e.activation(
            dq_t[:, h * 128:(h + 1) * 128], tabtmp[:, :], AF.Exp),
            wait=[("VE", ve)], inc="SE")
        ve = opn("vector", lambda e, h=h: e.tensor_scalar(
            tabtmp[:, :], difft_s[:, :], sl_c[:, h:h + 1], -1.0, ALU.mult, ALU.mult),
            wait=[("SE", P.ctr["SE"])], inc="VE")
        ge = opn("gpsimd", lambda e: e.affine_select(
            tabtmp[:, :], tabtmp[:, :], pattern=[[1, 128]], compare_op=ALU.is_ge,
            fill=-1e5, base=0, channel_multiplier=-1),
            wait=[("VE", ve)], inc="GE")
        opn("scalar", lambda e, h=h: e.activation(
            din_t[:, h * 128:(h + 1) * 128], tabtmp[:, :], AF.Exp),
            wait=[("GE", ge)], inc="SE")
        first = False

    def cols_v(e):
        e.memset(onesb[:, :], 1.0)
        e.memset(onesr_f[:, :], 1.0)
        for h in range(H):
            e.tensor_scalar(kd_c[:, h:h + 1], iotac_s[:, :], sl_c[:, h:h + 1],
                            None, ALU.mult)
        e.tensor_scalar(bd_c[:, :], sl_c[:, :], -128.0, None, ALU.mult)
        r = None
        for h in range(H):
            r = e.tensor_scalar(bcw[:, h * 8:(h + 1) * 8], dvec_s[:, :],
                                sl_c[:, h:h + 1], -1.0, ALU.mult, ALU.mult)
        return r
    ve = opn("vector", cols_v, wait=[("SE", P.ctr["SE"])], inc="VE")

    def cols_s(e):
        e.activation(kd_c[:, :], kd_c[:, :], AF.Exp)
        e.activation(bd_c[:, :], bd_c[:, :], AF.Exp)
        return e.activation(bcw[:, :], bcw[:, :], AF.Exp)
    tabs_se = opn("scalar", cols_s, wait=[("VE", ve)], inc="SE")

    xtr_ve = []
    n = 0
    for tc in range(TC):
        for k in range(KC):
            half = n % 2
            w = [("LG", x_ld), ("LD", consts_ld)] if n == 0 else []
            if n >= 2:
                w = [("VE", xtr_ve[n - 2])]
            mmv = opn("tensor", lambda e, tc=tc, k=k, half=half: e.transpose(
                (ptr if half == 0 else ptr2)[:, 0:128],
                xtm[:, tc * D + k * 128: tc * D + (k + 1) * 128], idb[:, :]),
                wait=w, inc="MM")
            if n % 8 == 7:
                P.raw("tensor", lambda e: e.drain())
            xtr_ve.append(opn("vector", lambda e, tc=tc, k=k, half=half: e.tensor_copy(
                xfm[:, k * T + tc * 128: k * T + (tc + 1) * 128],
                (ptr if half == 0 else ptr2)[:, 0:128]),
                wait=[("MM", mmv)], inc="VE"))
            n += 1
    xfm_ve = P.ctr["VE"]
    dbg("xfm", lambda: xfm[:, :], [128, KC * T], BF16)

    if upto >= 1:
        qk_mm = []
        qk_ve = []
        qk_ld = []

        def reg_qk_ld(i):
            w = []
            if i >= 2:
                w = [("MM", qk_mm[(i - 2) * 2 + 1])]
            qk_ld.append(opn("gpsimd", lambda e, i=i: e.dma_start(
                out=wqslab[i % 2][:, :].rearrange("p (kc fc) -> p kc fc", kc=KC),
                in_=wqkv.rearrange("(kc p) f -> p kc f", p=128)[:, :, i * 128:(i + 1) * 128]),
                wait=w, inc="LG", incn=16))

        reg_qk_ld(0)
        reg_qk_ld(1)
        gidx = 0
        for i in range(32):
            for th in range(2):
                bank = gidx % 4
                w = [("LG", qk_ld[i])]
                if gidx == 0:
                    w.append(("VE", xfm_ve))
                if gidx >= 4:
                    w.append(("VE", qk_ve[gidx - 4]))

                def mmf(e, i=i, th=th, bank=bank):
                    r = None
                    for k in range(KC):
                        r = e.matmul(pf[bank][:, :],
                                     wqslab[i % 2][:, k * 128:(k + 1) * 128],
                                     xfm[:, k * T + th * 512: k * T + th * 512 + 512],
                                     start=(k == 0), stop=(k == KC - 1))
                    return r
                qk_mm.append(opn("tensor", mmf, wait=w, inc="MM"))

                def evf(e, i=i, th=th, bank=bank):
                    dst = qfm if i < 16 else kfm
                    h = i % 16
                    return e.tensor_scalar(
                        dst[:, h * T + th * 512: h * T + th * 512 + 512],
                        pf[bank][:, :], bqc[:, i:i + 1], None, ALU.add)
                qk_ve.append(opn("vector", evf, wait=[("MM", qk_mm[-1])], inc="VE"))
                gidx += 1
            if i + 2 < 32:
                reg_qk_ld(i + 2)

        v_ld = []
        v_mm = []
        v_ve = []

        def reg_v_ld(fg):
            w = []
            if fg == 0:
                w = [("VE", xfm_ve)]
            if fg >= 2:
                w = [("MM", v_mm[(fg - 2) * TC + TC - 1])]
            v_ld.append(opn("gpsimd", lambda e, fg=fg: e.dma_start(
                out=vslab[fg % 2][:, :].rearrange("p (kc fc) -> p kc fc", kc=KC),
                in_=wqkv.rearrange("(kc p) f -> p kc f", p=128)[
                    :, :, 2 * D + fg * 512: 2 * D + (fg + 1) * 512]),
                wait=w, inc="LG", incn=16))

        reg_v_ld(0)
        reg_v_ld(1)
        gidx = 0
        for fg in range(4):
            for tc in range(TC):
                bank = gidx % 4
                w = [("LG", v_ld[fg]), ("SE", tabs_se)]
                if gidx == 0:
                    w.append(("VE", qk_ve[-1]))
                if gidx >= 4:
                    w.append(("VE", v_ve[gidx - 4]))

                def mmv(e, fg=fg, tc=tc, bank=bank):
                    for k in range(KC):
                        e.matmul(pf[bank][:, :],
                                 xfm[:, k * T + tc * 128:k * T + (tc + 1) * 128],
                                 vslab[fg % 2][:, k * 512:(k + 1) * 512],
                                 start=(k == 0), stop=False)
                    return e.matmul(pf[bank][:, :], onesb[0:1, 0:128],
                                    bq_vr[0:1, fg * 512:(fg + 1) * 512],
                                    start=False, stop=True)
                v_mm.append(opn("tensor", mmv, wait=w, inc="MM"))

                def evv(e, fg=fg, tc=tc, bank=bank):
                    src = pf[bank][:, :].rearrange("p (j e) -> p j e", j=4)
                    dst = vtm[:, :].rearrange("p (h c e) -> p h c e", h=H, c=NCH)[
                        :, fg * 4:fg * 4 + 4, tc, :]
                    return e.tensor_copy(dst, src)
                v_ve.append(opn("vector", evv, wait=[("MM", v_mm[-1])], inc="VE"))
                gidx += 1
            if fg + 2 < 4:
                reg_v_ld(fg + 2)

        ktr_ve = []
        n = 0
        for h in range(H):
            for c in range(NCH):
                half = n % 2
                w = []
                if n >= 2:
                    w = [("VE", ktr_ve[n - 2])]
                mmv = opn("tensor", lambda e, h=h, c=c, half=half: e.transpose(
                    (ptr if half == 0 else ptr2)[:, 0:128],
                    kfm[:, h * T + c * 128: h * T + (c + 1) * 128], idb[:, :]),
                    wait=w, inc="MM")
                if n % 8 == 7:
                    P.raw("tensor", lambda e: e.drain())
                ktr_ve.append(opn("vector", lambda e, h=h, c=c, half=half:
                                  e.tensor_scalar(
                                      kdt[:, (h * NCH + c) * 128:(h * NCH + c + 1) * 128],
                                      (ptr if half == 0 else ptr2)[:, 0:128],
                                      kd_c[:, h:h + 1], None, ALU.mult),
                                  wait=[("MM", mmv)], inc="VE"))
                n += 1
        kdt_ve = P.ctr["VE"]
        last_v_mm = v_mm[-1]
        dbg("qfm", lambda: qfm[:, :], [128, H * 1024], BF16)
        dbg("kfm", lambda: kfm[:, :], [128, H * 1024], BF16)
        dbg("vtm", lambda: vtm[:, :], [128, H * 1024], BF16)
        dbg("kdt", lambda: kdt[:, :], [128, H * 1024], BF16)

    if upto >= 2:
        sz_ve = opn("vector", lambda e: e.memset(sbf[:, :], 0.0), inc="VE")
        p1_mm = []
        p1_ve = []
        nidx = 0
        for h in range(H):
            for c in range(NCH):
                bank = 4 + (nidx % 2)
                w = [("VE", kdt_ve)] if nidx == 0 else []
                if nidx >= 2:
                    w.append(("VE", p1_ve[nidx - 2]))

                def smm(e, h=h, c=c, bank=bank):
                    i = h * NCH + c
                    return e.matmul(pf[bank][:, 0:128],
                                    kdt[:, i * 128:(i + 1) * 128],
                                    vtm[:, i * 128:(i + 1) * 128],
                                    start=True, stop=True)
                p1_mm.append(opn("tensor", smm, wait=w, inc="MM"))
                if nidx % 4 == 3:
                    P.raw("tensor", lambda e: e.drain())

                def sstt(e, h=h, bank=bank):
                    return e.scalar_tensor_tensor(
                        sbf[:, h * 128:(h + 1) * 128], sbf[:, h * 128:(h + 1) * 128],
                        bd_c[:, h:h + 1], pf[bank][:, 0:128], ALU.mult, ALU.add)
                p1_ve.append(opn("vector", sstt,
                                 wait=[("MM", p1_mm[-1]), ("VE", sz_ve)], inc="VE"))
                nidx += 1
        p1_done_ve = P.ctr["VE"]

        if subph < 2:
            upto = 0
        texp = None
        if subph >= 2:
            texp = opn("sync", lambda e: e.dma_start(out=agt_in[:, :], in_=sbf[:, :]),
                       wait=[("VE", p1_done_ve)], inc="LD", incn=16)
        agT = None
        if subph >= 2:
            agT = opn("gpsimd", lambda e: e.collective_compute(
                "AllGather", ALU.bypass, replica_groups=RG,
                ins=[agt_in.opt()], outs=[agt_out.opt()]),
                wait=[("LD", texp)], inc="CC")

        if subph < 3:
            upto = 0
        wsh_ld = []
        if subph >= 3:
            wsh_ve = []
            for i in range(24):
                if i < 2:
                    w = [("MM", last_v_mm)]
                else:
                    w = [("VE", wsh_ve[i - 2])]

                def lsh(e, i=i, buf=(wsh_f32, wsh_f32b)[i % 2]):
                    if i < 8:
                        return e.dma_start(out=buf[:, :],
                                           in_=wfc_sh[i * 128:(i + 1) * 128, :])
                    j = i - 8
                    return e.dma_start(out=buf[:, 0:1024],
                                       in_=wpr_sh[j * 128:(j + 1) * 128, :])
                wsh_ld.append(opn("scalar", lsh, wait=w, inc="LS", incn=16))

                def msh(e, i=i, buf=(wsh_f32, wsh_f32b)[i % 2]):
                    width = D if i < 8 else 1024
                    col = 8 if i < 8 else 9
                    e.tensor_reduce(mcol[:, :], buf[:, 0:width], AX.X, ALU.max,
                                    apply_absolute_value=True)
                    e.drain()
                    if i == 0 or i == 8:
                        r = e.tensor_copy(scal[:, col:col + 1], mcol[:, :])
                    else:
                        r = e.tensor_tensor(scal[:, col:col + 1],
                                            scal[:, col:col + 1], mcol[:, :],
                                            ALU.max)
                    e.drain()
                    return r
                wsh_ve.append(opn("vector", msh, wait=[("LS", wsh_ld[-1])], inc="VE"))

            def gmax_f(e):
                e.tensor_reduce(scal[0:1, 8:9], scal[:, 8:9], AX.C, ALU.max)
                return e.tensor_reduce(scal[0:1, 9:10], scal[:, 9:10], AX.C, ALU.max)
            gmax = opn("gpsimd", gmax_f, wait=[("VE", wsh_ve[-1])], inc="GE")

            def stgw_f(e):
                e.dma_start(out=arw_in[0:1, 0:1], in_=scal[0:1, 8:9]).then_inc(
                    P.sems["LG"], 16)
                return e.dma_start(out=arw_in[0:1, 1:2], in_=scal[0:1, 9:10])
            opn("gpsimd", stgw_f, wait=[("GE", gmax)], inc="LG", incn=16)
            P.ctr["LG"] += 16
            stgw = P.ctr["LG"]
            arw = opn("gpsimd", lambda e: e.collective_compute(
                "AllReduce", ALU.max, replica_groups=RG,
                ins=[arw_in.opt()], outs=[arw_out.opt()]),
                wait=[("LG", stgw)], inc="CC")

            def ldw_f(e):
                e.dma_start(out=scal[:, 14:15],
                            in_=arw_out[0:1, 0:1].broadcast_to((128, 1))).then_inc(
                    P.sems["LG"], 16)
                return e.dma_start(out=scal[:, 15:16],
                                   in_=arw_out[0:1, 1:2].broadcast_to((128, 1)))
            opn("gpsimd", ldw_f, wait=[("CC", arw)], inc="LG", incn=16)
            P.ctr["LG"] += 16
            ldw = P.ctr["LG"]

            def wscal(e):
                e.tensor_scalar(scal[:, 5:7], scal[:, 14:16], 1.0 / 448.0, 1e-6,
                                ALU.mult, ALU.max)
                e.drain()
                e.reciprocal(scal[:, 0:2], scal[:, 5:7])
                e.drain()
                r = e.tensor_scalar(scal[:, 0:2], scal[:, 0:2], 0.5, None, ALU.mult)
                e.drain()
                return r
            bc_ve = opn("vector", wscal, wait=[("LG", ldw)], inc="VE")

            if subph < 6:
                upto = 0
            if subph >= 4:
                for i in range(8):
                    w = [("VE", bc_ve)] if i == 0 else [("VE", P.ctr["VE"])]
                    ldq = opn("scalar", lambda e, i=i: e.dma_start(
                        out=wsh_f32[:, :], in_=wfc_sh[i * 128:(i + 1) * 128, :]),
                        wait=w, inc="LS", incn=16)
                    qv = opn("vector", lambda e: e.tensor_scalar(
                        wsh_f8[:, :], wsh_f32[:, :], scal[:, 0:1], None, ALU.mult),
                        wait=[("LS", ldq)], inc="VE")
                    prev_tr = []
                    for k in range(KC):
                        half = k % 2
                        w2 = [("VE", qv)]
                        if k >= 2:
                            w2.append(("VE", prev_tr[k - 2]))
                        tmm = opn("tensor", lambda e, k=k, half=half: e.transpose(
                            (ptr8 if half == 0 else ptr28)[:, 0:256:2],
                            wsh_f8[:, k * 128:(k + 1) * 128], id8[:, :]),
                            wait=w2, inc="MM")
                        if k % 8 == 7:
                            P.raw("tensor", lambda e: e.drain())
                        prev_tr.append(opn("vector", lambda e, k=k, half=half, i=i:
                                           e.tensor_copy(
                                               wsh_stg[:, k * 1024 + i * 128:
                                                       k * 1024 + (i + 1) * 128],
                                               (ptr8 if half == 0 else ptr28)[:, 0:256:2]),
                                           wait=[("MM", tmm)], inc="VE"))
                stg_f = opn("sync", lambda e: e.dma_start(
                    out=w8f_in.rearrange("(kc p) u -> p kc u", p=128),
                    in_=wsh_stg[:, :].rearrange("p (kc u) -> p kc u", kc=KC)),
                    wait=[("VE", P.ctr["VE"])], inc="LD", incn=16)
                agF = opn("gpsimd", lambda e: e.collective_compute(
                    "AllGather", ALU.bypass, replica_groups=RG,
                    ins=[w8f_in.opt()], outs=[w8f_out.opt()]),
                    wait=[("LD", stg_f), ("CC", agT)], inc="CC")

            if subph >= 5:
                for i in range(16):
                    w = [("LD", stg_f)] if i == 0 else [("VE", P.ctr["VE"])]
                    ldq = opn("scalar", lambda e, i=i: e.dma_start(
                        out=wsh_f32[:, 0:1024], in_=wpr_sh[i * 128:(i + 1) * 128, :]),
                        wait=w, inc="LS", incn=16)
                    qv = opn("vector", lambda e: e.tensor_scalar(
                        wsh_f8[:, 0:1024], wsh_f32[:, 0:1024], scal[:, 1:2], None, ALU.mult),
                        wait=[("LS", ldq)], inc="VE")
                    prev_tr = []
                    for k in range(8):
                        half = k % 2
                        w2 = [("VE", qv)]
                        if k >= 2:
                            w2.append(("VE", prev_tr[k - 2]))
                        tmm = opn("tensor", lambda e, k=k, half=half: e.transpose(
                            (ptr8 if half == 0 else ptr28)[:, 0:256:2],
                            wsh_f8[:, k * 128:(k + 1) * 128], id8[:, :]),
                            wait=w2, inc="MM")
                        prev_tr.append(opn("vector", lambda e, k=k, half=half, i=i:
                                           e.tensor_copy(
                                               wsh_stg[:, k * 2048 + i * 128:
                                                       k * 2048 + (i + 1) * 128],
                                               (ptr8 if half == 0 else ptr28)[:, 0:256:2]),
                                           wait=[("MM", tmm)], inc="VE"))
                    P.raw("tensor", lambda e: e.drain())
                stg_p = opn("sync", lambda e: e.dma_start(
                    out=w8p_in.rearrange("(kc p) o -> p kc o", p=128),
                    in_=wsh_stg[:, :].rearrange("p (kc o) -> p kc o", kc=8)),
                    wait=[("VE", P.ctr["VE"])], inc="LD", incn=16)
                agP = opn("gpsimd", lambda e: e.collective_compute(
                    "AllGather", ALU.bypass, replica_groups=RG,
                    ins=[w8p_in.opt()], outs=[w8p_out.opt()]),
                    wait=[("LD", stg_p), ("CC", agF)], inc="CC")

            if subph >= 6:
                comb_last = None
                for r in range(NCORES):
                    wl = [("CC", agT)]
                    if r == 0:
                        wl.append(("VE", P.ctr["VE"]))
                    else:
                        wl.append(("VE", comb_last))
                    lt = opn("sync", lambda e, r=r: e.dma_start(
                        out=tjin[:, :], in_=agt_out[r * 128:(r + 1) * 128, :]),
                        wait=wl, inc="LD", incn=16)
                    for h in range(H):
                        def cmb(e, r=r, h=h):
                            if r == 0:
                                return e.tensor_scalar(
                                    sini[:, h * 128:(h + 1) * 128],
                                    tjin[:, h * 128:(h + 1) * 128],
                                    bcw[:, h * 8 + r:h * 8 + r + 1], None, ALU.mult)
                            return e.scalar_tensor_tensor(
                                sini[:, h * 128:(h + 1) * 128],
                                tjin[:, h * 128:(h + 1) * 128],
                                bcw[:, h * 8 + r:h * 8 + r + 1],
                                sini[:, h * 128:(h + 1) * 128], ALU.mult, ALU.add)
                        comb_last = opn("vector", cmb,
                                        wait=[("LD", lt)] if h == 0 else [], inc="VE")
                sinit_ve = opn("vector", lambda e: e.tensor_copy(sbf[:, :], sini[:, :]),
                               inc="VE")
                dbg("sinit", lambda: sbf[:, :], [128, H * 128], BF16)
    dbg("bcw", lambda: bcw[:, :], [128, 128], F32)
    dbg("kdc", lambda: kd_c[:, :], [128, H], F32)

    if upto >= 3:
        p2_sco, p2_stv, p2_qdv, p2_omm, p2_umm, p2_oev, p2_stt = ([] for _ in range(7))
        nidx = 0
        for h in range(H):
            for c in range(NCH):
                p = nidx % 2
                A, B, Cb = pf[0 + p], pf[2 + p], pf[4 + p]
                i = h * NCH + c
                w = [("VE", sinit_ve)] if nidx == 0 else []
                if nidx >= 2:
                    w.append(("VE", p2_stv[nidx - 2]))
                p2_sco.append(opn("tensor", lambda e, i=i, A=A: e.matmul(
                    A[:, 0:128], kfm[:, i * 128:(i + 1) * 128],
                    qfm[:, i * 128:(i + 1) * 128], start=True, stop=True),
                    wait=w, inc="MM"))

                w2 = [("MM", p2_sco[-1])]
                if nidx >= 2:
                    w2.append(("MM", p2_omm[nidx - 2]))
                p2_stv.append(opn("vector", lambda e, h=h, p=p, A=A:
                                  e.tensor_tensor(st_s[:, p * 128:(p + 1) * 128],
                                                  A[:, 0:128],
                                                  din_t[:, h * 128:(h + 1) * 128],
                                                  ALU.mult), wait=w2, inc="VE"))
                w3 = []
                if nidx >= 2:
                    w3.append(("MM", p2_omm[nidx - 2]))
                p2_qdv.append(opn("vector", lambda e, i=i, h=h, p=p:
                                  e.tensor_tensor(qd_s[:, p * 128:(p + 1) * 128],
                                                  qfm[:, i * 128:(i + 1) * 128],
                                                  dq_t[:, h * 128:(h + 1) * 128],
                                                  ALU.mult), wait=w3, inc="VE"))
                w4 = [("VE", p2_qdv[-1])]
                if nidx >= 2:
                    w4.append(("SE", p2_oev[nidx - 2]))
                if nidx >= 1:
                    w4.append(("VE", p2_stt[nidx - 1]))

                def mo(e, i=i, h=h, p=p, B=B):
                    e.matmul(B[:, 0:128], vtm[:, i * 128:(i + 1) * 128],
                             st_s[:, p * 128:(p + 1) * 128], start=True, stop=False)
                    return e.matmul(B[:, 0:128], sbf[:, h * 128:(h + 1) * 128],
                                    qd_s[:, p * 128:(p + 1) * 128],
                                    start=False, stop=True)
                p2_omm.append(opn("tensor", mo, wait=w4, inc="MM"))

                w5 = []
                if nidx >= 2:
                    w5.append(("VE", p2_stt[nidx - 2]))
                p2_umm.append(opn("tensor", lambda e, i=i, Cb=Cb: e.matmul(
                    Cb[:, 0:128], kdt[:, i * 128:(i + 1) * 128],
                    vtm[:, i * 128:(i + 1) * 128], start=True, stop=True),
                    wait=w5, inc="MM"))
                if nidx % 2 == 1:
                    P.raw("tensor", lambda e: e.drain())
                p2_oev.append(opn("scalar", lambda e, i=i, B=B: e.activation(
                    attnt[:, i * 128:(i + 1) * 128], B[:, 0:128], AF.Copy),
                    wait=[("MM", p2_omm[-1])], inc="SE"))
                p2_stt.append(opn("vector", lambda e, h=h, Cb=Cb:
                                  e.scalar_tensor_tensor(
                                      sbf[:, h * 128:(h + 1) * 128],
                                      sbf[:, h * 128:(h + 1) * 128],
                                      bd_c[:, h:h + 1], Cb[:, 0:128],
                                      ALU.mult, ALU.add),
                                  wait=[("MM", p2_umm[-1])], inc="VE"))
                nidx += 1
        pass2_se = P.ctr["SE"]
        pass2_ve = P.ctr["VE"]
        dbg("attnt", lambda: attnt[:, :], [128, H * 1024], BF16)

    if upto >= 4:
        op_mm = []
        op_ve = []
        wo_ld = []
        xr_ld = []

        def reg_wo_ld(og):
            w = [("VE", pass2_ve), ("SE", pass2_se)]
            if og >= 2:
                w = [("MM", op_mm[(og - 2) * TC + TC - 1])]
            wo_ld.append(opn("gpsimd", lambda e, og=og: e.dma_start(
                out=woslab[og % 2][:, :].rearrange("p (kc oc) -> p kc oc", kc=KC),
                in_=wout.rearrange("(kc p) o -> p kc o", p=128)[
                    :, :, og * 512:(og + 1) * 512]),
                wait=w, inc="LG", incn=16))

        reg_wo_ld(0)
        reg_wo_ld(1)
        gidx = 0
        for og in range(4):
            for tc in range(TC):
                bank = gidx % 4
                wxl = []
                if gidx >= 2:
                    wxl = [("VE", op_ve[gidx - 2])]
                elif og == 0 and tc == 0:
                    wxl = [("SE", pass2_se)]
                xr_ld.append(opn("sync", lambda e, og=og, tc=tc, par=gidx % 2:
                                 e.dma_start(out=xres[par][:, :],
                                             in_=xin[tc * 128:(tc + 1) * 128,
                                                     og * 512:(og + 1) * 512]),
                                 wait=wxl, inc="LD", incn=16))

                w = [("LG", wo_ld[og]), ("LD", xr_ld[-1])]
                if gidx == 0:
                    w += [("VE", pass2_ve), ("SE", pass2_se)]
                if gidx >= 4:
                    w.append(("VE", op_ve[gidx - 4]))

                def mop(e, tc=tc, og=og, bank=bank):
                    for k in range(KC):
                        e.matmul(pf[bank][:, :],
                                 attnt[:, k * T + tc * 128:k * T + (tc + 1) * 128],
                                 woslab[og % 2][:, k * 512:(k + 1) * 512],
                                 start=(k == 0), stop=False)
                    return e.matmul(pf[bank][:, :], onesb[0:1, 0:128],
                                    bo_r[0:1, og * 512:(og + 1) * 512],
                                    start=False, stop=True)
                op_mm.append(opn("tensor", mop, wait=w, inc="MM"))

                def vev(e, tc=tc, og=og, bank=bank, par=gidx % 2):
                    return e.tensor_tensor(
                        x1[:, tc * D + og * 512: tc * D + (og + 1) * 512],
                        pf[bank][:, :], xres[par][:, :], ALU.add)
                op_ve.append(opn("vector", vev, wait=[("MM", op_mm[-1])], inc="VE"))
                gidx += 1
            if og + 2 < 4:
                reg_wo_ld(og + 2)
        opj_ve = P.ctr["VE"]

        ln_last = None
        for tc in range(TC):
            def ln1(e, tc=tc):
                xt = x1[:, tc * D:(tc + 1) * D]
                e.tensor_reduce(bnag[:, 0:1], xt, AX.X, ALU.add)
                e.drain()
                e.tensor_scalar(bnag[:, 0:1], bnag[:, 0:1], 1.0 / D, None, ALU.mult)
                e.drain()
                e.tensor_scalar(x2t[0][:, :], xt, bnag[:, 0:1], None, ALU.subtract)
                e.tensor_tensor(x2t[0][:, :], x2t[0][:, :], x2t[0][:, :], ALU.mult)
                e.tensor_reduce(bnag[:, 1:2], x2t[0][:, :], AX.X, ALU.add)
                e.drain()
                r = e.tensor_scalar(rstd[:, :], bnag[:, 1:2], 1.0 / D, EPS,
                                    ALU.mult, ALU.add)
                e.drain()
                return r
            wv = [("VE", opj_ve)] if tc == 0 else [("VE", ln_last)]
            lv = opn("vector", ln1, wait=wv, inc="VE")
            sq = opn("scalar", lambda e: e.activation(rstd[:, :], rstd[:, :], AF.Sqrt),
                     wait=[("VE", lv)], inc="SE")
            rv = opn("vector", lambda e: (e.reciprocal(rstd[:, :], rstd[:, :]),
                                          e.drain())[0],
                     wait=[("SE", sq)], inc="VE")

            def nrm(e, tc=tc):
                e.drain()
                e.tensor_scalar(x1[:, tc * D:(tc + 1) * D],
                                x1[:, tc * D:(tc + 1) * D],
                                bnag[:, 0:1], rstd[:, :], ALU.subtract, ALU.mult)
                e.tensor_reduce(mcol[:, :], x1[:, tc * D:(tc + 1) * D], AX.X,
                                ALU.max, apply_absolute_value=True)
                e.drain()
                if tc == 0:
                    r = e.tensor_copy(scal[:, 10:11], mcol[:, :])
                else:
                    r = e.tensor_tensor(scal[:, 10:11], scal[:, 10:11], mcol[:, :],
                                        ALU.max)
                e.drain()
                return r
            ln_last = opn("vector", nrm, wait=[("VE", rv)], inc="VE")
        ln1_ve = P.ctr["VE"]
        dbg("x1", lambda: x1[:, :], [128, TC * D], F32)

        gx = opn("gpsimd", lambda e: e.tensor_reduce(
            scal[0:1, 10:11], scal[:, 10:11], AX.C, ALU.max),
            wait=[("VE", ln1_ve)], inc="GE")
        sgx = opn("gpsimd", lambda e: e.dma_start(out=arx_in[0:1, 0:1],
                                                  in_=scal[0:1, 10:11]),
                  wait=[("GE", gx)], inc="LG", incn=16)
        arx = opn("gpsimd", lambda e: e.collective_compute(
            "AllReduce", ALU.max, replica_groups=RG,
            ins=[arx_in.opt()], outs=[arx_out.opt()]),
            wait=[("LG", sgx), ("CC", agP)], inc="CC")
        ldx = opn("gpsimd", lambda e: e.dma_start(
            out=scal[:, 14:15], in_=arx_out[0:1, 0:1].broadcast_to((128, 1))),
                  wait=[("CC", arx)], inc="LG", incn=16)

        def xscal(e):
            e.tensor_scalar(scal[:, 7:8], scal[:, 14:15], 1.0 / 448.0, 1e-6,
                            ALU.mult, ALU.max)
            e.drain()
            e.reciprocal(scal[:, 2:3], scal[:, 7:8])
            e.drain()
            e.tensor_scalar(scal[:, 2:3], scal[:, 2:3], 0.5, None, ALU.mult)
            e.tensor_tensor(scal[:, 13:14], scal[:, 7:8], scal[:, 5:6], ALU.mult)
            e.drain()
            r = e.tensor_scalar(scal[:, 13:14], scal[:, 13:14], 4.0, None,
                                ALU.mult)
            e.drain()
            return r
        bcx_ve = opn("vector", xscal, wait=[("LG", ldx)], inc="VE")

    if upto >= 5:
        x8_ve = []
        n = 0
        for tc in range(TC):
            wq = [("VE", bcx_ve)] if tc == 0 else [("VE", P.ctr["VE"])]
            qv = opn("vector", lambda e, tc=tc: e.tensor_scalar(
                qtm8[:, :], x1[:, tc * D:(tc + 1) * D], scal[:, 2:3], None, ALU.mult),
                wait=wq, inc="VE")
            for k in range(KC):
                half = n % 2
                w = [("VE", qv)]
                if n >= 2:
                    w.append(("VE", x8_ve[n - 2]))
                tmm = opn("tensor", lambda e, k=k, half=half: e.transpose(
                    (ptr8 if half == 0 else ptr28)[:, 0:256:2],
                    qtm8[:, k * 128:(k + 1) * 128], id8[:, :]),
                    wait=w, inc="MM")
                if n % 8 == 7:
                    P.raw("tensor", lambda e: e.drain())
                x8_ve.append(opn("vector", lambda e, k=k, tc=tc, half=half:
                                 e.tensor_copy(
                                     x8fm[:, k * T + tc * 128:k * T + (tc + 1) * 128],
                                     (ptr8 if half == 0 else ptr28)[:, 0:256:2]),
                                 wait=[("MM", tmm)], inc="VE"))
                n += 1
        x8_done = P.ctr["VE"]
        dbg("x8fm", lambda: x8fm[:, :], [128, KC * T], FP8)

    if upto >= 6:
        fc_ld, fc_mm, fc_se, fc_st, fc_vmax = [], [], [], [], []

        def reg_fc_ld(ug):
            w = [("CC", agF)]
            if ug == 0:
                w += [("MM", op_mm[-1]), ("SE", pass2_se)]
            if ug >= 2:
                w.append(("MM", fc_mm[(ug - 2) * 8 + 7]))

            def lfc(e, ug=ug):
                r, lh = ug // 2, (ug % 2) * 512
                src = w8f_out.rearrange("(r kc p) u -> p r kc u", r=NCORES, kc=KC)[
                    :, r, :, lh:lh + 512]
                return e.dma_start(
                    out=w8slab[ug % 2][:, :].rearrange("p (kc u) -> p kc u", kc=KC),
                    in_=src)
            fc_ld.append((("sync", "LD") if ug % 2 == 0 else ("scalar", "LS"),
                          opn("sync" if ug % 2 == 0 else "scalar", lfc, wait=w,
                              inc="LD" if ug % 2 == 0 else "LS", incn=16)))

        reg_fc_ld(0)
        reg_fc_ld(1)
        for ug in range(16):
            for uc4 in range(4):
                for th in range(2):
                    gi = len(fc_mm)
                    bank = gi % 4
                    w2 = [(fc_ld[ug][0][1], fc_ld[ug][1]), ("VE", x8_done)]
                    if gi >= 4:
                        w2.append(("SE", fc_se[gi - 4]))

                    def mfc(e, ug=ug, uc4=uc4, th=th, bank=bank):
                        r = None
                        for k in range(KC):
                            r = e.matmul(pf[bank][:, :],
                                         w8slab[ug % 2][:, k * 512 + uc4 * 128:
                                                        k * 512 + (uc4 + 1) * 128],
                                         x8fm[:, k * T + th * 512:
                                              k * T + th * 512 + 512],
                                         start=(k == 0), stop=(k == KC - 1))
                        return r
                    fc_mm.append(opn("tensor", mfc, wait=w2, inc="MM"))

                    w3 = [("MM", fc_mm[-1])]
                    if gi >= 2:
                        w3 += [("ST", fc_st[gi - 2]), ("VE", fc_vmax[gi - 2])]
                    fc_se.append(opn("scalar", lambda e, bank=bank, gi=gi:
                                     e.activation(rstg[gi % 2][:, :], pf[bank][:, :],
                                                  AF.Relu), wait=w3, inc="SE"))

                    def vmx(e, gi=gi):
                        e.tensor_reduce(mcol[:, :], rstg[gi % 2][:, :], AX.X,
                                        ALU.max, apply_absolute_value=True)
                        e.drain()
                        if gi == 0:
                            r = e.tensor_copy(scal[:, 11:12], mcol[:, :])
                        else:
                            r = e.tensor_tensor(scal[:, 11:12], scal[:, 11:12],
                                                mcol[:, :], ALU.max)
                        e.drain()
                        return r
                    fc_vmax.append(opn("vector", vmx, wait=[("SE", fc_se[-1])],
                                       inc="VE"))
                    uc = ug * 4 + uc4
                    fc_st.append(opn("sync", lambda e, uc=uc, th=th, gi=gi:
                                     e.dma_start(out=gdram[uc, :, th * 512:(th + 1) * 512],
                                                 in_=rstg[gi % 2][:, :]),
                                     wait=[("SE", fc_se[-1])], inc="ST", incn=16))
            if ug + 2 < 16:
                reg_fc_ld(ug + 2)
        fc_done_st = P.ctr["ST"]
        fc_done_mm = fc_mm[-1]

        gh = opn("gpsimd", lambda e: e.tensor_reduce(
            scal[0:1, 11:12], scal[:, 11:12], AX.C, ALU.max),
            wait=[("VE", P.ctr["VE"])], inc="GE")
        sgh = opn("gpsimd", lambda e: e.dma_start(out=arh_in[0:1, 0:1],
                                                  in_=scal[0:1, 11:12]),
                  wait=[("GE", gh)], inc="LG", incn=16)
        arh = opn("gpsimd", lambda e: e.collective_compute(
            "AllReduce", ALU.max, replica_groups=RG,
            ins=[arh_in.opt()], outs=[arh_out.opt()]),
            wait=[("LG", sgh)], inc="CC")
        ldh = opn("gpsimd", lambda e: e.dma_start(
            out=scal[:, 14:15], in_=arh_out[0:1, 0:1].broadcast_to((128, 1))),
            wait=[("CC", arh)], inc="LG", incn=16)

        def hscal(e):
            e.tensor_tensor(scal[:, 15:16], scal[:, 13:14], scal[:, 14:15], ALU.mult)
            e.drain()
            e.tensor_tensor(scal[:, 15:16], scal[:, 15:16], scal[:, 15:16], ALU.mult)
            e.drain()
            e.tensor_scalar(scal[:, 12:13], scal[:, 15:16], 1.0 / 448.0, 1e-6,
                            ALU.mult, ALU.max)
            e.tensor_tensor(scal[:, 15:16], scal[:, 13:14], scal[:, 13:14], ALU.mult)
            e.drain()
            e.reciprocal(scal[:, 3:4], scal[:, 12:13])
            e.drain()
            e.tensor_tensor(scal[:, 3:4], scal[:, 3:4], scal[:, 15:16], ALU.mult)
            e.drain()
            e.tensor_scalar(scal[:, 3:4], scal[:, 3:4], 0.5, None, ALU.mult)
            e.tensor_tensor(scal[:, 4:5], scal[:, 12:13], scal[:, 6:7], ALU.mult)
            e.drain()
            r = e.tensor_scalar(scal[:, 4:5], scal[:, 4:5], 4.0, None, ALU.mult)
            e.drain()
            return r
        bch_ve = opn("vector", hscal, wait=[("LG", ldh)], inc="VE")
        if "rg" in debug:
            rg_out = nc.dram_tensor("dbg_rg", [4 * 128, T], BF16,
                                    kind="ExternalOutput").ap()
            for uu, uc in enumerate((0, 1, 31, 63)):
                opn("sync", lambda e, uu=uu, uc=uc: e.dma_start(
                    out=rg_out[uu * 128:(uu + 1) * 128, :], in_=gdram[uc, :, :]),
                    wait=[("ST", fc_done_st)], inc="ST", incn=16)
        dbg("scal", lambda: scal[:, :], [128, 16], F32)

    if upto >= 7:
        prev_pair_mm = None
        prev2_pair_mm = None
        prev_gq_last = None
        prev_out_st = [None, None]
        for pr in range(4):
            gcur = g8ab[pr % 2]
            gq_ve = []
            for hfl in range(2):
                lds = []
                for blk in range(4):
                    w = []
                    if pr == 0 and hfl == 0 and blk == 0:
                        w = [("ST", fc_done_st), ("VE", bch_ve), ("MM", fc_done_mm)]
                    elif blk == 0:
                        if hfl == 1:
                            w = [("VE", gq_ve[-1])]
                        elif prev_gq_last is not None:
                            w = [("VE", prev_gq_last)]

                    def ldr(e, pr=pr, hfl=hfl, blk=blk):
                        u0 = hfl * 32 + blk * 8
                        src = gdram[u0:u0 + 8, :, pr * 256:(pr + 1) * 256].rearrange(
                            "u p c -> p u c")
                        dst = rload[:, blk * 8 * 256:(blk + 1) * 8 * 256].rearrange(
                            "p (u c) -> p u c", u=8)
                        return e.dma_start(out=dst, in_=src)
                    lds.append(opn("sync", ldr, wait=w, inc="LD", incn=16))

                def gq(e, hfl=hfl, gcur=gcur):
                    r = None
                    for uo in range(32):
                        uc = hfl * 32 + uo
                        r = e.scalar_tensor_tensor(
                            gcur[:, uc * 256:(uc + 1) * 256],
                            rload[:, uo * 256:(uo + 1) * 256], scal[:, 3:4],
                            rload[:, uo * 256:(uo + 1) * 256], ALU.mult, ALU.mult)
                    return r
                wq = [("LD", lds[-1])]
                if hfl == 0 and prev2_pair_mm is not None:
                    wq.append(("MM", prev2_pair_mm))
                gq_ve.append(opn("vector", gq, wait=wq, inc="VE"))
            prev_gq_last = gq_ve[-1]

            pr_mm = []
            pr_ev = []
            for og in range(4):
                for us in range(8):
                    if len(pr_mm) >= 2:
                        wl = [("MM", pr_mm[-2])]
                    elif prev_pair_mm is not None:
                        wl = [("MM", prev_pair_mm)]
                    else:
                        wl = [("VE", gq_ve[-1])]
                    qeng = "sync" if us % 2 == 0 else "scalar"
                    qsem = "LD" if us % 2 == 0 else "LS"
                    sli = opn(qeng, lambda e, og=og, us=us: e.dma_start(
                        out=w8psl[us % 2][:, :].rearrange("p (u o) -> p u o", u=8),
                        in_=w8p_out.rearrange("(u p) o -> p u o", p=128)[
                            :, us * 8:(us + 1) * 8, og * 512:(og + 1) * 512]),
                        wait=wl, inc=qsem, incn=16)

                    wm = [(qsem, sli), ("VE", gq_ve[-1])]
                    if og >= 2 and us == 0:
                        wm.append(("VE", pr_ev[(og - 2) * 2 + 1]))

                    def mpr(e, og=og, us=us, gcur=gcur):
                        r = None
                        for j in range(8):
                            uc = us * 8 + j
                            for half in range(2):
                                r = e.matmul(pf[(og % 2) * 2 + half][:, :],
                                             gcur[:, uc * 256 + half * 128:
                                                  uc * 256 + half * 128 + 128],
                                             w8psl[us % 2][:, j * 512:(j + 1) * 512],
                                             start=(us == 0 and j == 0),
                                             stop=(us == 7 and j == 7))
                        return r
                    pr_mm.append(opn("tensor", mpr, wait=wm, inc="MM"))
                for half in range(2):
                    tc = pr * 2 + half

                    def pev(e, og=og, half=half, tc=tc):
                        return e.scalar_tensor_tensor(
                            x2t[half][:, og * 512:(og + 1) * 512],
                            pf[(og % 2) * 2 + half][:, :], scal[:, 4:5],
                            x1[:, tc * D + og * 512: tc * D + (og + 1) * 512],
                            ALU.mult, ALU.add)
                    wpe = [("MM", pr_mm[-1])]
                    if og == 0 and prev_out_st[half] is not None:
                        wpe.append(("ST", prev_out_st[half]))
                    pr_ev.append(opn("vector", pev, wait=wpe, inc="VE"))
            prev2_pair_mm = prev_pair_mm
            prev_pair_mm = pr_mm[-1]

            for half in range(2):
                tc = pr * 2 + half

                def ln2(e, half=half):
                    xt = x2t[half][:, :]
                    e.tensor_reduce(bnag[:, 0:1], xt, AX.X, ALU.add)
                    e.drain()
                    e.tensor_scalar(bnag[:, 0:1], bnag[:, 0:1], 1.0 / D, None,
                                    ALU.mult)
                    e.drain()
                    e.tensor_scalar(lnscr2[:, :], xt, bnag[:, 0:1], None,
                                    ALU.subtract)
                    e.tensor_tensor(lnscr2[:, :], lnscr2[:, :], lnscr2[:, :],
                                    ALU.mult)
                    e.tensor_reduce(bnag[:, 1:2], lnscr2[:, :], AX.X, ALU.add)
                    e.drain()
                    r = e.tensor_scalar(rstd[:, :], bnag[:, 1:2], 1.0 / D, EPS,
                                        ALU.mult, ALU.add)
                    e.drain()
                    return r
                lv = opn("vector", ln2, wait=[("VE", pr_ev[-1])], inc="VE")
                sq2 = opn("scalar", lambda e: e.activation(rstd[:, :], rstd[:, :],
                                                           AF.Sqrt),
                          wait=[("VE", lv)], inc="SE")
                rv2 = opn("vector", lambda e: (e.reciprocal(rstd[:, :], rstd[:, :]),
                                               e.drain())[0],
                          wait=[("SE", sq2)], inc="VE")
                nv = opn("vector", lambda e, half=half: e.tensor_scalar(
                    x2t[half][:, :], x2t[half][:, :], bnag[:, 0:1], rstd[:, :],
                    ALU.subtract, ALU.mult), wait=[("VE", rv2)], inc="VE")
                prev_out_st[half] = opn("sync", lambda e, tc=tc, half=half:
                    e.dma_start(out=yout[tc * 128:(tc + 1) * 128, :],
                                in_=x2t[half][:, :]),
                    wait=[("VE", nv)], inc="ST", incn=16)

    P.raw("sync", lambda e, v=P.ctr["VE"]: e.wait_ge(P.sems["VE"], v))
    P.raw("sync", lambda e, v=P.ctr["SE"]: e.wait_ge(P.sems["SE"], v))
    P.raw("sync", lambda e, v=P.ctr["MM"]: e.wait_ge(P.sems["MM"], v))
    P.raw("sync", lambda e, v=P.ctr["GE"]: e.wait_ge(P.sems["GE"], v))

    def fin(e):
        for (dap, apget) in dbg_list:
            e.dma_start(out=dap[:, :], in_=apget()).then_inc(P.sems["ST"], 16)
            P.ctr["ST"] += 16
        if upto < 7:
            e.dma_start(out=yout[0:128, :], in_=x1[:, 0:D]).then_inc(P.sems["ST"], 16)
            P.ctr["ST"] += 16
        e.wait_ge(P.sems["ST"], P.ctr["ST"])
    P.raw("sync", fin)

    with nc.Block() as block:
        @block.sync
        def _(e):
            for f in P.streams["sync"]:
                f(e)

        @block.gpsimd
        def _(e):
            for f in P.streams["gpsimd"]:
                f(e)

        @block.vector
        def _(e):
            for f in P.streams["vector"]:
                f(e)

        @block.scalar
        def _(e):
            for f in P.streams["scalar"]:
                f(e)

        @block.tensor
        def _(e):
            for f in P.streams["tensor"]:
                f(e)

    es.close()
    return nc


def make_consts(core_id):
    import ml_dtypes
    i = np.arange(128, dtype=np.float32)
    return {
        "identb": np.eye(128).astype(ml_dtypes.bfloat16),
        "ident8": np.eye(128).astype(ml_dtypes.float8_e4m3fn),
        "bciota": np.ascontiguousarray(np.tile(i[None, :], (128, 1))),
        "difft": np.ascontiguousarray(i[None, :] - i[:, None]),
        "iotac": np.ascontiguousarray((i - 128.0)[:, None]),
        "dvecbc": np.ascontiguousarray(np.tile(
            np.where(np.arange(8) < core_id,
                     1024.0 * (core_id - 1 - np.arange(8)),
                     1e9).astype(np.float32)[None, :], (128, 1))),
    }


def kernel(L0_attn_x, L0_attn_Wqkv, L0_attn_bqkv, L0_attn_Wout, L0_attn_bout,
           L0_attn_slopes, L0_ffn_w_fc, L0_ffn_w_proj):
    x = np.ascontiguousarray(np.asarray(L0_attn_x, dtype=np.float32).reshape(8192, 2048))
    wqkv = np.ascontiguousarray(np.asarray(L0_attn_Wqkv, dtype=np.float32))
    bq = np.asarray(L0_attn_bqkv, dtype=np.float32)
    wout_a = np.ascontiguousarray(np.asarray(L0_attn_Wout, dtype=np.float32))
    bo = np.asarray(L0_attn_bout, dtype=np.float32)
    wfc = np.ascontiguousarray(np.asarray(L0_ffn_w_fc, dtype=np.float32))
    wpr = np.ascontiguousarray(np.asarray(L0_ffn_w_proj, dtype=np.float32))
    slopes_t = np.ascontiguousarray(
        np.tile(np.asarray(L0_attn_slopes, dtype=np.float32)[None, :], (128, 1)))
    bq_cols = np.ascontiguousarray(bq[:4096].reshape(32, 128).T)
    nc = build_program(upto=7)
    in_maps = []
    for c in range(NCORES):
        m = dict(
            xin=np.ascontiguousarray(x[c * T:(c + 1) * T]),
            wqkv=wqkv,
            bq_cols=bq_cols,
            bq_vrow=np.ascontiguousarray(bq[4096:].reshape(1, 2048)),
            wout=wout_a,
            bo_row=np.ascontiguousarray(bo.reshape(1, 2048)),
            slopes=slopes_t,
            wfc_sh=np.ascontiguousarray(wfc[c * (FF // 8):(c + 1) * (FF // 8)]),
            wpr_sh=np.ascontiguousarray(wpr[:, c * (FF // 8):(c + 1) * (FF // 8)]),
        )
        m.update(make_consts(c))
        in_maps.append(m)
    res = run_bass_kernel_spmd(nc, in_maps, core_ids=list(range(NCORES)))
    out = np.concatenate([r["yout"] for r in res.results], axis=0)
    return np.ascontiguousarray(out.reshape(1, 8192, 2048))


# version 07
# speedup vs baseline: 1.0760x; 1.0105x over previous
import contextlib
import numpy as np
import concourse.bass as bass
import concourse.mybir as mybir
from concourse.bass_utils import run_bass_kernel_spmd

F32 = mybir.dt.float32
F32R = mybir.dt.float32r
BF16 = mybir.dt.bfloat16
FP8 = mybir.dt.float8e4
AF = mybir.ActivationFunctionType
ALU = mybir.AluOpType
AX = mybir.AxisListType

NCORES = 8
T = 1024
D = 2048
H = 16
CB = 128
NCH = T // CB
FF = 8192
KC = D // 128
TC = T // 128
EPS = 1e-5

SB_BASE = 16896
SB_TAB = SB_BASE + 0
SB_SBF = SB_BASE + 16384
SB_SCR = SB_BASE + 20480
SB_STRA = SB_BASE + 28672
SB_STRB = SB_BASE + 36864
SB_BIGA = SB_BASE + 69632
SB_BIGB = SB_BASE + 102400
SB_BIGC = SB_BASE + 135168
SB_BIGD = SB_BASE + 167936


class Prog:
    ENGS = ("sync", "scalar", "vector", "tensor", "gpsimd")

    def __init__(self, nc, es):
        self.nc = nc
        self.streams = {k: [] for k in self.ENGS}
        self.sems = {}
        self.ctr = {}
        for n in ("LD", "LG", "LS", "MM", "VE", "SE", "GE", "CC", "ST"):
            self.sems[n] = es.enter_context(nc.semaphore(n))
            self.ctr[n] = 0

    def op(self, eng, fn, wait=(), inc=None, incn=1):
        post = None
        if inc is not None:
            self.ctr[inc] += incn
            post = self.ctr[inc]
        sems = self.sems

        def run(e, fn=fn, wait=tuple(wait), inc=inc, incn=incn):
            for (n, v) in wait:
                if v > 0:
                    e.wait_ge(sems[n], v)
            r = fn(e)
            if inc is not None:
                if r is None:
                    r = e.nop()
                r.then_inc(sems[inc], incn)
        self.streams[eng].append(run)
        return post

    def raw(self, eng, fn):
        self.streams[eng].append(fn)


def build_program(upto=10, debug=(), subph=99):
    import os as _os
    V4 = int(_os.environ.get("V4", "9"))
    nc = bass.Bass("TRN2", target_bir_lowering=False)

    xin = nc.dram_tensor("xin", [T, D], F32, kind="ExternalInput").ap()
    wqkv = nc.dram_tensor("wqkv", [D, 3 * D], F32, kind="ExternalInput").ap()
    bq_cols = nc.dram_tensor("bq_cols", [128, 32], F32, kind="ExternalInput").ap()
    bq_vrow = nc.dram_tensor("bq_vrow", [1, D], F32, kind="ExternalInput").ap()
    wout = nc.dram_tensor("wout", [D, D], F32, kind="ExternalInput").ap()
    bo_row = nc.dram_tensor("bo_row", [1, D], F32, kind="ExternalInput").ap()
    slopes = nc.dram_tensor("slopes", [128, H], F32, kind="ExternalInput").ap()
    wfc_sh = nc.dram_tensor("wfc_sh", [FF // 8, D], F32, kind="ExternalInput").ap()
    wpr_sh = nc.dram_tensor("wpr_sh", [D, FF // 8], F32, kind="ExternalInput").ap()
    identb = nc.dram_tensor("identb", [128, 128], BF16, kind="ExternalInput").ap()
    ident8 = nc.dram_tensor("ident8", [128, 128], FP8, kind="ExternalInput").ap()
    bciota = nc.dram_tensor("bciota", [128, 128], F32, kind="ExternalInput").ap()
    difft = nc.dram_tensor("difft", [128, 128], F32, kind="ExternalInput").ap()
    iotac = nc.dram_tensor("iotac", [128, 1], F32, kind="ExternalInput").ap()
    dvecbc = nc.dram_tensor("dvecbc", [128, 8], F32, kind="ExternalInput").ap()

    yout = nc.dram_tensor("yout", [T, D], F32, kind="ExternalOutput").ap()

    agt_in = nc.dram_tensor("agt_in", [128, H * 128], BF16).ap()
    agt_out = nc.dram_tensor("agt_out", [NCORES * 128, H * 128], BF16).ap()
    arw_in = nc.dram_tensor("arw_in", [1, 128], F32).ap()
    arw_out = nc.dram_tensor("arw_out", [1, 128], F32).ap()
    arx_in = nc.dram_tensor("arx_in", [1, 128], F32).ap()
    arx_out = nc.dram_tensor("arx_out", [1, 128], F32).ap()
    arh_in = nc.dram_tensor("arh_in", [1, 128], F32).ap()
    arh_out = nc.dram_tensor("arh_out", [1, 128], F32).ap()
    w8f_in = nc.dram_tensor("w8f_in", [D, FF // 8], FP8).ap()
    w8f_out = nc.dram_tensor("w8f_out", [NCORES * D, FF // 8], FP8).ap()
    w8p_in = nc.dram_tensor("w8p_in", [FF // 8, D], FP8).ap()
    w8p_out = nc.dram_tensor("w8p_out", [NCORES * (FF // 8), D], FP8).ap()
    gdram = nc.dram_tensor("gdram", [FF // 128, 128, T], BF16).ap()

    def sb(name, shape, dt, off):
        return nc.alloc_sbuf_tensor_at(name, shape, dt, offset=off).ap()

    dq_t = sb("dq_t", [128, H * 128], BF16, SB_TAB + 0)
    din_t = sb("din_t", [128, H * 128], BF16, SB_TAB + 4096)
    bq_vr = sb("bq_vr", [1, D], BF16, SB_TAB + 8192)
    bo_r = sb("bo_r", [1, D], BF16, SB_TAB + 12288)
    kd_c = sb("kd_c", [128, H], F32, SB_SCR + 0)
    bd_c = sb("bd_c", [128, H], F32, SB_SCR + 64)
    sl_c = sb("sl_c", [128, H], F32, SB_SCR + 128)
    bqc = sb("bqc", [128, 32], F32, SB_SCR + 192)
    scal = sb("scal", [128, 16], F32, SB_SCR + 320)
    s11 = sb("s11", [1, 32], F32, SB_SCR + 384)
    bcw = sb("bcw", [128, 128], F32, SB_SCR + 512)
    idb = sb("idb", [128, 128], BF16, SB_SCR + 1024)
    id8 = sb("id8", [128, 128], FP8, SB_SCR + 1280)
    onesb = sb("onesb", [1, 512], BF16, SB_SCR + 1408)
    onesr_f = sb("onesr", [1, 128], F32, SB_SCR + 2432)
    onesr = sb("onesr_r", [1, 128], F32R, SB_SCR + 5760)
    s11r = sb("s11r", [1, 8], F32R, SB_SCR + 6272)
    dmy = sb("dmy", [128, 1], F32, SB_SCR + 6304)
    bcio = sb("bcio", [128, 128], F32, SB_SCR + 2944)
    difft_s = sb("difft_s", [128, 128], F32, SB_SCR + 3456)
    iotac_s = sb("iotac_s", [128, 1], F32, SB_SCR + 3968)
    dvec_s = sb("dvec_s", [128, 8], F32, SB_SCR + 4000)
    mcol = sb("mcol", [128, 1], F32, SB_SCR + 4032)
    bnst = sb("bnst", [128, 24], F32, SB_SCR + 4064)
    bnag = sb("bnag", [128, 2], F32, SB_SCR + 4160)
    rstd = sb("rstd", [128, 1], F32, SB_SCR + 4192)
    qd_s = sb("qd_s", [128, 2 * 128], BF16, SB_SCR + 4224)
    st_s = sb("st_s", [128, 2 * 128], BF16, SB_SCR + 4736)
    tabtmp = sb("tabtmp", [128, 128], F32, SB_SCR + 5248)
    qtm8 = sb("qtm8", [128, D], FP8, SB_STRB + 24576)

    sbf = sb("sbf", [128, H * 128], BF16, SB_SBF)

    wqslab = [sb(f"wqslab{i}", [128, KC * 128], BF16, SB_STRA + i * 4096) for i in range(2)]
    w8psl = [sb(f"w8psl{i}", [128, 8 * 512], FP8, SB_STRA + i * 4096) for i in range(2)]
    xres = [sb(f"xres{i}", [128, 512], F32, SB_STRA + i * 2048) for i in range(2)]

    xtm = sb("xtm", [128, TC * D], BF16, SB_STRB)
    vslab = [sb(f"vslab{i}", [128, KC * 512], BF16, SB_STRB + i * 16384) for i in range(2)]
    wsh_f32 = sb("wsh_f32", [128, D], F32, SB_STRB + 0)
    wsh_f32b = sb("wsh_f32b", [128, D], F32, SB_STRB + 8192)
    wsh_f8 = sb("wsh_f8", [128, D], FP8, SB_STRB + 8192)
    wsh_stg = sb("wsh_stg", [128, 16 * 1024], FP8, SB_STRB + 10752)
    tjin = sb("tjin", [128, H * 128], BF16, SB_STRB + 27136)
    sini = sb("sini", [128, H * 128], F32, SB_STRB + 0)
    attnt = sb("attnt", [128, H * 1024], BF16, SB_STRB)
    w8slab = [sb(f"w8slab{i}", [128, KC * 512], FP8, SB_STRB + i * 8192) for i in range(2)]
    rstg = [sb(f"rstg{i}", [128, 512], BF16, SB_STRB + 16384 + i * 1024) for i in range(2)]
    lnscr2 = sb("lnscr2", [128, D], F32, SB_BIGA + 0)
    g8ab = [sb("g8a", [128, 64 * 256], FP8, SB_STRB + 16384),
            sb("g8b", [128, 64 * 256], FP8, SB_STRB + 0)]

    qfm = sb("qfm", [128, H * 1024], BF16, SB_BIGA)
    woslab = [sb(f"woslab{i}", [128, KC * 512], BF16, SB_BIGA + i * 16384) for i in range(2)]
    x8fm = sb("x8fm", [128, KC * T], FP8, SB_BIGA)
    rload = sb("rload", [128, 32 * 256], BF16, SB_BIGA + 16384)
    kfm = sb("kfm", [128, H * 1024], BF16, SB_BIGB)
    x2t = [sb(f"x2t{i}", [128, D], F32, SB_BIGB + 16384 + i * 8192) for i in range(2)]
    vtm = sb("vtm", [128, H * 1024], BF16, SB_BIGC)
    xfm = sb("xfm", [128, KC * T], BF16, SB_BIGD)
    kdt = sb("kdt", [128, H * 1024], BF16, SB_BIGD)
    x1 = sb("x1", [128, TC * D], F32, SB_BIGC)

    pf = [nc.alloc_psum_tensor(f"pf{i}", [128, 512], F32).ap() for i in range(6)]
    ptr = nc.alloc_psum_tensor("ptr", [128, 1024], BF16).ap()
    paux = nc.alloc_psum_tensor("paux", [128, 512], F32).ap()
    ptr8 = ptr.bitcast(FP8)
    ptr2 = paux.bitcast(BF16)
    ptr28 = paux.bitcast(FP8)

    es = contextlib.ExitStack()
    P = Prog(nc, es)
    opn = P.op
    RG = [list(range(NCORES))]

    dbg_list = []

    def dbg(name, apget, shape, dt):
        if name in debug:
            dap = nc.dram_tensor("dbg_" + name, shape, dt, kind="ExternalOutput").ap()
            dbg_list.append((dap, apget))

    for dst, src in [(idb, identb), (id8, ident8), (bcio, bciota),
                     (difft_s, difft), (iotac_s, iotac), (dvec_s, dvecbc),
                     (sl_c, slopes), (bqc, bq_cols)]:
        opn("sync", lambda e, d=dst, s=src: e.dma_start(out=d[:, :], in_=s[:, :]),
            inc="LD", incn=16)
    consts_ld = P.ctr["LD"]

    opn("gpsimd", lambda e: e.dma_start(out=bq_vr[:, :], in_=bq_vrow[:, :]),
        inc="LG", incn=16)
    opn("gpsimd", lambda e: e.dma_start(out=bo_r[:, :], in_=bo_row[:, :]),
        inc="LG", incn=16)
    for tc in range(TC):
        opn("gpsimd", lambda e, tc=tc: e.dma_start(
            out=xtm[:, tc * D:(tc + 1) * D], in_=xin[tc * 128:(tc + 1) * 128, :]),
            inc="LG", incn=16)
    x_ld = P.ctr["LG"]

    first = True
    for h in range(H):
        w = [("LD", consts_ld)] if first else [("SE", P.ctr["SE"])]
        ve = opn("vector", lambda e, h=h: e.tensor_scalar(
            tabtmp[:, :], bcio[:, :], sl_c[:, h:h + 1], -1.0, ALU.mult, ALU.mult),
            wait=w, inc="VE")
        opn("scalar", lambda e, h=h: e.activation(
            dq_t[:, h * 128:(h + 1) * 128], tabtmp[:, :], AF.Exp),
            wait=[("VE", ve)], inc="SE")
        ve = opn("vector", lambda e, h=h: e.tensor_scalar(
            tabtmp[:, :], difft_s[:, :], sl_c[:, h:h + 1], -1.0, ALU.mult, ALU.mult),
            wait=[("SE", P.ctr["SE"])], inc="VE")
        ge = opn("gpsimd", lambda e: e.affine_select(
            tabtmp[:, :], tabtmp[:, :], pattern=[[1, 128]], compare_op=ALU.is_ge,
            fill=-1e5, base=0, channel_multiplier=-1),
            wait=[("VE", ve)], inc="GE")
        opn("scalar", lambda e, h=h: e.activation(
            din_t[:, h * 128:(h + 1) * 128], tabtmp[:, :], AF.Exp),
            wait=[("GE", ge)], inc="SE")
        first = False

    def cols_v(e):
        e.memset(onesb[:, :], 1.0)
        e.memset(onesr_f[:, :], 1.0)
        for h in range(H):
            e.tensor_scalar(kd_c[:, h:h + 1], iotac_s[:, :], sl_c[:, h:h + 1],
                            None, ALU.mult)
        e.tensor_scalar(bd_c[:, :], sl_c[:, :], -128.0, None, ALU.mult)
        r = None
        for h in range(H):
            r = e.tensor_scalar(bcw[:, h * 8:(h + 1) * 8], dvec_s[:, :],
                                sl_c[:, h:h + 1], -1.0, ALU.mult, ALU.mult)
        return r
    ve = opn("vector", cols_v, wait=[("SE", P.ctr["SE"])], inc="VE")

    def cols_s(e):
        e.activation(kd_c[:, :], kd_c[:, :], AF.Exp)
        e.activation(bd_c[:, :], bd_c[:, :], AF.Exp)
        return e.activation(bcw[:, :], bcw[:, :], AF.Exp)
    tabs_se = opn("scalar", cols_s, wait=[("VE", ve)], inc="SE")

    xtr_ve = []
    n = 0
    for tc in range(TC):
        for k in range(KC):
            half = n % 2
            w = [("LG", x_ld), ("LD", consts_ld)] if n == 0 else []
            if n >= 2:
                w = [("VE", xtr_ve[n - 2])]
            mmv = opn("tensor", lambda e, tc=tc, k=k, half=half: e.transpose(
                (ptr if half == 0 else ptr2)[:, 0:128],
                xtm[:, tc * D + k * 128: tc * D + (k + 1) * 128], idb[:, :]),
                wait=w, inc="MM")
            if n % 8 == 7:
                P.raw("tensor", lambda e: e.drain())
            xtr_ve.append(opn("vector", lambda e, tc=tc, k=k, half=half: e.tensor_copy(
                xfm[:, k * T + tc * 128: k * T + (tc + 1) * 128],
                (ptr if half == 0 else ptr2)[:, 0:128]),
                wait=[("MM", mmv)], inc="VE"))
            n += 1
    xfm_ve = P.ctr["VE"]
    dbg("xfm", lambda: xfm[:, :], [128, KC * T], BF16)

    if upto >= 1:
        qk_mm = []
        qk_ve = []
        qk_ld = []

        def reg_qk_ld(i):
            w = []
            if i >= 2:
                w = [("MM", qk_mm[(i - 2) * 2 + 1])]
            qk_ld.append(opn("gpsimd", lambda e, i=i: e.dma_start(
                out=wqslab[i % 2][:, :].rearrange("p (kc fc) -> p kc fc", kc=KC),
                in_=wqkv.rearrange("(kc p) f -> p kc f", p=128)[:, :, i * 128:(i + 1) * 128]),
                wait=w, inc="LG", incn=16))

        reg_qk_ld(0)
        reg_qk_ld(1)
        gidx = 0
        for i in range(32):
            for th in range(2):
                bank = gidx % 4
                w = [("LG", qk_ld[i])]
                if gidx == 0:
                    w.append(("VE", xfm_ve))
                if gidx >= 4:
                    w.append(("VE", qk_ve[gidx - 4]))

                def mmf(e, i=i, th=th, bank=bank):
                    r = None
                    for k in range(KC):
                        r = e.matmul(pf[bank][:, :],
                                     wqslab[i % 2][:, k * 128:(k + 1) * 128],
                                     xfm[:, k * T + th * 512: k * T + th * 512 + 512],
                                     start=(k == 0), stop=(k == KC - 1))
                    return r
                qk_mm.append(opn("tensor", mmf, wait=w, inc="MM"))

                def evf(e, i=i, th=th, bank=bank):
                    dst = qfm if i < 16 else kfm
                    h = i % 16
                    return e.tensor_scalar(
                        dst[:, h * T + th * 512: h * T + th * 512 + 512],
                        pf[bank][:, :], bqc[:, i:i + 1], None, ALU.add)
                qk_ve.append(opn("vector", evf, wait=[("MM", qk_mm[-1])], inc="VE"))
                gidx += 1
            if i + 2 < 32:
                reg_qk_ld(i + 2)

        v_ld = []
        v_mm = []
        v_ve = []

        def reg_v_ld(fg):
            w = []
            if fg == 0:
                w = [("VE", xfm_ve)]
            if fg >= 2:
                w = [("MM", v_mm[(fg - 2) * TC + TC - 1])]
            v_ld.append(opn("gpsimd", lambda e, fg=fg: e.dma_start(
                out=vslab[fg % 2][:, :].rearrange("p (kc fc) -> p kc fc", kc=KC),
                in_=wqkv.rearrange("(kc p) f -> p kc f", p=128)[
                    :, :, 2 * D + fg * 512: 2 * D + (fg + 1) * 512]),
                wait=w, inc="LG", incn=16))

        reg_v_ld(0)
        reg_v_ld(1)
        gidx = 0
        for fg in range(4):
            for tc in range(TC):
                bank = gidx % 4
                w = [("LG", v_ld[fg]), ("SE", tabs_se)]
                if gidx == 0:
                    w.append(("VE", qk_ve[-1]))
                if gidx >= 4:
                    w.append(("VE", v_ve[gidx - 4]))

                def mmv(e, fg=fg, tc=tc, bank=bank):
                    for k in range(KC):
                        e.matmul(pf[bank][:, :],
                                 xfm[:, k * T + tc * 128:k * T + (tc + 1) * 128],
                                 vslab[fg % 2][:, k * 512:(k + 1) * 512],
                                 start=(k == 0), stop=False)
                    return e.matmul(pf[bank][:, :], onesb[0:1, 0:128],
                                    bq_vr[0:1, fg * 512:(fg + 1) * 512],
                                    start=False, stop=True)
                v_mm.append(opn("tensor", mmv, wait=w, inc="MM"))

                def evv(e, fg=fg, tc=tc, bank=bank):
                    src = pf[bank][:, :].rearrange("p (j e) -> p j e", j=4)
                    dst = vtm[:, :].rearrange("p (h c e) -> p h c e", h=H, c=NCH)[
                        :, fg * 4:fg * 4 + 4, tc, :]
                    return e.tensor_copy(dst, src)
                v_ve.append(opn("vector", evv, wait=[("MM", v_mm[-1])], inc="VE"))
                gidx += 1
            if fg + 2 < 4:
                reg_v_ld(fg + 2)

        ktr_ve = []
        n = 0
        for h in range(H):
            for c in range(NCH):
                half = n % 2
                w = []
                if n >= 2:
                    w = [("VE", ktr_ve[n - 2])]
                mmv = opn("tensor", lambda e, h=h, c=c, half=half: e.transpose(
                    (ptr if half == 0 else ptr2)[:, 0:128],
                    kfm[:, h * T + c * 128: h * T + (c + 1) * 128], idb[:, :]),
                    wait=w, inc="MM")
                if n % 8 == 7:
                    P.raw("tensor", lambda e: e.drain())
                ktr_ve.append(opn("vector", lambda e, h=h, c=c, half=half:
                                  e.tensor_scalar(
                                      kdt[:, (h * NCH + c) * 128:(h * NCH + c + 1) * 128],
                                      (ptr if half == 0 else ptr2)[:, 0:128],
                                      kd_c[:, h:h + 1], None, ALU.mult),
                                  wait=[("MM", mmv)], inc="VE"))
                n += 1
        kdt_ve = P.ctr["VE"]
        last_v_mm = v_mm[-1]
        dbg("qfm", lambda: qfm[:, :], [128, H * 1024], BF16)
        dbg("kfm", lambda: kfm[:, :], [128, H * 1024], BF16)
        dbg("vtm", lambda: vtm[:, :], [128, H * 1024], BF16)
        dbg("kdt", lambda: kdt[:, :], [128, H * 1024], BF16)

    if upto >= 2:
        sz_ve = opn("vector", lambda e: e.memset(sbf[:, :], 0.0), inc="VE")
        p1_mm = []
        p1_ve = []
        nidx = 0
        p1_order = [(hp + dh, c) for hp in range(0, H, 2) for c in range(NCH)
                    for dh in (0, 1)]
        for (h, c) in p1_order:
            if True:
                bank = 4 + (nidx % 2)
                w = [("VE", kdt_ve)] if nidx == 0 else []
                if nidx >= 2:
                    w.append(("VE", p1_ve[nidx - 2]))

                def smm(e, h=h, c=c, bank=bank):
                    i = h * NCH + c
                    return e.matmul(pf[bank][:, 0:128],
                                    kdt[:, i * 128:(i + 1) * 128],
                                    vtm[:, i * 128:(i + 1) * 128],
                                    start=True, stop=True)
                p1_mm.append(opn("tensor", smm, wait=w, inc="MM"))
                if nidx % 4 == 3:
                    P.raw("tensor", lambda e: e.drain())

                def sstt(e, h=h, bank=bank):
                    return e.scalar_tensor_tensor(
                        sbf[:, h * 128:(h + 1) * 128], sbf[:, h * 128:(h + 1) * 128],
                        bd_c[:, h:h + 1], pf[bank][:, 0:128], ALU.mult, ALU.add)
                p1_ve.append(opn("vector", sstt,
                                 wait=[("MM", p1_mm[-1]), ("VE", sz_ve)], inc="VE"))
                nidx += 1
        p1_done_ve = P.ctr["VE"]

        if subph < 2:
            upto = 0
        texp = None
        if subph >= 2:
            texp = opn("sync", lambda e: e.dma_start(out=agt_in[:, :], in_=sbf[:, :]),
                       wait=[("VE", p1_done_ve)], inc="LD", incn=16)
        agT = None
        if subph >= 2:
            agT = opn("gpsimd", lambda e: e.collective_compute(
                "AllGather", ALU.bypass, replica_groups=RG,
                ins=[agt_in.opt()], outs=[agt_out.opt()]),
                wait=[("LD", texp)], inc="CC")

        if subph < 3:
            upto = 0
        wsh_ld = []
        if subph >= 3:
            wsh_ve = []
            for i in range(24):
                if i < 2:
                    w = [("MM", last_v_mm)]
                else:
                    w = [("VE", wsh_ve[i - 2])]

                def lsh(e, i=i, buf=(wsh_f32, wsh_f32b)[i % 2]):
                    if i < 8:
                        return e.dma_start(out=buf[:, :],
                                           in_=wfc_sh[i * 128:(i + 1) * 128, :])
                    j = i - 8
                    return e.dma_start(out=buf[:, 0:1024],
                                       in_=wpr_sh[j * 128:(j + 1) * 128, :])
                wsh_ld.append(opn("scalar", lsh, wait=w, inc="LS", incn=16))

                def msh(e, i=i, buf=(wsh_f32, wsh_f32b)[i % 2]):
                    width = D if i < 8 else 1024
                    col = 8 if i < 8 else 9
                    e.tensor_reduce(mcol[:, :], buf[:, 0:width], AX.X, ALU.max,
                                    apply_absolute_value=True)
                    e.drain()
                    if i == 0 or i == 8:
                        e.tensor_copy(scal[:, col:col + 1], mcol[:, :])
                    else:
                        r = e.tensor_tensor(scal[:, col:col + 1],
                                            scal[:, col:col + 1], mcol[:, :],
                                            ALU.max)
                    e.drain()
                    return e.tensor_copy(dmy[:, :], bnag[:, 0:1])
                wsh_ve.append(opn("vector", msh, wait=[("LS", wsh_ld[-1])], inc="VE"))

            def gmax_f(e):
                e.tensor_reduce(scal[0:1, 8:9], scal[:, 8:9], AX.C, ALU.max)
                return e.tensor_reduce(scal[0:1, 9:10], scal[:, 9:10], AX.C, ALU.max)
            gmax = opn("gpsimd", gmax_f, wait=[("VE", wsh_ve[-1])], inc="GE")

            def stgw_f(e):
                e.dma_start(out=arw_in[0:1, 0:1], in_=scal[0:1, 8:9]).then_inc(
                    P.sems["LG"], 16)
                return e.dma_start(out=arw_in[0:1, 1:2], in_=scal[0:1, 9:10])
            opn("gpsimd", stgw_f, wait=[("GE", gmax)], inc="LG", incn=16)
            P.ctr["LG"] += 16
            stgw = P.ctr["LG"]
            arw = opn("gpsimd", lambda e: e.collective_compute(
                "AllReduce", ALU.max, replica_groups=RG,
                ins=[arw_in.opt()], outs=[arw_out.opt()]),
                wait=[("LG", stgw)], inc="CC")

            def ldw_f(e):
                e.dma_start(out=scal[:, 14:15],
                            in_=arw_out[0:1, 0:1].broadcast_to((128, 1))).then_inc(
                    P.sems["LG"], 16)
                return e.dma_start(out=scal[:, 15:16],
                                   in_=arw_out[0:1, 1:2].broadcast_to((128, 1)))
            opn("gpsimd", ldw_f, wait=[("CC", arw)], inc="LG", incn=16)
            P.ctr["LG"] += 16
            ldw = P.ctr["LG"]

            def wscal(e):
                e.tensor_scalar(scal[:, 5:7], scal[:, 14:16], 1.0 / 448.0, 1e-6,
                                ALU.mult, ALU.max)
                e.drain()
                e.reciprocal(scal[:, 0:2], scal[:, 5:7])
                e.drain()
                e.tensor_scalar(scal[:, 0:2], scal[:, 0:2], 0.5, None, ALU.mult)
                e.drain()
                return e.tensor_copy(dmy[:, :], bnag[:, 0:1])
            bc_ve = opn("vector", wscal, wait=[("LG", ldw)], inc="VE")

            if subph < 6:
                upto = 0
            if subph >= 4:
                for i in range(8):
                    w = [("VE", bc_ve)] if i == 0 else [("VE", P.ctr["VE"])]
                    ldq = opn("scalar", lambda e, i=i: e.dma_start(
                        out=wsh_f32[:, :], in_=wfc_sh[i * 128:(i + 1) * 128, :]),
                        wait=w, inc="LS", incn=16)
                    qv = opn("vector", lambda e: e.tensor_scalar(
                        wsh_f8[:, :], wsh_f32[:, :], scal[:, 0:1], None, ALU.mult),
                        wait=[("LS", ldq)], inc="VE")
                    prev_tr = []
                    for k in range(KC):
                        half = k % 2
                        w2 = [("VE", qv)]
                        if k >= 2:
                            w2.append(("VE", prev_tr[k - 2]))
                        tmm = opn("tensor", lambda e, k=k, half=half: e.transpose(
                            (ptr8 if half == 0 else ptr28)[:, 0:256:2],
                            wsh_f8[:, k * 128:(k + 1) * 128], id8[:, :]),
                            wait=w2, inc="MM")
                        if k % 8 == 7:
                            P.raw("tensor", lambda e: e.drain())
                        prev_tr.append(opn("vector", lambda e, k=k, half=half, i=i:
                                           e.tensor_copy(
                                               wsh_stg[:, k * 1024 + i * 128:
                                                       k * 1024 + (i + 1) * 128],
                                               (ptr8 if half == 0 else ptr28)[:, 0:256:2]),
                                           wait=[("MM", tmm)], inc="VE"))
                stg_f = opn("sync", lambda e: e.dma_start(
                    out=w8f_in.rearrange("(kc p) u -> p kc u", p=128),
                    in_=wsh_stg[:, :].rearrange("p (kc u) -> p kc u", kc=KC)),
                    wait=[("VE", P.ctr["VE"])], inc="LD", incn=16)
                agF = opn("gpsimd", lambda e: e.collective_compute(
                    "AllGather", ALU.bypass, replica_groups=RG,
                    ins=[w8f_in.opt()], outs=[w8f_out.opt()]),
                    wait=[("LD", stg_f), ("CC", agT)], inc="CC")

            if subph >= 5:
                for i in range(16):
                    w = [("LD", stg_f)] if i == 0 else [("VE", P.ctr["VE"])]
                    ldq = opn("scalar", lambda e, i=i: e.dma_start(
                        out=wsh_f32[:, 0:1024], in_=wpr_sh[i * 128:(i + 1) * 128, :]),
                        wait=w, inc="LS", incn=16)
                    qv = opn("vector", lambda e: e.tensor_scalar(
                        wsh_f8[:, 0:1024], wsh_f32[:, 0:1024], scal[:, 1:2], None, ALU.mult),
                        wait=[("LS", ldq)], inc="VE")
                    prev_tr = []
                    for k in range(8):
                        half = k % 2
                        w2 = [("VE", qv)]
                        if k >= 2:
                            w2.append(("VE", prev_tr[k - 2]))
                        tmm = opn("tensor", lambda e, k=k, half=half: e.transpose(
                            (ptr8 if half == 0 else ptr28)[:, 0:256:2],
                            wsh_f8[:, k * 128:(k + 1) * 128], id8[:, :]),
                            wait=w2, inc="MM")
                        prev_tr.append(opn("vector", lambda e, k=k, half=half, i=i:
                                           e.tensor_copy(
                                               wsh_stg[:, k * 2048 + i * 128:
                                                       k * 2048 + (i + 1) * 128],
                                               (ptr8 if half == 0 else ptr28)[:, 0:256:2]),
                                           wait=[("MM", tmm)], inc="VE"))
                    P.raw("tensor", lambda e: e.drain())
                stg_p = opn("sync", lambda e: e.dma_start(
                    out=w8p_in.rearrange("(kc p) o -> p kc o", p=128),
                    in_=wsh_stg[:, :].rearrange("p (kc o) -> p kc o", kc=8)),
                    wait=[("VE", P.ctr["VE"])], inc="LD", incn=16)
                agP = opn("gpsimd", lambda e: e.collective_compute(
                    "AllGather", ALU.bypass, replica_groups=RG,
                    ins=[w8p_in.opt()], outs=[w8p_out.opt()]),
                    wait=[("LD", stg_p), ("CC", agF)], inc="CC")

            if subph >= 6:
                comb_last = None
                for r in range(NCORES):
                    wl = [("CC", agT)]
                    if r == 0:
                        wl.append(("VE", P.ctr["VE"]))
                    else:
                        wl.append(("VE", comb_last))
                    lt = opn("sync", lambda e, r=r: e.dma_start(
                        out=tjin[:, :], in_=agt_out[r * 128:(r + 1) * 128, :]),
                        wait=wl, inc="LD", incn=16)
                    for h in range(H):
                        def cmb(e, r=r, h=h):
                            if r == 0:
                                return e.tensor_scalar(
                                    sini[:, h * 128:(h + 1) * 128],
                                    tjin[:, h * 128:(h + 1) * 128],
                                    bcw[:, h * 8 + r:h * 8 + r + 1], None, ALU.mult)
                            return e.scalar_tensor_tensor(
                                sini[:, h * 128:(h + 1) * 128],
                                tjin[:, h * 128:(h + 1) * 128],
                                bcw[:, h * 8 + r:h * 8 + r + 1],
                                sini[:, h * 128:(h + 1) * 128], ALU.mult, ALU.add)
                        comb_last = opn("vector", cmb,
                                        wait=[("LD", lt)] if h == 0 else [], inc="VE")
                sinit_ve = opn("vector", lambda e: e.tensor_copy(sbf[:, :], sini[:, :]),
                               inc="VE")
                dbg("sinit", lambda: sbf[:, :], [128, H * 128], BF16)
    dbg("bcw", lambda: bcw[:, :], [128, 128], F32)
    dbg("kdc", lambda: kd_c[:, :], [128, H], F32)

    if upto >= 3:
        p2_sco, p2_stv, p2_qdv, p2_omm, p2_umm, p2_oev, p2_stt = ([] for _ in range(7))
        nidx = 0
        p2_order = [(hp + dh, c) for hp in range(0, H, 2) for c in range(NCH)
                    for dh in (0, 1)]
        for (h, c) in p2_order:
            if True:
                p = nidx % 2
                A, B, Cb = pf[0 + p], pf[2 + p], pf[4 + p]
                i = h * NCH + c
                w = [("VE", sinit_ve)] if nidx == 0 else []
                if nidx >= 2:
                    w.append(("VE", p2_stv[nidx - 2]))
                p2_sco.append(opn("tensor", lambda e, i=i, A=A: e.matmul(
                    A[:, 0:128], kfm[:, i * 128:(i + 1) * 128],
                    qfm[:, i * 128:(i + 1) * 128], start=True, stop=True),
                    wait=w, inc="MM"))

                w2 = [("MM", p2_sco[-1])]
                if nidx >= 2:
                    w2.append(("MM", p2_omm[nidx - 2]))
                p2_stv.append(opn("vector", lambda e, h=h, p=p, A=A:
                                  e.tensor_tensor(st_s[:, p * 128:(p + 1) * 128],
                                                  A[:, 0:128],
                                                  din_t[:, h * 128:(h + 1) * 128],
                                                  ALU.mult), wait=w2, inc="VE"))
                w3 = []
                if nidx >= 2:
                    w3.append(("MM", p2_omm[nidx - 2]))
                p2_qdv.append(opn("vector", lambda e, i=i, h=h, p=p:
                                  e.tensor_tensor(qd_s[:, p * 128:(p + 1) * 128],
                                                  qfm[:, i * 128:(i + 1) * 128],
                                                  dq_t[:, h * 128:(h + 1) * 128],
                                                  ALU.mult), wait=w3, inc="VE"))
                w4 = [("VE", p2_qdv[-1])]
                if nidx >= 2:
                    w4.append(("SE", p2_oev[nidx - 2]))
                    w4.append(("VE", p2_stt[nidx - 2]))

                def mo(e, i=i, h=h, p=p, B=B):
                    e.matmul(B[:, 0:128], vtm[:, i * 128:(i + 1) * 128],
                             st_s[:, p * 128:(p + 1) * 128], start=True, stop=False)
                    return e.matmul(B[:, 0:128], sbf[:, h * 128:(h + 1) * 128],
                                    qd_s[:, p * 128:(p + 1) * 128],
                                    start=False, stop=True)
                p2_omm.append(opn("tensor", mo, wait=w4, inc="MM"))

                w5 = []
                if nidx >= 2:
                    w5.append(("VE", p2_stt[nidx - 2]))
                p2_umm.append(opn("tensor", lambda e, i=i, Cb=Cb: e.matmul(
                    Cb[:, 0:128], kdt[:, i * 128:(i + 1) * 128],
                    vtm[:, i * 128:(i + 1) * 128], start=True, stop=True),
                    wait=w5, inc="MM"))
                if nidx % 2 == 1:
                    P.raw("tensor", lambda e: e.drain())
                p2_oev.append(opn("scalar", lambda e, i=i, B=B: e.activation(
                    attnt[:, i * 128:(i + 1) * 128], B[:, 0:128], AF.Copy),
                    wait=[("MM", p2_omm[-1])], inc="SE"))
                p2_stt.append(opn("vector", lambda e, h=h, Cb=Cb:
                                  e.scalar_tensor_tensor(
                                      sbf[:, h * 128:(h + 1) * 128],
                                      sbf[:, h * 128:(h + 1) * 128],
                                      bd_c[:, h:h + 1], Cb[:, 0:128],
                                      ALU.mult, ALU.add),
                                  wait=[("MM", p2_umm[-1])], inc="VE"))
                nidx += 1
        pass2_se = P.ctr["SE"]
        pass2_ve = P.ctr["VE"]
        dbg("attnt", lambda: attnt[:, :], [128, H * 1024], BF16)

    if upto >= 4:
        op_mm = []
        op_ve = []
        wo_ld = []
        xr_ld = []

        def reg_wo_ld(og):
            w = [("VE", pass2_ve), ("SE", pass2_se)]
            if og >= 2:
                w = [("MM", op_mm[(og - 2) * TC * 2 + TC * 2 - 1])]
            evs = []
            for hh in range(2):
                evs.append(opn("gpsimd", lambda e, og=og, hh=hh: e.dma_start(
                    out=woslab[og % 2][:, hh * 8 * 512:(hh + 1) * 8 * 512].rearrange(
                        "p (kc oc) -> p kc oc", kc=8),
                    in_=wout.rearrange("(kc p) o -> p kc o", p=128)[
                        :, hh * 8:(hh + 1) * 8, og * 512:(og + 1) * 512]),
                    wait=w if hh == 0 else [], inc="LG", incn=16))
            wo_ld.append(evs)

        reg_wo_ld(0)
        reg_wo_ld(1)
        gidx = 0
        ln_last = None
        for og in range(4):
            for tc in range(TC):
                bank = gidx % 4
                wxl = []
                if gidx >= 2:
                    wxl = [("VE", op_ve[gidx - 2])]
                elif og == 0 and tc == 0:
                    wxl = [("SE", pass2_se)]
                xr_ld.append(opn("sync", lambda e, og=og, tc=tc, par=gidx % 2:
                                 e.dma_start(out=xres[par][:, :],
                                             in_=xin[tc * 128:(tc + 1) * 128,
                                                     og * 512:(og + 1) * 512]),
                                 wait=wxl, inc="LD", incn=16))

                w = [("LG", wo_ld[og][0])]
                if gidx == 0:
                    w += [("VE", pass2_ve), ("SE", pass2_se)]
                if gidx >= 4:
                    w.append(("VE", op_ve[gidx - 4]))

                def mop0(e, tc=tc, og=og, bank=bank):
                    r = None
                    for k in range(8):
                        r = e.matmul(pf[bank][:, :],
                                     attnt[:, k * T + tc * 128:k * T + (tc + 1) * 128],
                                     woslab[og % 2][:, k * 512:(k + 1) * 512],
                                     start=(k == 0), stop=False)
                    return r
                op_mm.append(opn("tensor", mop0, wait=w, inc="MM"))

                def mop1(e, tc=tc, og=og, bank=bank):
                    for k in range(8, KC):
                        e.matmul(pf[bank][:, :],
                                 attnt[:, k * T + tc * 128:k * T + (tc + 1) * 128],
                                 woslab[og % 2][:, k * 512:(k + 1) * 512],
                                 start=False, stop=False)
                    return e.matmul(pf[bank][:, :], onesb[0:1, 0:128],
                                    bo_r[0:1, og * 512:(og + 1) * 512],
                                    start=False, stop=True)
                op_mm.append(opn("tensor", mop1,
                                 wait=[("LG", wo_ld[og][1]), ("LD", xr_ld[-1])],
                                 inc="MM"))

                def vev(e, tc=tc, og=og, bank=bank, par=gidx % 2):
                    return e.tensor_tensor(
                        x1[:, tc * D + og * 512: tc * D + (og + 1) * 512],
                        pf[bank][:, :], xres[par][:, :], ALU.add)
                op_ve.append(opn("vector", vev, wait=[("MM", op_mm[-1])], inc="VE"))
                gidx += 1

                if og == 3:
                    def ln1(e, tc=tc):
                        xt = x1[:, tc * D:(tc + 1) * D]
                        e.tensor_reduce(bnag[:, 0:1], xt, AX.X, ALU.add)
                        e.drain()
                        e.tensor_scalar(bnag[:, 0:1], bnag[:, 0:1], 1.0 / D, None,
                                        ALU.mult)
                        e.drain()
                        e.tensor_scalar(x2t[0][:, :], xt, bnag[:, 0:1], None,
                                        ALU.subtract)
                        e.tensor_tensor(x2t[0][:, :], x2t[0][:, :], x2t[0][:, :],
                                        ALU.mult)
                        e.tensor_reduce(bnag[:, 1:2], x2t[0][:, :], AX.X, ALU.add)
                        e.drain()
                        r = e.tensor_scalar(rstd[:, :], bnag[:, 1:2], 1.0 / D, EPS,
                                            ALU.mult, ALU.add)
                        e.drain()
                        return e.tensor_copy(dmy[:, :], bnag[:, 0:1])
                    lv = opn("vector", ln1, wait=[("VE", op_ve[-1])], inc="VE")
                    sq = opn("scalar", lambda e: e.activation(rstd[:, :], rstd[:, :],
                                                              AF.Sqrt),
                             wait=[("VE", lv)], inc="SE")
                    rv = opn("vector", lambda e: (e.reciprocal(rstd[:, :], rstd[:, :]),
                                                  e.drain())[0],
                             wait=[("SE", sq)], inc="VE")

                    def nrm(e, tc=tc):
                        e.drain()
                        e.tensor_scalar(x1[:, tc * D:(tc + 1) * D],
                                        x1[:, tc * D:(tc + 1) * D],
                                        bnag[:, 0:1], rstd[:, :], ALU.subtract,
                                        ALU.mult)
                        e.tensor_reduce(mcol[:, :], x1[:, tc * D:(tc + 1) * D], AX.X,
                                        ALU.max, apply_absolute_value=True)
                        e.drain()
                        if tc == 0:
                            e.tensor_copy(scal[:, 10:11], mcol[:, :])
                        else:
                            r = e.tensor_tensor(scal[:, 10:11], scal[:, 10:11],
                                                mcol[:, :], ALU.max)
                        e.drain()
                        return e.tensor_copy(dmy[:, :], bnag[:, 0:1])
                    ln_last = opn("vector", nrm, wait=[("VE", rv)], inc="VE")
            if og + 2 < 4:
                reg_wo_ld(og + 2)
        opj_ve = P.ctr["VE"]
        ln1_ve = P.ctr["VE"]

        dbg("x1", lambda: x1[:, :], [128, TC * D], F32)

        gx = opn("gpsimd", lambda e: e.tensor_reduce(
            scal[0:1, 10:11], scal[:, 10:11], AX.C, ALU.max),
            wait=[("VE", ln1_ve)], inc="GE")
        sgx = opn("gpsimd", lambda e: e.dma_start(out=arx_in[0:1, 0:1],
                                                  in_=scal[0:1, 10:11]),
                  wait=[("GE", gx)], inc="LG", incn=16)
        arx = opn("gpsimd", lambda e: e.collective_compute(
            "AllReduce", ALU.max, replica_groups=RG,
            ins=[arx_in.opt()], outs=[arx_out.opt()]),
            wait=[("LG", sgx), ("CC", agP)], inc="CC")
        ldx = opn("gpsimd", lambda e: e.dma_start(
            out=scal[:, 14:15], in_=arx_out[0:1, 0:1].broadcast_to((128, 1))),
                  wait=[("CC", arx)], inc="LG", incn=16)

        def xscal(e):
            e.tensor_scalar(scal[:, 7:8], scal[:, 14:15], 1.0 / 448.0, 1e-6,
                            ALU.mult, ALU.max)
            e.drain()
            e.reciprocal(scal[:, 2:3], scal[:, 7:8])
            e.drain()
            e.tensor_scalar(scal[:, 2:3], scal[:, 2:3], 0.5, None, ALU.mult)
            e.tensor_tensor(scal[:, 13:14], scal[:, 7:8], scal[:, 5:6], ALU.mult)
            e.drain()
            e.tensor_scalar(scal[:, 13:14], scal[:, 13:14], 4.0, None,
                                ALU.mult)
            e.drain()
            return e.tensor_copy(dmy[:, :], bnag[:, 0:1])
        bcx_ve = opn("vector", xscal, wait=[("LG", ldx)], inc="VE")

    if upto >= 5:
        x8_ve = []
        n = 0
        for tc in range(TC):
            wq = [("VE", bcx_ve)] if tc == 0 else [("VE", P.ctr["VE"])]
            qv = opn("vector", lambda e, tc=tc: e.tensor_scalar(
                qtm8[:, :], x1[:, tc * D:(tc + 1) * D], scal[:, 2:3], None, ALU.mult),
                wait=wq, inc="VE")
            for k in range(KC):
                half = n % 2
                w = [("VE", qv)]
                if n >= 2:
                    w.append(("VE", x8_ve[n - 2]))
                tmm = opn("tensor", lambda e, k=k, half=half: e.transpose(
                    (ptr8 if half == 0 else ptr28)[:, 0:256:2],
                    qtm8[:, k * 128:(k + 1) * 128], id8[:, :]),
                    wait=w, inc="MM")
                if n % 8 == 7:
                    P.raw("tensor", lambda e: e.drain())
                x8_ve.append(opn("vector", lambda e, k=k, tc=tc, half=half:
                                 e.tensor_copy(
                                     x8fm[:, k * T + tc * 128:k * T + (tc + 1) * 128],
                                     (ptr8 if half == 0 else ptr28)[:, 0:256:2]),
                                 wait=[("MM", tmm)], inc="VE"))
                n += 1
        x8_done = P.ctr["VE"]
        dbg("x8fm", lambda: x8fm[:, :], [128, KC * T], FP8)

    if upto >= 6:
        fc_ld, fc_mm, fc_se, fc_st, fc_vmax = [], [], [], [], []

        def reg_fc_ld(ug):
            w = [("CC", agF)]
            if ug == 0:
                w += [("MM", op_mm[-1]), ("SE", pass2_se)]
            if ug >= 2:
                w.append(("MM", fc_mm[(ug - 2) * 8 + 7]))

            def lfc(e, ug=ug):
                r, lh = ug // 2, (ug % 2) * 512
                src = w8f_out.rearrange("(r kc p) u -> p r kc u", r=NCORES, kc=KC)[
                    :, r, :, lh:lh + 512]
                return e.dma_start(
                    out=w8slab[ug % 2][:, :].rearrange("p (kc u) -> p kc u", kc=KC),
                    in_=src)
            fc_ld.append((("sync", "LD") if ug % 2 == 0 else ("scalar", "LS"),
                          opn("sync" if ug % 2 == 0 else "scalar", lfc, wait=w,
                              inc="LD" if ug % 2 == 0 else "LS", incn=16)))

        reg_fc_ld(0)
        reg_fc_ld(1)
        for ug in range(16):
            for uc4 in range(4):
                for th in range(2):
                    gi = len(fc_mm)
                    bank = gi % 4
                    w2 = [(fc_ld[ug][0][1], fc_ld[ug][1]), ("VE", x8_done)]
                    if gi >= 4:
                        w2.append(("SE", fc_se[gi - 4]))

                    def mfc(e, ug=ug, uc4=uc4, th=th, bank=bank):
                        r = None
                        for k in range(KC):
                            r = e.matmul(pf[bank][:, :],
                                         w8slab[ug % 2][:, k * 512 + uc4 * 128:
                                                        k * 512 + (uc4 + 1) * 128],
                                         x8fm[:, k * T + th * 512:
                                              k * T + th * 512 + 512],
                                         start=(k == 0), stop=(k == KC - 1))
                        return r
                    fc_mm.append(opn("tensor", mfc, wait=w2, inc="MM"))

                    w3 = [("MM", fc_mm[-1])]
                    if gi >= 2:
                        w3 += [("ST", fc_st[gi - 2]), ("VE", fc_vmax[gi - 2])]
                    fc_se.append(opn("scalar", lambda e, bank=bank, gi=gi:
                                     e.activation(rstg[gi % 2][:, :], pf[bank][:, :],
                                                  AF.Relu), wait=w3, inc="SE"))

                    def vmx(e, gi=gi):
                        e.tensor_reduce(mcol[:, :], rstg[gi % 2][:, :], AX.X,
                                        ALU.max, apply_absolute_value=True)
                        e.drain()
                        if gi == 0:
                            r = e.tensor_copy(scal[:, 11:12], mcol[:, :])
                        else:
                            r = e.tensor_tensor(scal[:, 11:12], scal[:, 11:12],
                                                mcol[:, :], ALU.max)
                        e.drain()
                        return e.tensor_copy(dmy[:, :], bnag[:, 0:1])
                    fc_vmax.append(opn("vector", vmx, wait=[("SE", fc_se[-1])],
                                       inc="VE"))
                    uc = ug * 4 + uc4
                    fc_st.append(opn("sync", lambda e, uc=uc, th=th, gi=gi:
                                     e.dma_start(out=gdram[uc, :, th * 512:(th + 1) * 512],
                                                 in_=rstg[gi % 2][:, :]),
                                     wait=[("SE", fc_se[-1])], inc="ST", incn=16))
            if ug + 2 < 16:
                reg_fc_ld(ug + 2)
        fc_done_st = P.ctr["ST"]
        fc_done_mm = fc_mm[-1]

        gh = opn("gpsimd", lambda e: e.tensor_reduce(
            scal[0:1, 11:12], scal[:, 11:12], AX.C, ALU.max),
            wait=[("VE", P.ctr["VE"])], inc="GE")
        sgh = opn("gpsimd", lambda e: e.dma_start(out=arh_in[0:1, 0:1],
                                                  in_=scal[0:1, 11:12]),
                  wait=[("GE", gh)], inc="LG", incn=16)
        arh = opn("gpsimd", lambda e: e.collective_compute(
            "AllReduce", ALU.max, replica_groups=RG,
            ins=[arh_in.opt()], outs=[arh_out.opt()]),
            wait=[("LG", sgh)], inc="CC")
        ldh = opn("gpsimd", lambda e: e.dma_start(
            out=scal[:, 14:15], in_=arh_out[0:1, 0:1].broadcast_to((128, 1))),
            wait=[("CC", arh)], inc="LG", incn=16)

        def hscal(e):
            e.tensor_tensor(scal[:, 15:16], scal[:, 13:14], scal[:, 14:15], ALU.mult)
            e.drain()
            e.tensor_tensor(scal[:, 15:16], scal[:, 15:16], scal[:, 15:16], ALU.mult)
            e.drain()
            e.tensor_scalar(scal[:, 12:13], scal[:, 15:16], 1.0 / 448.0, 1e-6,
                            ALU.mult, ALU.max)
            e.tensor_tensor(scal[:, 15:16], scal[:, 13:14], scal[:, 13:14], ALU.mult)
            e.drain()
            e.reciprocal(scal[:, 3:4], scal[:, 12:13])
            e.drain()
            e.tensor_tensor(scal[:, 3:4], scal[:, 3:4], scal[:, 15:16], ALU.mult)
            e.drain()
            e.tensor_scalar(scal[:, 3:4], scal[:, 3:4], 0.5, None, ALU.mult)
            e.tensor_tensor(scal[:, 4:5], scal[:, 12:13], scal[:, 6:7], ALU.mult)
            e.drain()
            e.tensor_scalar(scal[:, 4:5], scal[:, 4:5], 4.0, None, ALU.mult)
            e.drain()
            return e.tensor_copy(dmy[:, :], bnag[:, 0:1])
        bch_ve = opn("vector", hscal, wait=[("LG", ldh)], inc="VE")
        if "rg" in debug:
            rg_out = nc.dram_tensor("dbg_rg", [4 * 128, T], BF16,
                                    kind="ExternalOutput").ap()
            for uu, uc in enumerate((0, 1, 31, 63)):
                opn("sync", lambda e, uu=uu, uc=uc: e.dma_start(
                    out=rg_out[uu * 128:(uu + 1) * 128, :], in_=gdram[uc, :, :]),
                    wait=[("ST", fc_done_st)], inc="ST", incn=16)
        dbg("scal", lambda: scal[:, :], [128, 16], F32)

    if upto >= 7:
        prev_pair_mm = None
        prev2_pair_mm = None
        prev_gq_last = None
        prev_out_st = [None, None]
        gq_of = {}

        def prep(pr):
            gcur = g8ab[pr % 2]
            gq_ve = []
            for hfl in range(2):
                lds = []
                for blk in range(4):
                    w = []
                    if pr == 0 and hfl == 0 and blk == 0:
                        w = [("ST", fc_done_st), ("VE", bch_ve), ("MM", fc_done_mm)]
                    elif blk == 0:
                        if hfl == 1:
                            w = [("VE", gq_ve[-1])]
                        elif prev_gq_last is not None:
                            w = [("VE", prev_gq_last)]

                    def ldr(e, pr=pr, hfl=hfl, blk=blk):
                        u0 = hfl * 32 + blk * 8
                        src = gdram[u0:u0 + 8, :, pr * 256:(pr + 1) * 256].rearrange(
                            "u p c -> p u c")
                        dst = rload[:, blk * 8 * 256:(blk + 1) * 8 * 256].rearrange(
                            "p (u c) -> p u c", u=8)
                        return e.dma_start(out=dst, in_=src)
                    lds.append(opn("sync", ldr, wait=w, inc="LD", incn=16))

                def gq(e, hfl=hfl, gcur=gcur):
                    r = None
                    for uo in range(32):
                        uc = hfl * 32 + uo
                        r = e.scalar_tensor_tensor(
                            gcur[:, uc * 256:(uc + 1) * 256],
                            rload[:, uo * 256:(uo + 1) * 256], scal[:, 3:4],
                            rload[:, uo * 256:(uo + 1) * 256], ALU.mult, ALU.mult)
                    return r
                wq = [("LD", lds[-1])]
                if hfl == 0 and pr >= 2:
                    wq.append(("MM", pair_last_mm[pr - 2]))
                gq_ve.append(opn("vector", gq, wait=wq, inc="VE"))
            gq_of[pr] = gq_ve[-1]
            return gq_ve[-1]

        pair_last_mm = {}
        prev_gq_last = prep(0)
        for pr in range(4):
            gcur = g8ab[pr % 2]
            gq_done = gq_of[pr]
            pr_mm = []
            pr_ev = []
            for og in range(4):
                for us in range(8):
                    if len(pr_mm) >= 2:
                        wl = [("MM", pr_mm[-2])]
                    elif pr > 0:
                        wl = [("MM", pair_last_mm[pr - 1])]
                    else:
                        wl = [("VE", gq_done)]
                    qeng = "sync" if us % 2 == 0 else "scalar"
                    qsem = "LD" if us % 2 == 0 else "LS"
                    sli = opn(qeng, lambda e, og=og, us=us: e.dma_start(
                        out=w8psl[us % 2][:, :].rearrange("p (u o) -> p u o", u=8),
                        in_=w8p_out.rearrange("(u p) o -> p u o", p=128)[
                            :, us * 8:(us + 1) * 8, og * 512:(og + 1) * 512]),
                        wait=wl, inc=qsem, incn=16)

                    wm = [(qsem, sli), ("VE", gq_done)]
                    if og >= 2 and us == 0:
                        wm.append(("VE", pr_ev[(og - 2) * 2 + 1]))

                    def mpr(e, og=og, us=us, gcur=gcur):
                        r = None
                        for j in range(8):
                            uc = us * 8 + j
                            for half in range(2):
                                r = e.matmul(pf[(og % 2) * 2 + half][:, :],
                                             gcur[:, uc * 256 + half * 128:
                                                  uc * 256 + half * 128 + 128],
                                             w8psl[us % 2][:, j * 512:(j + 1) * 512],
                                             start=(us == 0 and j == 0),
                                             stop=(us == 7 and j == 7))
                        return r
                    pr_mm.append(opn("tensor", mpr, wait=wm, inc="MM"))
                for half in range(2):
                    tc = pr * 2 + half

                    def pev(e, og=og, half=half, tc=tc):
                        return e.scalar_tensor_tensor(
                            x2t[half][:, og * 512:(og + 1) * 512],
                            pf[(og % 2) * 2 + half][:, :], scal[:, 4:5],
                            x1[:, tc * D + og * 512: tc * D + (og + 1) * 512],
                            ALU.mult, ALU.add)
                    wpe = [("MM", pr_mm[-1])]
                    if og == 0 and prev_out_st[half] is not None:
                        wpe.append(("ST", prev_out_st[half]))
                    pr_ev.append(opn("vector", pev, wait=wpe, inc="VE"))
                if og == 1 and pr + 1 < 4:
                    prev_gq_last = prep(pr + 1)
            pair_last_mm[pr] = pr_mm[-1]
            prev2_pair_mm = prev_pair_mm
            prev_pair_mm = pr_mm[-1]

            for half in range(2):
                tc = pr * 2 + half

                def ln2(e, half=half):
                    xt = x2t[half][:, :]
                    e.tensor_reduce(bnag[:, 0:1], xt, AX.X, ALU.add)
                    e.drain()
                    e.tensor_scalar(bnag[:, 0:1], bnag[:, 0:1], 1.0 / D, None,
                                    ALU.mult)
                    e.drain()
                    e.tensor_scalar(lnscr2[:, :], xt, bnag[:, 0:1], None,
                                    ALU.subtract)
                    e.tensor_tensor(lnscr2[:, :], lnscr2[:, :], lnscr2[:, :],
                                    ALU.mult)
                    e.tensor_reduce(bnag[:, 1:2], lnscr2[:, :], AX.X, ALU.add)
                    e.drain()
                    r = e.tensor_scalar(rstd[:, :], bnag[:, 1:2], 1.0 / D, EPS,
                                        ALU.mult, ALU.add)
                    e.drain()
                    return e.tensor_copy(dmy[:, :], bnag[:, 0:1])
                lv = opn("vector", ln2, wait=[("VE", pr_ev[-1])], inc="VE")
                sq2 = opn("scalar", lambda e: e.activation(rstd[:, :], rstd[:, :],
                                                           AF.Sqrt),
                          wait=[("VE", lv)], inc="SE")
                rv2 = opn("vector", lambda e: (e.reciprocal(rstd[:, :], rstd[:, :]),
                                               e.drain())[0],
                          wait=[("SE", sq2)], inc="VE")
                nv = opn("vector", lambda e, half=half: e.tensor_scalar(
                    x2t[half][:, :], x2t[half][:, :], bnag[:, 0:1], rstd[:, :],
                    ALU.subtract, ALU.mult), wait=[("VE", rv2)], inc="VE")
                prev_out_st[half] = opn("sync", lambda e, tc=tc, half=half:
                    e.dma_start(out=yout[tc * 128:(tc + 1) * 128, :],
                                in_=x2t[half][:, :]),
                    wait=[("VE", nv)], inc="ST", incn=16)
    P.raw("sync", lambda e, v=P.ctr["VE"]: e.wait_ge(P.sems["VE"], v))
    P.raw("sync", lambda e, v=P.ctr["SE"]: e.wait_ge(P.sems["SE"], v))
    P.raw("sync", lambda e, v=P.ctr["MM"]: e.wait_ge(P.sems["MM"], v))
    P.raw("sync", lambda e, v=P.ctr["GE"]: e.wait_ge(P.sems["GE"], v))

    def fin(e):
        for (dap, apget) in dbg_list:
            e.dma_start(out=dap[:, :], in_=apget()).then_inc(P.sems["ST"], 16)
            P.ctr["ST"] += 16
        if upto < 7:
            e.dma_start(out=yout[0:128, :], in_=x1[:, 0:D]).then_inc(P.sems["ST"], 16)
            P.ctr["ST"] += 16
        e.wait_ge(P.sems["ST"], P.ctr["ST"])
    P.raw("sync", fin)

    with nc.Block() as block:
        @block.sync
        def _(e):
            for f in P.streams["sync"]:
                f(e)

        @block.gpsimd
        def _(e):
            for f in P.streams["gpsimd"]:
                f(e)

        @block.vector
        def _(e):
            for f in P.streams["vector"]:
                f(e)

        @block.scalar
        def _(e):
            for f in P.streams["scalar"]:
                f(e)

        @block.tensor
        def _(e):
            for f in P.streams["tensor"]:
                f(e)

    es.close()
    return nc


def make_consts(core_id):
    import ml_dtypes
    i = np.arange(128, dtype=np.float32)
    return {
        "identb": np.eye(128).astype(ml_dtypes.bfloat16),
        "ident8": np.eye(128).astype(ml_dtypes.float8_e4m3fn),
        "bciota": np.ascontiguousarray(np.tile(i[None, :], (128, 1))),
        "difft": np.ascontiguousarray(i[None, :] - i[:, None]),
        "iotac": np.ascontiguousarray((i - 128.0)[:, None]),
        "dvecbc": np.ascontiguousarray(np.tile(
            np.where(np.arange(8) < core_id,
                     1024.0 * (core_id - 1 - np.arange(8)),
                     1e9).astype(np.float32)[None, :], (128, 1))),
    }


def kernel(L0_attn_x, L0_attn_Wqkv, L0_attn_bqkv, L0_attn_Wout, L0_attn_bout,
           L0_attn_slopes, L0_ffn_w_fc, L0_ffn_w_proj):
    x = np.ascontiguousarray(np.asarray(L0_attn_x, dtype=np.float32).reshape(8192, 2048))
    wqkv = np.ascontiguousarray(np.asarray(L0_attn_Wqkv, dtype=np.float32))
    bq = np.asarray(L0_attn_bqkv, dtype=np.float32)
    wout_a = np.ascontiguousarray(np.asarray(L0_attn_Wout, dtype=np.float32))
    bo = np.asarray(L0_attn_bout, dtype=np.float32)
    wfc = np.ascontiguousarray(np.asarray(L0_ffn_w_fc, dtype=np.float32))
    wpr = np.ascontiguousarray(np.asarray(L0_ffn_w_proj, dtype=np.float32))
    slopes_t = np.ascontiguousarray(
        np.tile(np.asarray(L0_attn_slopes, dtype=np.float32)[None, :], (128, 1)))
    bq_cols = np.ascontiguousarray(bq[:4096].reshape(32, 128).T)
    nc = build_program(upto=7)
    in_maps = []
    for c in range(NCORES):
        m = dict(
            xin=np.ascontiguousarray(x[c * T:(c + 1) * T]),
            wqkv=wqkv,
            bq_cols=bq_cols,
            bq_vrow=np.ascontiguousarray(bq[4096:].reshape(1, 2048)),
            wout=wout_a,
            bo_row=np.ascontiguousarray(bo.reshape(1, 2048)),
            slopes=slopes_t,
            wfc_sh=np.ascontiguousarray(wfc[c * (FF // 8):(c + 1) * (FF // 8)]),
            wpr_sh=np.ascontiguousarray(wpr[:, c * (FF // 8):(c + 1) * (FF // 8)]),
        )
        m.update(make_consts(c))
        in_maps.append(m)
    res = run_bass_kernel_spmd(nc, in_maps, core_ids=list(range(NCORES)))
    out = np.concatenate([r["yout"] for r in res.results], axis=0)
    return np.ascontiguousarray(out.reshape(1, 8192, 2048))


# version 13
# speedup vs baseline: 1.1170x; 1.0381x over previous
import contextlib
import numpy as np
import concourse.bass as bass
import concourse.mybir as mybir
from concourse.bass_utils import run_bass_kernel_spmd

F32 = mybir.dt.float32
F32R = mybir.dt.float32r
BF16 = mybir.dt.bfloat16
FP8 = mybir.dt.float8e4
AF = mybir.ActivationFunctionType
ALU = mybir.AluOpType
AX = mybir.AxisListType

NCORES = 8
T = 1024
D = 2048
H = 16
CB = 128
NCH = T // CB
FF = 8192
KC = D // 128
TC = T // 128
EPS = 1e-5

SB_BASE = 16896
SB_TAB = SB_BASE + 0
SB_SBF = SB_BASE + 16384
SB_SCR = SB_BASE + 20480
SB_STRA = SB_BASE + 28672
SB_STRB = SB_BASE + 36864
SB_BIGA = SB_BASE + 69632
SB_BIGB = SB_BASE + 102400
SB_BIGC = SB_BASE + 135168
SB_BIGD = SB_BASE + 167936


class Prog:
    ENGS = ("sync", "scalar", "vector", "tensor", "gpsimd")

    def __init__(self, nc, es):
        self.nc = nc
        self.streams = {k: [] for k in self.ENGS}
        self.sems = {}
        self.ctr = {}
        for n in ("LD", "LG", "LS", "MM", "VE", "SE", "GE", "CC", "ST"):
            self.sems[n] = es.enter_context(nc.semaphore(n))
            self.ctr[n] = 0

    def op(self, eng, fn, wait=(), inc=None, incn=1):
        post = None
        if inc is not None:
            self.ctr[inc] += incn
            post = self.ctr[inc]
        sems = self.sems

        def run(e, fn=fn, wait=tuple(wait), inc=inc, incn=incn):
            for (n, v) in wait:
                if v > 0:
                    e.wait_ge(sems[n], v)
            r = fn(e)
            if inc is not None:
                if r is None:
                    r = e.nop()
                r.then_inc(sems[inc], incn)
        self.streams[eng].append(run)
        return post

    def raw(self, eng, fn):
        self.streams[eng].append(fn)


def build_program(upto=10, debug=(), subph=99):
    import os as _os
    V4 = int(_os.environ.get("V4", "9"))
    nc = bass.Bass("TRN2", target_bir_lowering=False)

    xin = nc.dram_tensor("xin", [T, D], F32, kind="ExternalInput").ap()
    wqkv = nc.dram_tensor("wqkv", [D, 3 * D], F32, kind="ExternalInput").ap()
    bq_cols = nc.dram_tensor("bq_cols", [128, 32], F32, kind="ExternalInput").ap()
    bq_vrow = nc.dram_tensor("bq_vrow", [1, D], F32, kind="ExternalInput").ap()
    wout = nc.dram_tensor("wout", [D, D], F32, kind="ExternalInput").ap()
    bo_row = nc.dram_tensor("bo_row", [1, D], F32, kind="ExternalInput").ap()
    slopes = nc.dram_tensor("slopes", [128, H], F32, kind="ExternalInput").ap()
    wfc_sh = nc.dram_tensor("wfc_sh", [FF // 8, D], F32, kind="ExternalInput").ap()
    wpr_sh = nc.dram_tensor("wpr_sh", [D, FF // 8], F32, kind="ExternalInput").ap()
    identb = nc.dram_tensor("identb", [128, 128], BF16, kind="ExternalInput").ap()
    ident8 = nc.dram_tensor("ident8", [128, 128], FP8, kind="ExternalInput").ap()
    bciota = nc.dram_tensor("bciota", [128, 128], F32, kind="ExternalInput").ap()
    difft = nc.dram_tensor("difft", [128, 128], F32, kind="ExternalInput").ap()
    iotac = nc.dram_tensor("iotac", [128, 1], F32, kind="ExternalInput").ap()
    dvecbc = nc.dram_tensor("dvecbc", [128, 8], F32, kind="ExternalInput").ap()

    yout = nc.dram_tensor("yout", [T, D], F32, kind="ExternalOutput").ap()

    agt_in = nc.dram_tensor("agt_in", [128, H * 128], BF16).ap()
    agt_out = nc.dram_tensor("agt_out", [NCORES * 128, H * 128], BF16).ap()
    arw_in = nc.dram_tensor("arw_in", [1, 128], F32).ap()
    arw_out = nc.dram_tensor("arw_out", [1, 128], F32).ap()
    arx_in = nc.dram_tensor("arx_in", [1, 128], F32).ap()
    arx_out = nc.dram_tensor("arx_out", [1, 128], F32).ap()
    arh_in = nc.dram_tensor("arh_in", [1, 128], F32).ap()
    arh_out = nc.dram_tensor("arh_out", [1, 128], F32).ap()
    w8f_in = nc.dram_tensor("w8f_in", [D, FF // 8], FP8).ap()
    w8f_out = nc.dram_tensor("w8f_out", [NCORES * D, FF // 8], FP8).ap()
    w8p_in = nc.dram_tensor("w8p_in", [FF // 8, D], FP8).ap()
    w8p_out = nc.dram_tensor("w8p_out", [NCORES * (FF // 8), D], FP8).ap()
    gdram = nc.dram_tensor("gdram", [FF // 128, 128, T], BF16).ap()

    def sb(name, shape, dt, off):
        return nc.alloc_sbuf_tensor_at(name, shape, dt, offset=off).ap()

    dq_t = sb("dq_t", [128, H * 128], BF16, SB_TAB + 0)
    din_t = sb("din_t", [128, H * 128], BF16, SB_TAB + 4096)
    bq_vr = sb("bq_vr", [1, D], BF16, SB_TAB + 8192)
    bo_r = sb("bo_r", [1, D], BF16, SB_TAB + 12288)
    kd_c = sb("kd_c", [128, H], F32, SB_SCR + 0)
    bd_c = sb("bd_c", [128, H], F32, SB_SCR + 64)
    sl_c = sb("sl_c", [128, H], F32, SB_SCR + 128)
    bqc = sb("bqc", [128, 32], F32, SB_SCR + 192)
    scal = sb("scal", [128, 16], F32, SB_SCR + 320)
    s11 = sb("s11", [1, 32], F32, SB_SCR + 384)
    bcw = sb("bcw", [128, 128], F32, SB_SCR + 512)
    idb = sb("idb", [128, 128], BF16, SB_SCR + 1024)
    id8 = sb("id8", [128, 128], FP8, SB_SCR + 1280)
    onesb = sb("onesb", [1, 512], BF16, SB_SCR + 1408)
    onesr_f = sb("onesr", [1, 128], F32, SB_SCR + 2432)
    onesr = sb("onesr_r", [1, 128], F32R, SB_SCR + 5760)
    s11r = sb("s11r", [1, 8], F32R, SB_SCR + 6272)
    dmy = sb("dmy", [128, 1], F32, SB_SCR + 6304)
    bcio = sb("bcio", [128, 128], F32, SB_SCR + 2944)
    difft_s = sb("difft_s", [128, 128], F32, SB_SCR + 3456)
    iotac_s = sb("iotac_s", [128, 1], F32, SB_SCR + 3968)
    dvec_s = sb("dvec_s", [128, 8], F32, SB_SCR + 4000)
    mcol = sb("mcol", [128, 1], F32, SB_SCR + 4032)
    bnst = sb("bnst", [128, 24], F32, SB_SCR + 4064)
    bnag = sb("bnag", [128, 2], F32, SB_SCR + 4160)
    rstd = sb("rstd", [128, 1], F32, SB_SCR + 4192)
    qd_s = sb("qd_s", [128, 2 * 128], BF16, SB_SCR + 4224)
    st_s = sb("st_s", [128, 2 * 128], BF16, SB_SCR + 4736)
    tabtmp = sb("tabtmp", [128, 128], F32, SB_SCR + 5248)
    qtm8 = sb("qtm8", [128, D], FP8, SB_STRB + 24576)

    sbf = sb("sbf", [128, H * 128], BF16, SB_SBF)

    wqslab = [sb(f"wqslab{i}", [128, KC * 128], BF16, SB_STRA + i * 4096) for i in range(2)]
    w8psl = [sb(f"w8psl{i}", [128, 8 * 512], FP8, SB_STRA + i * 4096) for i in range(2)]
    xres = [sb(f"xres{i}", [128, 512], F32, SB_STRA + i * 2048) for i in range(2)]

    xtm = sb("xtm", [128, TC * D], BF16, SB_STRB)
    vslab = [sb(f"vslab{i}", [128, KC * 512], BF16, SB_STRB + i * 16384) for i in range(2)]
    wsh_f32 = sb("wsh_f32", [128, D], F32, SB_STRB + 0)
    wsh_f32b = sb("wsh_f32b", [128, D], F32, SB_STRB + 8192)
    wsh_f8 = sb("wsh_f8", [128, D], FP8, SB_STRB + 8192)
    wsh_stg = sb("wsh_stg", [128, 16 * 1024], FP8, SB_STRB + 10752)
    tjin = sb("tjin", [128, H * 128], BF16, SB_STRB + 27136)
    sini = sb("sini", [128, H * 128], F32, SB_STRB + 0)
    attnt = sb("attnt", [128, H * 1024], BF16, SB_STRB)
    w8slab = [sb(f"w8slab{i}", [128, KC * 512], FP8, SB_STRB + i * 8192) for i in range(2)]
    rstg = [sb(f"rstg{i}", [128, 512], BF16, SB_STRB + 16384 + i * 1024) for i in range(2)]
    lnscr2 = sb("lnscr2", [128, D], F32, SB_BIGA + 0)
    g8ab = [sb("g8a", [128, 64 * 256], FP8, SB_STRB + 16384),
            sb("g8b", [128, 64 * 256], FP8, SB_STRB + 0)]

    qfm = sb("qfm", [128, H * 1024], BF16, SB_BIGA)
    woslab = [sb(f"woslab{i}", [128, KC * 512], BF16, SB_BIGA + i * 16384) for i in range(2)]
    x8fm = sb("x8fm", [128, KC * T], FP8, SB_BIGA)
    rload = sb("rload", [128, 32 * 256], BF16, SB_BIGA + 16384)
    kfm = sb("kfm", [128, H * 1024], BF16, SB_BIGB)
    x2t = [sb(f"x2t{i}", [128, D], F32, SB_BIGB + 16384 + i * 8192) for i in range(2)]
    wostg = [sb(f"wostg{i}", [128, D], F32, SB_BIGB + 16384 + i * 8192) for i in range(2)]
    vtm = sb("vtm", [128, H * 1024], BF16, SB_BIGC)
    xfm = sb("xfm", [128, KC * T], BF16, SB_BIGD)
    kdt = sb("kdt", [128, H * 1024], BF16, SB_BIGD)
    x1 = sb("x1", [128, TC * D], F32, SB_BIGC)

    pf = [nc.alloc_psum_tensor(f"pf{i}", [128, 512], F32).ap() for i in range(6)]
    ptr = nc.alloc_psum_tensor("ptr", [128, 1024], BF16).ap()
    paux = nc.alloc_psum_tensor("paux", [128, 512], F32).ap()
    ptr8 = ptr.bitcast(FP8)
    ptr2 = paux.bitcast(BF16)
    ptr28 = paux.bitcast(FP8)

    es = contextlib.ExitStack()
    P = Prog(nc, es)
    opn = P.op
    RG = [list(range(NCORES))]

    dbg_list = []

    def dbg(name, apget, shape, dt):
        if name in debug:
            dap = nc.dram_tensor("dbg_" + name, shape, dt, kind="ExternalOutput").ap()
            dbg_list.append((dap, apget))

    for dst, src in [(idb, identb), (id8, ident8), (bcio, bciota),
                     (difft_s, difft), (iotac_s, iotac), (dvec_s, dvecbc),
                     (sl_c, slopes), (bqc, bq_cols)]:
        opn("sync", lambda e, d=dst, s=src: e.dma_start(out=d[:, :], in_=s[:, :]),
            inc="LD", incn=16)
    consts_ld = P.ctr["LD"]

    opn("gpsimd", lambda e: e.dma_start(out=bq_vr[:, :], in_=bq_vrow[:, :]),
        inc="LG", incn=16)
    opn("gpsimd", lambda e: e.dma_start(out=bo_r[:, :], in_=bo_row[:, :]),
        inc="LG", incn=16)
    for tc in range(TC):
        opn("gpsimd", lambda e, tc=tc: e.dma_start(
            out=xtm[:, tc * D:(tc + 1) * D], in_=xin[tc * 128:(tc + 1) * 128, :]),
            inc="LG", incn=16)
    x_ld = P.ctr["LG"]

    first = True
    for h in range(H):
        w = [("LD", consts_ld)] if first else [("SE", P.ctr["SE"])]
        ve = opn("vector", lambda e, h=h: e.tensor_scalar(
            tabtmp[:, :], bcio[:, :], sl_c[:, h:h + 1], -1.0, ALU.mult, ALU.mult),
            wait=w, inc="VE")
        opn("scalar", lambda e, h=h: e.activation(
            dq_t[:, h * 128:(h + 1) * 128], tabtmp[:, :], AF.Exp),
            wait=[("VE", ve)], inc="SE")
        ve = opn("vector", lambda e, h=h: e.tensor_scalar(
            tabtmp[:, :], difft_s[:, :], sl_c[:, h:h + 1], -1.0, ALU.mult, ALU.mult),
            wait=[("SE", P.ctr["SE"])], inc="VE")
        ge = opn("gpsimd", lambda e: e.affine_select(
            tabtmp[:, :], tabtmp[:, :], pattern=[[1, 128]], compare_op=ALU.is_ge,
            fill=-1e5, base=0, channel_multiplier=-1),
            wait=[("VE", ve)], inc="GE")
        opn("scalar", lambda e, h=h: e.activation(
            din_t[:, h * 128:(h + 1) * 128], tabtmp[:, :], AF.Exp),
            wait=[("GE", ge)], inc="SE")
        first = False

    def cols_v(e):
        e.memset(onesb[:, :], 1.0)
        e.memset(onesr_f[:, :], 1.0)
        for h in range(H):
            e.tensor_scalar(kd_c[:, h:h + 1], iotac_s[:, :], sl_c[:, h:h + 1],
                            None, ALU.mult)
        e.tensor_scalar(bd_c[:, :], sl_c[:, :], -128.0, None, ALU.mult)
        r = None
        for h in range(H):
            r = e.tensor_scalar(bcw[:, h * 8:(h + 1) * 8], dvec_s[:, :],
                                sl_c[:, h:h + 1], -1.0, ALU.mult, ALU.mult)
        return r
    ve = opn("vector", cols_v, wait=[("SE", P.ctr["SE"])], inc="VE")

    def cols_s(e):
        e.activation(kd_c[:, :], kd_c[:, :], AF.Exp)
        e.activation(bd_c[:, :], bd_c[:, :], AF.Exp)
        return e.activation(bcw[:, :], bcw[:, :], AF.Exp)
    tabs_se = opn("scalar", cols_s, wait=[("VE", ve)], inc="SE")

    xtr_ve = []
    n = 0
    for tc in range(TC):
        for k in range(KC):
            half = n % 2
            w = [("LG", x_ld), ("LD", consts_ld)] if n == 0 else []
            if n >= 2:
                w = [("VE", xtr_ve[n - 2])]
            mmv = opn("tensor", lambda e, tc=tc, k=k, half=half: e.transpose(
                (ptr if half == 0 else ptr2)[:, 0:128],
                xtm[:, tc * D + k * 128: tc * D + (k + 1) * 128], idb[:, :]),
                wait=w, inc="MM")
            if n % 8 == 7:
                P.raw("tensor", lambda e: e.drain())
            xtr_ve.append(opn("vector", lambda e, tc=tc, k=k, half=half: e.tensor_copy(
                xfm[:, k * T + tc * 128: k * T + (tc + 1) * 128],
                (ptr if half == 0 else ptr2)[:, 0:128]),
                wait=[("MM", mmv)], inc="VE"))
            n += 1
    xfm_ve = P.ctr["VE"]
    dbg("xfm", lambda: xfm[:, :], [128, KC * T], BF16)

    if upto >= 1:
        qk_mm = []
        qk_ve = []
        qk_ld = []

        def reg_qk_ld(i):
            w = []
            if i >= 2:
                w = [("MM", qk_mm[(i - 2) * 2 + 1])]
            qk_ld.append(opn("gpsimd", lambda e, i=i: e.dma_start(
                out=wqslab[i % 2][:, :].rearrange("p (kc fc) -> p kc fc", kc=KC),
                in_=wqkv.rearrange("(kc p) f -> p kc f", p=128)[:, :, i * 128:(i + 1) * 128]),
                wait=w, inc="LG", incn=16))

        reg_qk_ld(0)
        reg_qk_ld(1)
        gidx = 0
        for i in range(32):
            for th in range(2):
                bank = gidx % 4
                w = [("LG", qk_ld[i])]
                if gidx == 0:
                    w.append(("VE", xfm_ve))
                if gidx >= 4:
                    w.append(("VE", qk_ve[gidx - 4]))

                def mmf(e, i=i, th=th, bank=bank):
                    r = None
                    for k in range(KC):
                        r = e.matmul(pf[bank][:, :],
                                     wqslab[i % 2][:, k * 128:(k + 1) * 128],
                                     xfm[:, k * T + th * 512: k * T + th * 512 + 512],
                                     start=(k == 0), stop=(k == KC - 1))
                    return r
                qk_mm.append(opn("tensor", mmf, wait=w, inc="MM"))

                def evf(e, i=i, th=th, bank=bank):
                    dst = qfm if i < 16 else kfm
                    h = i % 16
                    return e.tensor_scalar(
                        dst[:, h * T + th * 512: h * T + th * 512 + 512],
                        pf[bank][:, :], bqc[:, i:i + 1], None, ALU.add)
                qk_ve.append(opn("vector", evf, wait=[("MM", qk_mm[-1])], inc="VE"))
                gidx += 1
            if i + 2 < 32:
                reg_qk_ld(i + 2)

        v_ld = []
        v_mm = []
        v_ve = []

        def reg_v_ld(fg):
            w = []
            if fg == 0:
                w = [("VE", xfm_ve)]
            if fg >= 2:
                w = [("MM", v_mm[(fg - 2) * TC + TC - 1])]
            v_ld.append(opn("gpsimd", lambda e, fg=fg: e.dma_start(
                out=vslab[fg % 2][:, :].rearrange("p (kc fc) -> p kc fc", kc=KC),
                in_=wqkv.rearrange("(kc p) f -> p kc f", p=128)[
                    :, :, 2 * D + fg * 512: 2 * D + (fg + 1) * 512]),
                wait=w, inc="LG", incn=16))

        reg_v_ld(0)
        reg_v_ld(1)
        gidx = 0
        for fg in range(4):
            for tc in range(TC):
                bank = gidx % 4
                w = [("LG", v_ld[fg]), ("SE", tabs_se)]
                if gidx == 0:
                    w.append(("VE", qk_ve[-1]))
                if gidx >= 4:
                    w.append(("VE", v_ve[gidx - 4]))

                def mmv(e, fg=fg, tc=tc, bank=bank):
                    for k in range(KC):
                        e.matmul(pf[bank][:, :],
                                 xfm[:, k * T + tc * 128:k * T + (tc + 1) * 128],
                                 vslab[fg % 2][:, k * 512:(k + 1) * 512],
                                 start=(k == 0), stop=False)
                    return e.matmul(pf[bank][:, :], onesb[0:1, 0:128],
                                    bq_vr[0:1, fg * 512:(fg + 1) * 512],
                                    start=False, stop=True)
                v_mm.append(opn("tensor", mmv, wait=w, inc="MM"))

                def evv(e, fg=fg, tc=tc, bank=bank):
                    src = pf[bank][:, :].rearrange("p (j e) -> p j e", j=4)
                    dst = vtm[:, :].rearrange("p (h c e) -> p h c e", h=H, c=NCH)[
                        :, fg * 4:fg * 4 + 4, tc, :]
                    return e.tensor_copy(dst, src)
                v_ve.append(opn("vector", evv, wait=[("MM", v_mm[-1])], inc="VE"))
                gidx += 1
            if fg + 2 < 4:
                reg_v_ld(fg + 2)

        ktr_ve = []
        n = 0
        for h in range(H):
            for c in range(NCH):
                half = n % 2
                w = []
                if n >= 2:
                    w = [("VE", ktr_ve[n - 2])]
                mmv = opn("tensor", lambda e, h=h, c=c, half=half: e.transpose(
                    (ptr if half == 0 else ptr2)[:, 0:128],
                    kfm[:, h * T + c * 128: h * T + (c + 1) * 128], idb[:, :]),
                    wait=w, inc="MM")
                if n % 8 == 7:
                    P.raw("tensor", lambda e: e.drain())
                ktr_ve.append(opn("vector", lambda e, h=h, c=c, half=half:
                                  e.tensor_scalar(
                                      kdt[:, (h * NCH + c) * 128:(h * NCH + c + 1) * 128],
                                      (ptr if half == 0 else ptr2)[:, 0:128],
                                      kd_c[:, h:h + 1], None, ALU.mult),
                                  wait=[("MM", mmv)], inc="VE"))
                n += 1
        kdt_ve = P.ctr["VE"]
        last_v_mm = v_mm[-1]
        dbg("qfm", lambda: qfm[:, :], [128, H * 1024], BF16)
        dbg("kfm", lambda: kfm[:, :], [128, H * 1024], BF16)
        dbg("vtm", lambda: vtm[:, :], [128, H * 1024], BF16)
        dbg("kdt", lambda: kdt[:, :], [128, H * 1024], BF16)

    if upto >= 2:
        sz_ve = opn("vector", lambda e: e.memset(sbf[:, :], 0.0), inc="VE")
        p1_mm = []
        p1_ve = []
        nidx = 0
        p1_order = [(hp + dh, c) for hp in range(0, H, 2) for c in range(NCH)
                    for dh in (0, 1)]
        for (h, c) in p1_order:
            if True:
                bank = 4 + (nidx % 2)
                w = [("VE", kdt_ve)] if nidx == 0 else []
                if nidx >= 2:
                    w.append(("VE", p1_ve[nidx - 2]))

                def smm(e, h=h, c=c, bank=bank):
                    i = h * NCH + c
                    return e.matmul(pf[bank][:, 0:128],
                                    kdt[:, i * 128:(i + 1) * 128],
                                    vtm[:, i * 128:(i + 1) * 128],
                                    start=True, stop=True)
                p1_mm.append(opn("tensor", smm, wait=w, inc="MM"))
                if nidx % 4 == 3:
                    P.raw("tensor", lambda e: e.drain())

                def sstt(e, h=h, bank=bank):
                    return e.scalar_tensor_tensor(
                        sbf[:, h * 128:(h + 1) * 128], sbf[:, h * 128:(h + 1) * 128],
                        bd_c[:, h:h + 1], pf[bank][:, 0:128], ALU.mult, ALU.add)
                p1_ve.append(opn("vector", sstt,
                                 wait=[("MM", p1_mm[-1]), ("VE", sz_ve)], inc="VE"))
                nidx += 1
        p1_done_ve = P.ctr["VE"]

        if subph < 2:
            upto = 0
        texp = None
        if subph >= 2:
            texp = opn("sync", lambda e: e.dma_start(out=agt_in[:, :], in_=sbf[:, :]),
                       wait=[("VE", p1_done_ve)], inc="LD", incn=16)
        agT = None
        if subph >= 2:
            agT = opn("gpsimd", lambda e: e.collective_compute(
                "AllGather", ALU.bypass, replica_groups=RG,
                ins=[agt_in.opt()], outs=[agt_out.opt()]),
                wait=[("LD", texp)], inc="CC")

        if subph < 3:
            upto = 0
        wsh_ld = []
        if subph >= 3:
            wsh_ve = []
            for i in range(24):
                if i < 2:
                    w = [("MM", last_v_mm)]
                else:
                    w = [("VE", wsh_ve[i - 2])]

                def lsh(e, i=i, buf=(wsh_f32, wsh_f32b)[i % 2]):
                    if i < 8:
                        return e.dma_start(out=buf[:, :],
                                           in_=wfc_sh[i * 128:(i + 1) * 128, :])
                    j = i - 8
                    return e.dma_start(out=buf[:, 0:1024],
                                       in_=wpr_sh[j * 128:(j + 1) * 128, :])
                wsh_ld.append(opn("scalar", lsh, wait=w, inc="LS", incn=16))

                def msh(e, i=i, buf=(wsh_f32, wsh_f32b)[i % 2]):
                    width = D if i < 8 else 1024
                    col = 8 if i < 8 else 9
                    e.tensor_reduce(mcol[:, :], buf[:, 0:width], AX.X, ALU.max,
                                    apply_absolute_value=True)
                    e.drain()
                    if i == 0 or i == 8:
                        e.tensor_copy(scal[:, col:col + 1], mcol[:, :])
                    else:
                        r = e.tensor_tensor(scal[:, col:col + 1],
                                            scal[:, col:col + 1], mcol[:, :],
                                            ALU.max)
                    e.drain()
                    return e.tensor_copy(dmy[:, :], bnag[:, 0:1])
                wsh_ve.append(opn("vector", msh, wait=[("LS", wsh_ld[-1])], inc="VE"))

            def gmax_f(e):
                e.tensor_reduce(scal[0:1, 8:9], scal[:, 8:9], AX.C, ALU.max)
                return e.tensor_reduce(scal[0:1, 9:10], scal[:, 9:10], AX.C, ALU.max)
            gmax = opn("gpsimd", gmax_f, wait=[("VE", wsh_ve[-1])], inc="GE")

            def stgw_f(e):
                e.dma_start(out=arw_in[0:1, 0:1], in_=scal[0:1, 8:9]).then_inc(
                    P.sems["LG"], 16)
                return e.dma_start(out=arw_in[0:1, 1:2], in_=scal[0:1, 9:10])
            opn("gpsimd", stgw_f, wait=[("GE", gmax)], inc="LG", incn=16)
            P.ctr["LG"] += 16
            stgw = P.ctr["LG"]
            arw = opn("gpsimd", lambda e: e.collective_compute(
                "AllReduce", ALU.max, replica_groups=RG,
                ins=[arw_in.opt()], outs=[arw_out.opt()]),
                wait=[("LG", stgw)], inc="CC")

            def ldw_f(e):
                e.dma_start(out=scal[:, 14:15],
                            in_=arw_out[0:1, 0:1].broadcast_to((128, 1))).then_inc(
                    P.sems["LG"], 16)
                return e.dma_start(out=scal[:, 15:16],
                                   in_=arw_out[0:1, 1:2].broadcast_to((128, 1)))
            opn("gpsimd", ldw_f, wait=[("CC", arw)], inc="LG", incn=16)
            P.ctr["LG"] += 16
            ldw = P.ctr["LG"]

            def wscal(e):
                e.tensor_scalar(scal[:, 5:7], scal[:, 14:16], 1.0 / 448.0, 1e-6,
                                ALU.mult, ALU.max)
                e.drain()
                e.reciprocal(scal[:, 0:2], scal[:, 5:7])
                e.drain()
                e.tensor_scalar(scal[:, 0:2], scal[:, 0:2], 0.5, None, ALU.mult)
                e.drain()
                return e.tensor_copy(dmy[:, :], bnag[:, 0:1])
            bc_ve = opn("vector", wscal, wait=[("LG", ldw)], inc="VE")


            if subph < 6:
                upto = 0
            if subph >= 6:
                comb_last = None
                for r in range(NCORES):
                    wl = [("CC", agT)]
                    if r == 0:
                        wl.append(("VE", wsh_ve[-1]))
                    else:
                        wl.append(("VE", comb_last))
                    lt = opn("sync", lambda e, r=r: e.dma_start(
                        out=tjin[:, :], in_=agt_out[r * 128:(r + 1) * 128, :]),
                        wait=wl, inc="LD", incn=16)
                    for h in range(H):
                        def cmb(e, r=r, h=h):
                            if r == 0:
                                return e.tensor_scalar(
                                    sini[:, h * 128:(h + 1) * 128],
                                    tjin[:, h * 128:(h + 1) * 128],
                                    bcw[:, h * 8 + r:h * 8 + r + 1], None, ALU.mult)
                            return e.scalar_tensor_tensor(
                                sini[:, h * 128:(h + 1) * 128],
                                tjin[:, h * 128:(h + 1) * 128],
                                bcw[:, h * 8 + r:h * 8 + r + 1],
                                sini[:, h * 128:(h + 1) * 128], ALU.mult, ALU.add)
                        comb_last = opn("vector", cmb,
                                        wait=[("LD", lt)] if h == 0 else [], inc="VE")
                sinit_ve = opn("vector", lambda e: e.tensor_copy(sbf[:, :], sini[:, :]),
                               inc="VE")
                dbg("sinit", lambda: sbf[:, :], [128, H * 128], BF16)

                half_ev = {0: None, 1: None}
                first_q = True
                for i in range(8):
                    for hh in range(2):
                        w = []
                        if first_q:
                            w = [("VE", sinit_ve)]
                            first_q = False
                        elif half_ev[hh] is not None:
                            w = [("VE", half_ev[hh])]
                        ldq = opn("scalar", lambda e, i=i, hh=hh: e.dma_start(
                            out=wsh_f32[:, hh * 1024:(hh + 1) * 1024],
                            in_=wfc_sh[i * 128:(i + 1) * 128,
                                       hh * 1024:(hh + 1) * 1024]),
                            wait=w, inc="LS", incn=16)
                        qv = opn("vector", lambda e, hh=hh: e.tensor_scalar(
                            wsh_f8[:, hh * 1024:(hh + 1) * 1024],
                            wsh_f32[:, hh * 1024:(hh + 1) * 1024],
                            scal[:, 0:1], None, ALU.mult),
                            wait=[("LS", ldq)], inc="VE")
                        prev_tr = []
                        for kk in range(8):
                            k = hh * 8 + kk
                            halfp = kk % 2
                            w2 = [("VE", qv)]
                            if kk >= 2:
                                w2.append(("VE", prev_tr[kk - 2]))
                            tmm = opn("tensor", lambda e, k=k, halfp=halfp: e.transpose(
                                (ptr8 if halfp == 0 else ptr28)[:, 0:256:2],
                                wsh_f8[:, k * 128:(k + 1) * 128], id8[:, :]),
                                wait=w2, inc="MM")
                            if kk == 7:
                                P.raw("tensor", lambda e: e.drain())
                            prev_tr.append(opn("vector",
                                               lambda e, k=k, halfp=halfp, i=i:
                                               e.tensor_copy(
                                                   wsh_stg[:, k * 1024 + i * 128:
                                                           k * 1024 + (i + 1) * 128],
                                                   (ptr8 if halfp == 0 else ptr28)[:, 0:256:2]),
                                               wait=[("MM", tmm)], inc="VE"))
                        half_ev[hh] = prev_tr[-1]
                stg_f = opn("sync", lambda e: e.dma_start(
                    out=w8f_in.rearrange("(kc p) u -> p kc u", p=128),
                    in_=wsh_stg[:, :].rearrange("p (kc u) -> p kc u", kc=KC)),
                    wait=[("VE", P.ctr["VE"])], inc="LD", incn=16)
                agF = opn("gpsimd", lambda e: e.collective_compute(
                    "AllGather", ALU.bypass, replica_groups=RG,
                    ins=[w8f_in.opt()], outs=[w8f_out.opt()]),
                    wait=[("LD", stg_f), ("CC", agT)], inc="CC")

                for i in range(16):
                    w = [("LD", stg_f)] if i == 0 else [("VE", P.ctr["VE"])]
                    ldq = opn("scalar", lambda e, i=i: e.dma_start(
                        out=wsh_f32[:, 0:1024], in_=wpr_sh[i * 128:(i + 1) * 128, :]),
                        wait=w, inc="LS", incn=16)
                    qv = opn("vector", lambda e: e.tensor_scalar(
                        wsh_f8[:, 0:1024], wsh_f32[:, 0:1024], scal[:, 1:2], None, ALU.mult),
                        wait=[("LS", ldq)], inc="VE")
                    prev_tr = []
                    for k in range(8):
                        half = k % 2
                        w2 = [("VE", qv)]
                        if k >= 2:
                            w2.append(("VE", prev_tr[k - 2]))
                        tmm = opn("tensor", lambda e, k=k, half=half: e.transpose(
                            (ptr8 if half == 0 else ptr28)[:, 0:256:2],
                            wsh_f8[:, k * 128:(k + 1) * 128], id8[:, :]),
                            wait=w2, inc="MM")
                        prev_tr.append(opn("vector", lambda e, k=k, half=half, i=i:
                                           e.tensor_copy(
                                               wsh_stg[:, k * 2048 + i * 128:
                                                       k * 2048 + (i + 1) * 128],
                                               (ptr8 if half == 0 else ptr28)[:, 0:256:2]),
                                           wait=[("MM", tmm)], inc="VE"))
                    P.raw("tensor", lambda e: e.drain())
                stg_p = opn("sync", lambda e: e.dma_start(
                    out=w8p_in.rearrange("(kc p) o -> p kc o", p=128),
                    in_=wsh_stg[:, :].rearrange("p (kc o) -> p kc o", kc=8)),
                    wait=[("VE", P.ctr["VE"])], inc="LD", incn=16)
                agP = opn("gpsimd", lambda e: e.collective_compute(
                    "AllGather", ALU.bypass, replica_groups=RG,
                    ins=[w8p_in.opt()], outs=[w8p_out.opt()]),
                    wait=[("LD", stg_p), ("CC", agF)], inc="CC")
    dbg("bcw", lambda: bcw[:, :], [128, 128], F32)
    dbg("kdc", lambda: kd_c[:, :], [128, H], F32)

    if upto >= 3:
        p2_sco, p2_stv, p2_qdv, p2_omm, p2_umm, p2_oev, p2_stt = ([] for _ in range(7))
        nidx = 0
        p2_order = [(hp + dh, c) for hp in range(0, H, 2) for c in range(NCH)
                    for dh in (0, 1)]
        for (h, c) in p2_order:
            if True:
                p = nidx % 2
                A, B, Cb = pf[0 + p], pf[2 + p], pf[4 + p]
                i = h * NCH + c
                w = [("VE", sinit_ve)] if nidx == 0 else []
                if nidx >= 2:
                    w.append(("VE", p2_stv[nidx - 2]))
                p2_sco.append(opn("tensor", lambda e, i=i, A=A: e.matmul(
                    A[:, 0:128], kfm[:, i * 128:(i + 1) * 128],
                    qfm[:, i * 128:(i + 1) * 128], start=True, stop=True),
                    wait=w, inc="MM"))

                w2 = [("MM", p2_sco[-1])]
                if nidx >= 2:
                    w2.append(("MM", p2_omm[nidx - 2]))
                p2_stv.append(opn("vector", lambda e, h=h, p=p, A=A:
                                  e.tensor_tensor(st_s[:, p * 128:(p + 1) * 128],
                                                  A[:, 0:128],
                                                  din_t[:, h * 128:(h + 1) * 128],
                                                  ALU.mult), wait=w2, inc="VE"))
                w3 = []
                if nidx >= 2:
                    w3.append(("MM", p2_omm[nidx - 2]))
                p2_qdv.append(opn("vector", lambda e, i=i, h=h, p=p:
                                  e.tensor_tensor(qd_s[:, p * 128:(p + 1) * 128],
                                                  qfm[:, i * 128:(i + 1) * 128],
                                                  dq_t[:, h * 128:(h + 1) * 128],
                                                  ALU.mult), wait=w3, inc="VE"))
                w4 = [("VE", p2_qdv[-1])]
                if nidx >= 2:
                    w4.append(("SE", p2_oev[nidx - 2]))
                    w4.append(("VE", p2_stt[nidx - 2]))

                def mo(e, i=i, h=h, p=p, B=B):
                    e.matmul(B[:, 0:128], vtm[:, i * 128:(i + 1) * 128],
                             st_s[:, p * 128:(p + 1) * 128], start=True, stop=False)
                    return e.matmul(B[:, 0:128], sbf[:, h * 128:(h + 1) * 128],
                                    qd_s[:, p * 128:(p + 1) * 128],
                                    start=False, stop=True)
                p2_omm.append(opn("tensor", mo, wait=w4, inc="MM"))

                w5 = []
                if nidx >= 2:
                    w5.append(("VE", p2_stt[nidx - 2]))
                p2_umm.append(opn("tensor", lambda e, i=i, Cb=Cb: e.matmul(
                    Cb[:, 0:128], kdt[:, i * 128:(i + 1) * 128],
                    vtm[:, i * 128:(i + 1) * 128], start=True, stop=True),
                    wait=w5, inc="MM"))
                if nidx % 2 == 1:
                    P.raw("tensor", lambda e: e.drain())
                w_oev = [("MM", p2_omm[-1])]
                if nidx == 32:
                    w_oev.append(("LD", stg_p))
                p2_oev.append(opn("scalar", lambda e, i=i, B=B: e.activation(
                    attnt[:, i * 128:(i + 1) * 128], B[:, 0:128], AF.Copy),
                    wait=w_oev, inc="SE"))
                p2_stt.append(opn("vector", lambda e, h=h, Cb=Cb:
                                  e.scalar_tensor_tensor(
                                      sbf[:, h * 128:(h + 1) * 128],
                                      sbf[:, h * 128:(h + 1) * 128],
                                      bd_c[:, h:h + 1], Cb[:, 0:128],
                                      ALU.mult, ALU.add),
                                  wait=[("MM", p2_umm[-1])], inc="VE"))
                nidx += 1
        pass2_se = P.ctr["SE"]
        pass2_ve = P.ctr["VE"]
        dbg("attnt", lambda: attnt[:, :], [128, H * 1024], BF16)

    if upto >= 4:
        op_mm = []
        op_ve = []
        wo_ld = []
        xr_ld = []

        def reg_wo_ld(og):
            w = [("VE", pass2_ve), ("SE", pass2_se)]
            if og >= 2:
                w = [("MM", op_mm[(og - 2) * TC * 2 + TC * 2 - 1])]
            evs = []
            for hh in range(2):
                evs.append(opn("gpsimd", lambda e, og=og, hh=hh: e.dma_start(
                    out=woslab[og % 2][:, hh * 8 * 512:(hh + 1) * 8 * 512].rearrange(
                        "p (kc oc) -> p kc oc", kc=8),
                    in_=wout.rearrange("(kc p) o -> p kc o", p=128)[
                        :, hh * 8:(hh + 1) * 8, og * 512:(og + 1) * 512]),
                    wait=w if hh == 0 else [], inc="LG", incn=16))
            wo_ld.append(evs)

        reg_wo_ld(0)
        reg_wo_ld(1)
        gidx = 0
        ln_last = None
        for og in range(4):
            for tc in range(TC):
                bank = gidx % 4
                wxl = []
                if gidx >= 2:
                    wxl = [("VE", op_ve[gidx - 2])]
                elif og == 0 and tc == 0:
                    wxl = [("SE", pass2_se)]
                xr_ld.append(opn("sync", lambda e, og=og, tc=tc, par=gidx % 2:
                                 e.dma_start(out=xres[par][:, :],
                                             in_=xin[tc * 128:(tc + 1) * 128,
                                                     og * 512:(og + 1) * 512]),
                                 wait=wxl, inc="LD", incn=16))

                w = [("LG", wo_ld[og][0])]
                if gidx == 0:
                    w += [("VE", pass2_ve), ("SE", pass2_se)]
                if gidx >= 4:
                    w.append(("VE", op_ve[gidx - 4]))

                def mop0(e, tc=tc, og=og, bank=bank):
                    r = None
                    for k in range(8):
                        r = e.matmul(pf[bank][:, :],
                                     attnt[:, k * T + tc * 128:k * T + (tc + 1) * 128],
                                     woslab[og % 2][:, k * 512:(k + 1) * 512],
                                     start=(k == 0), stop=False)
                    return r
                op_mm.append(opn("tensor", mop0, wait=w, inc="MM"))

                def mop1(e, tc=tc, og=og, bank=bank):
                    for k in range(8, KC):
                        e.matmul(pf[bank][:, :],
                                 attnt[:, k * T + tc * 128:k * T + (tc + 1) * 128],
                                 woslab[og % 2][:, k * 512:(k + 1) * 512],
                                 start=False, stop=False)
                    return e.matmul(pf[bank][:, :], onesb[0:1, 0:128],
                                    bo_r[0:1, og * 512:(og + 1) * 512],
                                    start=False, stop=True)
                op_mm.append(opn("tensor", mop1,
                                 wait=[("LG", wo_ld[og][1]), ("LD", xr_ld[-1])],
                                 inc="MM"))

                def vev(e, tc=tc, og=og, bank=bank, par=gidx % 2):
                    return e.tensor_tensor(
                        x1[:, tc * D + og * 512: tc * D + (og + 1) * 512],
                        pf[bank][:, :], xres[par][:, :], ALU.add)
                op_ve.append(opn("vector", vev, wait=[("MM", op_mm[-1])], inc="VE"))
                gidx += 1

                if og == 3:
                    def ln1(e, tc=tc):
                        xt = x1[:, tc * D:(tc + 1) * D]
                        e.tensor_reduce(bnag[:, 0:1], xt, AX.X, ALU.add)
                        e.drain()
                        e.tensor_scalar(bnag[:, 0:1], bnag[:, 0:1], 1.0 / D, None,
                                        ALU.mult)
                        e.drain()
                        e.tensor_scalar(x2t[0][:, :], xt, bnag[:, 0:1], None,
                                        ALU.subtract)
                        e.tensor_tensor(x2t[0][:, :], x2t[0][:, :], x2t[0][:, :],
                                        ALU.mult)
                        e.tensor_reduce(bnag[:, 1:2], x2t[0][:, :], AX.X, ALU.add)
                        e.drain()
                        r = e.tensor_scalar(rstd[:, :], bnag[:, 1:2], 1.0 / D, EPS,
                                            ALU.mult, ALU.add)
                        e.drain()
                        return e.tensor_copy(dmy[:, :], bnag[:, 0:1])
                    lv = opn("vector", ln1, wait=[("VE", op_ve[-1])], inc="VE")
                    sq = opn("scalar", lambda e: e.activation(rstd[:, :], rstd[:, :],
                                                              AF.Sqrt),
                             wait=[("VE", lv)], inc="SE")
                    rv = opn("vector", lambda e: (e.reciprocal(rstd[:, :], rstd[:, :]),
                                                  e.drain())[0],
                             wait=[("SE", sq)], inc="VE")

                    def nrm(e, tc=tc):
                        e.drain()
                        e.tensor_scalar(x1[:, tc * D:(tc + 1) * D],
                                        x1[:, tc * D:(tc + 1) * D],
                                        bnag[:, 0:1], rstd[:, :], ALU.subtract,
                                        ALU.mult)
                        e.tensor_reduce(mcol[:, :], x1[:, tc * D:(tc + 1) * D], AX.X,
                                        ALU.max, apply_absolute_value=True)
                        e.drain()
                        if tc == 0:
                            e.tensor_copy(scal[:, 10:11], mcol[:, :])
                        else:
                            r = e.tensor_tensor(scal[:, 10:11], scal[:, 10:11],
                                                mcol[:, :], ALU.max)
                        e.drain()
                        return e.tensor_copy(dmy[:, :], bnag[:, 0:1])
                    ln_last = opn("vector", nrm, wait=[("VE", rv)], inc="VE")
            if og + 2 < 4:
                reg_wo_ld(og + 2)
        opj_ve = P.ctr["VE"]
        ln1_ve = P.ctr["VE"]

        dbg("x1", lambda: x1[:, :], [128, TC * D], F32)

        gx = opn("gpsimd", lambda e: e.tensor_reduce(
            scal[0:1, 10:11], scal[:, 10:11], AX.C, ALU.max),
            wait=[("VE", ln1_ve)], inc="GE")
        sgx = opn("gpsimd", lambda e: e.dma_start(out=arx_in[0:1, 0:1],
                                                  in_=scal[0:1, 10:11]),
                  wait=[("GE", gx)], inc="LG", incn=16)
        arx = opn("gpsimd", lambda e: e.collective_compute(
            "AllReduce", ALU.max, replica_groups=RG,
            ins=[arx_in.opt()], outs=[arx_out.opt()]),
            wait=[("LG", sgx), ("CC", agP)], inc="CC")
        ldx = opn("gpsimd", lambda e: e.dma_start(
            out=scal[:, 14:15], in_=arx_out[0:1, 0:1].broadcast_to((128, 1))),
                  wait=[("CC", arx)], inc="LG", incn=16)

        def xscal(e):
            e.tensor_scalar(scal[:, 7:8], scal[:, 14:15], 1.0 / 448.0, 1e-6,
                            ALU.mult, ALU.max)
            e.drain()
            e.reciprocal(scal[:, 2:3], scal[:, 7:8])
            e.drain()
            e.tensor_scalar(scal[:, 2:3], scal[:, 2:3], 0.5, None, ALU.mult)
            e.tensor_tensor(scal[:, 13:14], scal[:, 7:8], scal[:, 5:6], ALU.mult)
            e.drain()
            e.tensor_scalar(scal[:, 13:14], scal[:, 13:14], 4.0, None,
                                ALU.mult)
            e.drain()
            return e.tensor_copy(dmy[:, :], bnag[:, 0:1])
        bcx_ve = opn("vector", xscal, wait=[("LG", ldx)], inc="VE")

    if upto >= 5:
        x8_ve = []
        n = 0
        for tc in range(TC):
            wq = [("VE", bcx_ve)] if tc == 0 else [("VE", P.ctr["VE"])]
            qv = opn("vector", lambda e, tc=tc: e.tensor_scalar(
                qtm8[:, :], x1[:, tc * D:(tc + 1) * D], scal[:, 2:3], None, ALU.mult),
                wait=wq, inc="VE")
            for k in range(KC):
                half = n % 2
                w = [("VE", qv)]
                if n >= 2:
                    w.append(("VE", x8_ve[n - 2]))
                tmm = opn("tensor", lambda e, k=k, half=half: e.transpose(
                    (ptr8 if half == 0 else ptr28)[:, 0:256:2],
                    qtm8[:, k * 128:(k + 1) * 128], id8[:, :]),
                    wait=w, inc="MM")
                if n % 8 == 7:
                    P.raw("tensor", lambda e: e.drain())
                x8_ve.append(opn("vector", lambda e, k=k, tc=tc, half=half:
                                 e.tensor_copy(
                                     x8fm[:, k * T + tc * 128:k * T + (tc + 1) * 128],
                                     (ptr8 if half == 0 else ptr28)[:, 0:256:2]),
                                 wait=[("MM", tmm)], inc="VE"))
                n += 1
        x8_done = P.ctr["VE"]
        dbg("x8fm", lambda: x8fm[:, :], [128, KC * T], FP8)

    if upto >= 6:
        fc_ld, fc_mm, fc_se, fc_st, fc_vmax = [], [], [], [], []

        def reg_fc_ld(ug):
            w = [("CC", agF)]
            if ug == 0:
                w += [("MM", op_mm[-1]), ("SE", pass2_se)]
            if ug >= 2:
                w.append(("MM", fc_mm[(ug - 2) * 8 + 7]))

            def lfc(e, ug=ug):
                r, lh = ug // 2, (ug % 2) * 512
                src = w8f_out.rearrange("(r kc p) u -> p r kc u", r=NCORES, kc=KC)[
                    :, r, :, lh:lh + 512]
                return e.dma_start(
                    out=w8slab[ug % 2][:, :].rearrange("p (kc u) -> p kc u", kc=KC),
                    in_=src)
            fc_ld.append((("sync", "LD") if ug % 2 == 0 else ("scalar", "LS"),
                          opn("sync" if ug % 2 == 0 else "scalar", lfc, wait=w,
                              inc="LD" if ug % 2 == 0 else "LS", incn=16)))

        reg_fc_ld(0)
        reg_fc_ld(1)
        for ug in range(16):
            for uc4 in range(4):
                for th in range(2):
                    gi = len(fc_mm)
                    bank = gi % 4
                    w2 = [(fc_ld[ug][0][1], fc_ld[ug][1]), ("VE", x8_done)]
                    if gi >= 4:
                        w2.append(("SE", fc_se[gi - 4]))

                    def mfc(e, ug=ug, uc4=uc4, th=th, bank=bank):
                        r = None
                        for k in range(KC):
                            r = e.matmul(pf[bank][:, :],
                                         w8slab[ug % 2][:, k * 512 + uc4 * 128:
                                                        k * 512 + (uc4 + 1) * 128],
                                         x8fm[:, k * T + th * 512:
                                              k * T + th * 512 + 512],
                                         start=(k == 0), stop=(k == KC - 1))
                        return r
                    fc_mm.append(opn("tensor", mfc, wait=w2, inc="MM"))

                    w3 = [("MM", fc_mm[-1])]
                    if gi >= 2:
                        w3 += [("ST", fc_st[gi - 2]), ("VE", fc_vmax[gi - 2])]
                    fc_se.append(opn("scalar", lambda e, bank=bank, gi=gi:
                                     e.activation(rstg[gi % 2][:, :], pf[bank][:, :],
                                                  AF.Relu), wait=w3, inc="SE"))

                    def vmx(e, gi=gi):
                        e.tensor_reduce(mcol[:, :], rstg[gi % 2][:, :], AX.X,
                                        ALU.max, apply_absolute_value=True)
                        e.drain()
                        if gi == 0:
                            r = e.tensor_copy(scal[:, 11:12], mcol[:, :])
                        else:
                            r = e.tensor_tensor(scal[:, 11:12], scal[:, 11:12],
                                                mcol[:, :], ALU.max)
                        e.drain()
                        return e.tensor_copy(dmy[:, :], bnag[:, 0:1])
                    fc_vmax.append(opn("vector", vmx, wait=[("SE", fc_se[-1])],
                                       inc="VE"))
                    uc = ug * 4 + uc4
                    fc_st.append(opn("sync", lambda e, uc=uc, th=th, gi=gi:
                                     e.dma_start(out=gdram[uc, :, th * 512:(th + 1) * 512],
                                                 in_=rstg[gi % 2][:, :]),
                                     wait=[("SE", fc_se[-1])], inc="ST", incn=16))
            if ug + 2 < 16:
                reg_fc_ld(ug + 2)
        fc_done_st = P.ctr["ST"]
        fc_done_mm = fc_mm[-1]

        gh = opn("gpsimd", lambda e: e.tensor_reduce(
            scal[0:1, 11:12], scal[:, 11:12], AX.C, ALU.max),
            wait=[("VE", P.ctr["VE"])], inc="GE")
        sgh = opn("gpsimd", lambda e: e.dma_start(out=arh_in[0:1, 0:1],
                                                  in_=scal[0:1, 11:12]),
                  wait=[("GE", gh)], inc="LG", incn=16)
        arh = opn("gpsimd", lambda e: e.collective_compute(
            "AllReduce", ALU.max, replica_groups=RG,
            ins=[arh_in.opt()], outs=[arh_out.opt()]),
            wait=[("LG", sgh)], inc="CC")
        ldh = opn("gpsimd", lambda e: e.dma_start(
            out=scal[:, 14:15], in_=arh_out[0:1, 0:1].broadcast_to((128, 1))),
            wait=[("CC", arh)], inc="LG", incn=16)

        def hscal(e):
            e.tensor_tensor(scal[:, 15:16], scal[:, 13:14], scal[:, 14:15], ALU.mult)
            e.drain()
            e.tensor_tensor(scal[:, 15:16], scal[:, 15:16], scal[:, 15:16], ALU.mult)
            e.drain()
            e.tensor_scalar(scal[:, 12:13], scal[:, 15:16], 1.0 / 448.0, 1e-6,
                            ALU.mult, ALU.max)
            e.tensor_tensor(scal[:, 15:16], scal[:, 13:14], scal[:, 13:14], ALU.mult)
            e.drain()
            e.reciprocal(scal[:, 3:4], scal[:, 12:13])
            e.drain()
            e.tensor_tensor(scal[:, 3:4], scal[:, 3:4], scal[:, 15:16], ALU.mult)
            e.drain()
            e.tensor_scalar(scal[:, 3:4], scal[:, 3:4], 0.5, None, ALU.mult)
            e.tensor_tensor(scal[:, 4:5], scal[:, 12:13], scal[:, 6:7], ALU.mult)
            e.drain()
            e.tensor_scalar(scal[:, 4:5], scal[:, 4:5], 4.0, None, ALU.mult)
            e.drain()
            return e.tensor_copy(dmy[:, :], bnag[:, 0:1])
        bch_ve = opn("vector", hscal, wait=[("LG", ldh)], inc="VE")
        if "rg" in debug:
            rg_out = nc.dram_tensor("dbg_rg", [4 * 128, T], BF16,
                                    kind="ExternalOutput").ap()
            for uu, uc in enumerate((0, 1, 31, 63)):
                opn("sync", lambda e, uu=uu, uc=uc: e.dma_start(
                    out=rg_out[uu * 128:(uu + 1) * 128, :], in_=gdram[uc, :, :]),
                    wait=[("ST", fc_done_st)], inc="ST", incn=16)
        dbg("scal", lambda: scal[:, :], [128, 16], F32)

    if upto >= 7:
        prev_pair_mm = None
        prev2_pair_mm = None
        prev_gq_last = None
        prev_out_st = [None, None]
        gq_of = {}

        def prep(pr):
            gcur = g8ab[pr % 2]
            gq_ve = []
            for hfl in range(2):
                lds = []
                for blk in range(4):
                    w = []
                    if pr == 0 and hfl == 0 and blk == 0:
                        w = [("ST", fc_done_st), ("VE", bch_ve), ("MM", fc_done_mm)]
                    elif blk == 0:
                        if hfl == 1:
                            w = [("VE", gq_ve[-1])]
                        elif prev_gq_last is not None:
                            w = [("VE", prev_gq_last)]

                    def ldr(e, pr=pr, hfl=hfl, blk=blk):
                        u0 = hfl * 32 + blk * 8
                        src = gdram[u0:u0 + 8, :, pr * 256:(pr + 1) * 256].rearrange(
                            "u p c -> p u c")
                        dst = rload[:, blk * 8 * 256:(blk + 1) * 8 * 256].rearrange(
                            "p (u c) -> p u c", u=8)
                        return e.dma_start(out=dst, in_=src)
                    lds.append(opn("sync", ldr, wait=w, inc="LD", incn=16))

                def gq(e, hfl=hfl, gcur=gcur):
                    r = None
                    for uo in range(32):
                        uc = hfl * 32 + uo
                        r = e.scalar_tensor_tensor(
                            gcur[:, uc * 256:(uc + 1) * 256],
                            rload[:, uo * 256:(uo + 1) * 256], scal[:, 3:4],
                            rload[:, uo * 256:(uo + 1) * 256], ALU.mult, ALU.mult)
                    return r
                wq = [("LD", lds[-1])]
                if hfl == 0 and pr >= 2:
                    wq.append(("MM", pair_last_mm[pr - 2]))
                gq_ve.append(opn("vector", gq, wait=wq, inc="VE"))
            gq_of[pr] = gq_ve[-1]
            return gq_ve[-1]

        pair_last_mm = {}
        prev_gq_last = prep(0)
        for pr in range(4):
            gcur = g8ab[pr % 2]
            gq_done = gq_of[pr]
            pr_mm = []
            pr_ev = []
            for og in range(4):
                for us in range(8):
                    if len(pr_mm) >= 2:
                        wl = [("MM", pr_mm[-2])]
                    elif pr > 0:
                        wl = [("MM", pair_last_mm[pr - 1])]
                    else:
                        wl = [("VE", gq_done)]
                    qeng = "sync" if us % 2 == 0 else "scalar"
                    qsem = "LD" if us % 2 == 0 else "LS"
                    sli = opn(qeng, lambda e, og=og, us=us: e.dma_start(
                        out=w8psl[us % 2][:, :].rearrange("p (u o) -> p u o", u=8),
                        in_=w8p_out.rearrange("(u p) o -> p u o", p=128)[
                            :, us * 8:(us + 1) * 8, og * 512:(og + 1) * 512]),
                        wait=wl, inc=qsem, incn=16)

                    wm = [(qsem, sli), ("VE", gq_done)]
                    if og >= 2 and us == 0:
                        wm.append(("VE", pr_ev[(og - 2) * 2 + 1]))

                    def mpr(e, og=og, us=us, gcur=gcur):
                        r = None
                        for j in range(8):
                            uc = us * 8 + j
                            for half in range(2):
                                r = e.matmul(pf[(og % 2) * 2 + half][:, :],
                                             gcur[:, uc * 256 + half * 128:
                                                  uc * 256 + half * 128 + 128],
                                             w8psl[us % 2][:, j * 512:(j + 1) * 512],
                                             start=(us == 0 and j == 0),
                                             stop=(us == 7 and j == 7))
                        return r
                    pr_mm.append(opn("tensor", mpr, wait=wm, inc="MM"))
                for half in range(2):
                    tc = pr * 2 + half

                    def pev(e, og=og, half=half, tc=tc):
                        return e.scalar_tensor_tensor(
                            x2t[half][:, og * 512:(og + 1) * 512],
                            pf[(og % 2) * 2 + half][:, :], scal[:, 4:5],
                            x1[:, tc * D + og * 512: tc * D + (og + 1) * 512],
                            ALU.mult, ALU.add)
                    wpe = [("MM", pr_mm[-1])]
                    if og == 0 and prev_out_st[half] is not None:
                        wpe.append(("ST", prev_out_st[half]))
                    pr_ev.append(opn("vector", pev, wait=wpe, inc="VE"))
                if og == 1 and pr + 1 < 4:
                    prev_gq_last = prep(pr + 1)
            pair_last_mm[pr] = pr_mm[-1]
            prev2_pair_mm = prev_pair_mm
            prev_pair_mm = pr_mm[-1]

            for half in range(2):
                tc = pr * 2 + half

                def ln2(e, half=half):
                    xt = x2t[half][:, :]
                    e.tensor_reduce(bnag[:, 0:1], xt, AX.X, ALU.add)
                    e.drain()
                    e.tensor_scalar(bnag[:, 0:1], bnag[:, 0:1], 1.0 / D, None,
                                    ALU.mult)
                    e.drain()
                    e.tensor_scalar(lnscr2[:, :], xt, bnag[:, 0:1], None,
                                    ALU.subtract)
                    e.tensor_tensor(lnscr2[:, :], lnscr2[:, :], lnscr2[:, :],
                                    ALU.mult)
                    e.tensor_reduce(bnag[:, 1:2], lnscr2[:, :], AX.X, ALU.add)
                    e.drain()
                    r = e.tensor_scalar(rstd[:, :], bnag[:, 1:2], 1.0 / D, EPS,
                                        ALU.mult, ALU.add)
                    e.drain()
                    return e.tensor_copy(dmy[:, :], bnag[:, 0:1])
                lv = opn("vector", ln2, wait=[("VE", pr_ev[-1])], inc="VE")
                sq2 = opn("scalar", lambda e: e.activation(rstd[:, :], rstd[:, :],
                                                           AF.Sqrt),
                          wait=[("VE", lv)], inc="SE")
                rv2 = opn("vector", lambda e: (e.reciprocal(rstd[:, :], rstd[:, :]),
                                               e.drain())[0],
                          wait=[("SE", sq2)], inc="VE")
                nv = opn("vector", lambda e, half=half: e.tensor_scalar(
                    x2t[half][:, :], x2t[half][:, :], bnag[:, 0:1], rstd[:, :],
                    ALU.subtract, ALU.mult), wait=[("VE", rv2)], inc="VE")
                prev_out_st[half] = opn("sync", lambda e, tc=tc, half=half:
                    e.dma_start(out=yout[tc * 128:(tc + 1) * 128, :],
                                in_=x2t[half][:, :]),
                    wait=[("VE", nv)], inc="ST", incn=16)
    P.raw("sync", lambda e, v=P.ctr["VE"]: e.wait_ge(P.sems["VE"], v))
    P.raw("sync", lambda e, v=P.ctr["SE"]: e.wait_ge(P.sems["SE"], v))
    P.raw("sync", lambda e, v=P.ctr["MM"]: e.wait_ge(P.sems["MM"], v))
    P.raw("sync", lambda e, v=P.ctr["GE"]: e.wait_ge(P.sems["GE"], v))

    def fin(e):
        for (dap, apget) in dbg_list:
            e.dma_start(out=dap[:, :], in_=apget()).then_inc(P.sems["ST"], 16)
            P.ctr["ST"] += 16
        if upto < 7:
            e.dma_start(out=yout[0:128, :], in_=x1[:, 0:D]).then_inc(P.sems["ST"], 16)
            P.ctr["ST"] += 16
        e.wait_ge(P.sems["ST"], P.ctr["ST"])
    P.raw("sync", fin)

    with nc.Block() as block:
        @block.sync
        def _(e):
            for f in P.streams["sync"]:
                f(e)

        @block.gpsimd
        def _(e):
            for f in P.streams["gpsimd"]:
                f(e)

        @block.vector
        def _(e):
            for f in P.streams["vector"]:
                f(e)

        @block.scalar
        def _(e):
            for f in P.streams["scalar"]:
                f(e)

        @block.tensor
        def _(e):
            for f in P.streams["tensor"]:
                f(e)

    es.close()
    return nc


def make_consts(core_id):
    import ml_dtypes
    i = np.arange(128, dtype=np.float32)
    return {
        "identb": np.eye(128).astype(ml_dtypes.bfloat16),
        "ident8": np.eye(128).astype(ml_dtypes.float8_e4m3fn),
        "bciota": np.ascontiguousarray(np.tile(i[None, :], (128, 1))),
        "difft": np.ascontiguousarray(i[None, :] - i[:, None]),
        "iotac": np.ascontiguousarray((i - 128.0)[:, None]),
        "dvecbc": np.ascontiguousarray(np.tile(
            np.where(np.arange(8) < core_id,
                     1024.0 * (core_id - 1 - np.arange(8)),
                     1e9).astype(np.float32)[None, :], (128, 1))),
    }


def kernel(L0_attn_x, L0_attn_Wqkv, L0_attn_bqkv, L0_attn_Wout, L0_attn_bout,
           L0_attn_slopes, L0_ffn_w_fc, L0_ffn_w_proj):
    x = np.ascontiguousarray(np.asarray(L0_attn_x, dtype=np.float32).reshape(8192, 2048))
    wqkv = np.ascontiguousarray(np.asarray(L0_attn_Wqkv, dtype=np.float32))
    bq = np.asarray(L0_attn_bqkv, dtype=np.float32)
    wout_a = np.ascontiguousarray(np.asarray(L0_attn_Wout, dtype=np.float32))
    bo = np.asarray(L0_attn_bout, dtype=np.float32)
    wfc = np.ascontiguousarray(np.asarray(L0_ffn_w_fc, dtype=np.float32))
    wpr = np.ascontiguousarray(np.asarray(L0_ffn_w_proj, dtype=np.float32))
    slopes_t = np.ascontiguousarray(
        np.tile(np.asarray(L0_attn_slopes, dtype=np.float32)[None, :], (128, 1)))
    bq_cols = np.ascontiguousarray(bq[:4096].reshape(32, 128).T)
    nc = build_program(upto=7)
    in_maps = []
    for c in range(NCORES):
        m = dict(
            xin=np.ascontiguousarray(x[c * T:(c + 1) * T]),
            wqkv=wqkv,
            bq_cols=bq_cols,
            bq_vrow=np.ascontiguousarray(bq[4096:].reshape(1, 2048)),
            wout=wout_a,
            bo_row=np.ascontiguousarray(bo.reshape(1, 2048)),
            slopes=slopes_t,
            wfc_sh=np.ascontiguousarray(wfc[c * (FF // 8):(c + 1) * (FF // 8)]),
            wpr_sh=np.ascontiguousarray(wpr[:, c * (FF // 8):(c + 1) * (FF // 8)]),
        )
        m.update(make_consts(c))
        in_maps.append(m)
    res = run_bass_kernel_spmd(nc, in_maps, core_ids=list(range(NCORES)))
    out = np.concatenate([r["yout"] for r in res.results], axis=0)
    return np.ascontiguousarray(out.reshape(1, 8192, 2048))


# version 14 (final)
# speedup vs baseline: 1.1190x; 1.0018x over previous
import contextlib
import numpy as np
import concourse.bass as bass
import concourse.mybir as mybir
from concourse.bass_utils import run_bass_kernel_spmd

F32 = mybir.dt.float32
F32R = mybir.dt.float32r
BF16 = mybir.dt.bfloat16
FP8 = mybir.dt.float8e4
AF = mybir.ActivationFunctionType
ALU = mybir.AluOpType
AX = mybir.AxisListType

NCORES = 8
T = 1024
D = 2048
H = 16
CB = 128
NCH = T // CB
FF = 8192
KC = D // 128
TC = T // 128
EPS = 1e-5

SB_BASE = 16896
SB_TAB = SB_BASE + 0
SB_SBF = SB_BASE + 16384
SB_SCR = SB_BASE + 20480
SB_STRA = SB_BASE + 28672
SB_STRB = SB_BASE + 36864
SB_BIGA = SB_BASE + 69632
SB_BIGB = SB_BASE + 102400
SB_BIGC = SB_BASE + 135168
SB_BIGD = SB_BASE + 167936


class Prog:
    ENGS = ("sync", "scalar", "vector", "tensor", "gpsimd")

    def __init__(self, nc, es):
        self.nc = nc
        self.streams = {k: [] for k in self.ENGS}
        self.sems = {}
        self.ctr = {}
        for n in ("LD", "LG", "LS", "MM", "VE", "SE", "GE", "CC", "ST"):
            self.sems[n] = es.enter_context(nc.semaphore(n))
            self.ctr[n] = 0

    def op(self, eng, fn, wait=(), inc=None, incn=1):
        post = None
        if inc is not None:
            self.ctr[inc] += incn
            post = self.ctr[inc]
        sems = self.sems

        def run(e, fn=fn, wait=tuple(wait), inc=inc, incn=incn):
            for (n, v) in wait:
                if v > 0:
                    e.wait_ge(sems[n], v)
            r = fn(e)
            if inc is not None:
                if r is None:
                    r = e.nop()
                r.then_inc(sems[inc], incn)
        self.streams[eng].append(run)
        return post

    def raw(self, eng, fn):
        self.streams[eng].append(fn)


def build_program(upto=10, debug=(), subph=99):
    import os as _os
    V4 = int(_os.environ.get("V4", "9"))
    nc = bass.Bass("TRN2", target_bir_lowering=False)

    xin = nc.dram_tensor("xin", [T, D], F32, kind="ExternalInput").ap()
    wqkv = nc.dram_tensor("wqkv", [D, 3 * D], F32, kind="ExternalInput").ap()
    bq_cols = nc.dram_tensor("bq_cols", [128, 32], F32, kind="ExternalInput").ap()
    bq_vrow = nc.dram_tensor("bq_vrow", [1, D], F32, kind="ExternalInput").ap()
    wout = nc.dram_tensor("wout", [D, D], F32, kind="ExternalInput").ap()
    bo_row = nc.dram_tensor("bo_row", [1, D], F32, kind="ExternalInput").ap()
    slopes = nc.dram_tensor("slopes", [128, H], F32, kind="ExternalInput").ap()
    wfc_sh = nc.dram_tensor("wfc_sh", [FF // 8, D], F32, kind="ExternalInput").ap()
    wpr_sh = nc.dram_tensor("wpr_sh", [D, FF // 8], F32, kind="ExternalInput").ap()
    identb = nc.dram_tensor("identb", [128, 128], BF16, kind="ExternalInput").ap()
    ident8 = nc.dram_tensor("ident8", [128, 128], FP8, kind="ExternalInput").ap()
    bciota = nc.dram_tensor("bciota", [128, 128], F32, kind="ExternalInput").ap()
    difft = nc.dram_tensor("difft", [128, 128], F32, kind="ExternalInput").ap()
    iotac = nc.dram_tensor("iotac", [128, 1], F32, kind="ExternalInput").ap()
    dvecbc = nc.dram_tensor("dvecbc", [128, 8], F32, kind="ExternalInput").ap()

    yout = nc.dram_tensor("yout", [T, D], F32, kind="ExternalOutput").ap()

    agt_in = nc.dram_tensor("agt_in", [128, H * 128], BF16).ap()
    agt_out = nc.dram_tensor("agt_out", [NCORES * 128, H * 128], BF16).ap()
    arw_in = nc.dram_tensor("arw_in", [1, 128], F32).ap()
    arw_out = nc.dram_tensor("arw_out", [1, 128], F32).ap()
    arx_in = nc.dram_tensor("arx_in", [1, 128], F32).ap()
    arx_out = nc.dram_tensor("arx_out", [1, 128], F32).ap()
    arh_in = nc.dram_tensor("arh_in", [1, 128], F32).ap()
    arh_out = nc.dram_tensor("arh_out", [1, 128], F32).ap()
    w8f_in = nc.dram_tensor("w8f_in", [D, FF // 8], FP8).ap()
    w8f_out = nc.dram_tensor("w8f_out", [NCORES * D, FF // 8], FP8).ap()
    w8p_in = nc.dram_tensor("w8p_in", [FF // 8, D], FP8).ap()
    w8p_out = nc.dram_tensor("w8p_out", [NCORES * (FF // 8), D], FP8).ap()
    gdram = nc.dram_tensor("gdram", [FF // 128, 128, T], BF16).ap()

    def sb(name, shape, dt, off):
        return nc.alloc_sbuf_tensor_at(name, shape, dt, offset=off).ap()

    dq_t = sb("dq_t", [128, H * 128], BF16, SB_TAB + 0)
    din_t = sb("din_t", [128, H * 128], BF16, SB_TAB + 4096)
    bq_vr = sb("bq_vr", [1, D], BF16, SB_TAB + 8192)
    bo_r = sb("bo_r", [1, D], BF16, SB_TAB + 12288)
    kd_c = sb("kd_c", [128, H], F32, SB_SCR + 0)
    bd_c = sb("bd_c", [128, H], F32, SB_SCR + 64)
    sl_c = sb("sl_c", [128, H], F32, SB_SCR + 128)
    bqc = sb("bqc", [128, 32], F32, SB_SCR + 192)
    scal = sb("scal", [128, 16], F32, SB_SCR + 320)
    s11 = sb("s11", [1, 32], F32, SB_SCR + 384)
    bcw = sb("bcw", [128, 128], F32, SB_SCR + 512)
    idb = sb("idb", [128, 128], BF16, SB_SCR + 1024)
    id8 = sb("id8", [128, 128], FP8, SB_SCR + 1280)
    onesb = sb("onesb", [1, 512], BF16, SB_SCR + 1408)
    onesr_f = sb("onesr", [1, 128], F32, SB_SCR + 2432)
    onesr = sb("onesr_r", [1, 128], F32R, SB_SCR + 5760)
    s11r = sb("s11r", [1, 8], F32R, SB_SCR + 6272)
    dmy = sb("dmy", [128, 1], F32, SB_SCR + 6304)
    bcio = sb("bcio", [128, 128], F32, SB_SCR + 2944)
    difft_s = sb("difft_s", [128, 128], F32, SB_SCR + 3456)
    iotac_s = sb("iotac_s", [128, 1], F32, SB_SCR + 3968)
    dvec_s = sb("dvec_s", [128, 8], F32, SB_SCR + 4000)
    mcol = sb("mcol", [128, 1], F32, SB_SCR + 4032)
    bnst = sb("bnst", [128, 24], F32, SB_SCR + 4064)
    bnag = sb("bnag", [128, 2], F32, SB_SCR + 4160)
    rstd = sb("rstd", [128, 1], F32, SB_SCR + 4192)
    qd_s = sb("qd_s", [128, 2 * 128], BF16, SB_SCR + 4224)
    st_s = sb("st_s", [128, 2 * 128], BF16, SB_SCR + 4736)
    tabtmp = sb("tabtmp", [128, 128], F32, SB_SCR + 5248)
    qtm8 = sb("qtm8", [128, D], FP8, SB_STRB + 24576)

    sbf = sb("sbf", [128, H * 128], BF16, SB_SBF)

    wqslab = [sb(f"wqslab{i}", [128, KC * 128], BF16, SB_STRA + i * 4096) for i in range(2)]
    w8psl = [sb(f"w8psl{i}", [128, 8 * 512], FP8, SB_STRA + i * 4096) for i in range(2)]
    xres = [sb(f"xres{i}", [128, 512], F32, SB_STRA + i * 2048) for i in range(2)]

    xtm = sb("xtm", [128, TC * D], BF16, SB_STRB)
    vslab = [sb(f"vslab{i}", [128, KC * 512], BF16, SB_STRB + i * 16384) for i in range(2)]
    wsh_f32 = sb("wsh_f32", [128, D], F32, SB_STRB + 0)
    wsh_f32b = sb("wsh_f32b", [128, D], F32, SB_STRB + 8192)
    wsh_f8 = sb("wsh_f8", [128, D], FP8, SB_STRB + 8192)
    wsh_stg = sb("wsh_stg", [128, 16 * 1024], FP8, SB_STRB + 10752)
    tjin = sb("tjin", [128, H * 128], BF16, SB_STRB + 27136)
    sini = sb("sini", [128, H * 128], F32, SB_STRB + 0)
    attnt = sb("attnt", [128, H * 1024], BF16, SB_STRB)
    w8slab = [sb(f"w8slab{i}", [128, KC * 512], FP8, SB_STRB + i * 8192) for i in range(2)]
    rstg = [sb(f"rstg{i}", [128, 512], BF16, SB_STRB + 16384 + i * 1024) for i in range(2)]
    lnscr2 = sb("lnscr2", [128, D], F32, SB_BIGA + 0)
    g8ab = [sb("g8a", [128, 64 * 256], FP8, SB_STRB + 16384),
            sb("g8b", [128, 64 * 256], FP8, SB_STRB + 0)]

    qfm = sb("qfm", [128, H * 1024], BF16, SB_BIGA)
    woslab = [sb(f"woslab{i}", [128, KC * 512], BF16, SB_BIGA + i * 16384) for i in range(2)]
    x8fm = sb("x8fm", [128, KC * T], FP8, SB_BIGA)
    rload = sb("rload", [128, 32 * 256], BF16, SB_BIGA + 16384)
    kfm = sb("kfm", [128, H * 1024], BF16, SB_BIGB)
    x2t = [sb(f"x2t{i}", [128, D], F32, SB_BIGB + 16384 + i * 8192) for i in range(2)]
    wostg = [sb(f"wostg{i}", [128, D], F32, SB_BIGB + 16384 + i * 8192) for i in range(2)]
    vtm = sb("vtm", [128, H * 1024], BF16, SB_BIGC)
    xfm = sb("xfm", [128, KC * T], BF16, SB_BIGD)
    kdt = sb("kdt", [128, H * 1024], BF16, SB_BIGD)
    x1 = sb("x1", [128, TC * D], F32, SB_BIGC)

    pf = [nc.alloc_psum_tensor(f"pf{i}", [128, 512], F32).ap() for i in range(6)]
    ptr = nc.alloc_psum_tensor("ptr", [128, 1024], BF16).ap()
    paux = nc.alloc_psum_tensor("paux", [128, 512], F32).ap()
    ptr8 = ptr.bitcast(FP8)
    ptr2 = paux.bitcast(BF16)
    ptr28 = paux.bitcast(FP8)

    es = contextlib.ExitStack()
    P = Prog(nc, es)
    opn = P.op
    RG = [list(range(NCORES))]

    dbg_list = []

    def dbg(name, apget, shape, dt):
        if name in debug:
            dap = nc.dram_tensor("dbg_" + name, shape, dt, kind="ExternalOutput").ap()
            dbg_list.append((dap, apget))

    for dst, src in [(idb, identb), (id8, ident8), (bcio, bciota),
                     (difft_s, difft), (iotac_s, iotac), (dvec_s, dvecbc),
                     (sl_c, slopes), (bqc, bq_cols)]:
        opn("sync", lambda e, d=dst, s=src: e.dma_start(out=d[:, :], in_=s[:, :]),
            inc="LD", incn=16)
    consts_ld = P.ctr["LD"]

    opn("gpsimd", lambda e: e.dma_start(out=bq_vr[:, :], in_=bq_vrow[:, :]),
        inc="LG", incn=16)
    opn("gpsimd", lambda e: e.dma_start(out=bo_r[:, :], in_=bo_row[:, :]),
        inc="LG", incn=16)
    for tc in range(TC):
        opn("gpsimd", lambda e, tc=tc: e.dma_start(
            out=xtm[:, tc * D:(tc + 1) * D], in_=xin[tc * 128:(tc + 1) * 128, :]),
            inc="LG", incn=16)
    x_ld = P.ctr["LG"]

    first = True
    for h in range(H):
        w = [("LD", consts_ld)] if first else [("SE", P.ctr["SE"])]
        ve = opn("vector", lambda e, h=h: e.tensor_scalar(
            tabtmp[:, :], bcio[:, :], sl_c[:, h:h + 1], -1.0, ALU.mult, ALU.mult),
            wait=w, inc="VE")
        opn("scalar", lambda e, h=h: e.activation(
            dq_t[:, h * 128:(h + 1) * 128], tabtmp[:, :], AF.Exp),
            wait=[("VE", ve)], inc="SE")
        ve = opn("vector", lambda e, h=h: e.tensor_scalar(
            tabtmp[:, :], difft_s[:, :], sl_c[:, h:h + 1], -1.0, ALU.mult, ALU.mult),
            wait=[("SE", P.ctr["SE"])], inc="VE")
        ge = opn("gpsimd", lambda e: e.affine_select(
            tabtmp[:, :], tabtmp[:, :], pattern=[[1, 128]], compare_op=ALU.is_ge,
            fill=-1e5, base=0, channel_multiplier=-1),
            wait=[("VE", ve)], inc="GE")
        opn("scalar", lambda e, h=h: e.activation(
            din_t[:, h * 128:(h + 1) * 128], tabtmp[:, :], AF.Exp),
            wait=[("GE", ge)], inc="SE")
        first = False

    def cols_v(e):
        e.memset(onesb[:, :], 1.0)
        e.memset(onesr_f[:, :], 1.0)
        for h in range(H):
            e.tensor_scalar(kd_c[:, h:h + 1], iotac_s[:, :], sl_c[:, h:h + 1],
                            None, ALU.mult)
        e.tensor_scalar(bd_c[:, :], sl_c[:, :], -128.0, None, ALU.mult)
        r = None
        for h in range(H):
            r = e.tensor_scalar(bcw[:, h * 8:(h + 1) * 8], dvec_s[:, :],
                                sl_c[:, h:h + 1], -1.0, ALU.mult, ALU.mult)
        return r
    ve = opn("vector", cols_v, wait=[("SE", P.ctr["SE"])], inc="VE")

    def cols_s(e):
        e.activation(kd_c[:, :], kd_c[:, :], AF.Exp)
        e.activation(bd_c[:, :], bd_c[:, :], AF.Exp)
        return e.activation(bcw[:, :], bcw[:, :], AF.Exp)
    tabs_se = opn("scalar", cols_s, wait=[("VE", ve)], inc="SE")

    xtr_ve = []
    n = 0
    for tc in range(TC):
        for k in range(KC):
            half = n % 2
            w = [("LG", x_ld), ("LD", consts_ld)] if n == 0 else []
            if n >= 2:
                w = [("VE", xtr_ve[n - 2])]
            mmv = opn("tensor", lambda e, tc=tc, k=k, half=half: e.transpose(
                (ptr if half == 0 else ptr2)[:, 0:128],
                xtm[:, tc * D + k * 128: tc * D + (k + 1) * 128], idb[:, :]),
                wait=w, inc="MM")
            if n % 8 == 7:
                P.raw("tensor", lambda e: e.drain())
            xtr_ve.append(opn("vector", lambda e, tc=tc, k=k, half=half: e.tensor_copy(
                xfm[:, k * T + tc * 128: k * T + (tc + 1) * 128],
                (ptr if half == 0 else ptr2)[:, 0:128]),
                wait=[("MM", mmv)], inc="VE"))
            n += 1
    xfm_ve = P.ctr["VE"]
    dbg("xfm", lambda: xfm[:, :], [128, KC * T], BF16)

    if upto >= 1:
        qk_mm = []
        qk_ve = []
        qk_ld = []

        def reg_qk_ld(i):
            w = []
            if i >= 2:
                w = [("MM", qk_mm[(i - 2) * 2 + 1])]
            qk_ld.append(opn("gpsimd", lambda e, i=i: e.dma_start(
                out=wqslab[i % 2][:, :].rearrange("p (kc fc) -> p kc fc", kc=KC),
                in_=wqkv.rearrange("(kc p) f -> p kc f", p=128)[:, :, i * 128:(i + 1) * 128]),
                wait=w, inc="LG", incn=16))

        reg_qk_ld(0)
        reg_qk_ld(1)
        gidx = 0
        for i in range(32):
            for th in range(2):
                bank = gidx % 4
                w = [("LG", qk_ld[i])]
                if gidx == 0:
                    w.append(("VE", xfm_ve))
                if gidx >= 4:
                    w.append(("VE", qk_ve[gidx - 4]))

                def mmf(e, i=i, th=th, bank=bank):
                    r = None
                    for k in range(KC):
                        r = e.matmul(pf[bank][:, :],
                                     wqslab[i % 2][:, k * 128:(k + 1) * 128],
                                     xfm[:, k * T + th * 512: k * T + th * 512 + 512],
                                     start=(k == 0), stop=(k == KC - 1))
                    return r
                qk_mm.append(opn("tensor", mmf, wait=w, inc="MM"))

                def evf(e, i=i, th=th, bank=bank):
                    dst = qfm if i < 16 else kfm
                    h = i % 16
                    return e.tensor_scalar(
                        dst[:, h * T + th * 512: h * T + th * 512 + 512],
                        pf[bank][:, :], bqc[:, i:i + 1], None, ALU.add)
                qk_ve.append(opn("vector", evf, wait=[("MM", qk_mm[-1])], inc="VE"))
                gidx += 1
            if i + 2 < 32:
                reg_qk_ld(i + 2)

        v_ld = []
        v_mm = []
        v_ve = []

        def reg_v_ld(fg):
            w = []
            if fg == 0:
                w = [("VE", xfm_ve)]
            if fg >= 2:
                w = [("MM", v_mm[(fg - 2) * TC + TC - 1])]
            v_ld.append(opn("gpsimd", lambda e, fg=fg: e.dma_start(
                out=vslab[fg % 2][:, :].rearrange("p (kc fc) -> p kc fc", kc=KC),
                in_=wqkv.rearrange("(kc p) f -> p kc f", p=128)[
                    :, :, 2 * D + fg * 512: 2 * D + (fg + 1) * 512]),
                wait=w, inc="LG", incn=16))

        reg_v_ld(0)
        reg_v_ld(1)
        gidx = 0
        for fg in range(4):
            for tc in range(TC):
                bank = gidx % 4
                w = [("LG", v_ld[fg]), ("SE", tabs_se)]
                if gidx == 0:
                    w.append(("VE", qk_ve[-1]))
                if gidx >= 4:
                    w.append(("VE", v_ve[gidx - 4]))

                def mmv(e, fg=fg, tc=tc, bank=bank):
                    for k in range(KC):
                        e.matmul(pf[bank][:, :],
                                 xfm[:, k * T + tc * 128:k * T + (tc + 1) * 128],
                                 vslab[fg % 2][:, k * 512:(k + 1) * 512],
                                 start=(k == 0), stop=False)
                    return e.matmul(pf[bank][:, :], onesb[0:1, 0:128],
                                    bq_vr[0:1, fg * 512:(fg + 1) * 512],
                                    start=False, stop=True)
                v_mm.append(opn("tensor", mmv, wait=w, inc="MM"))

                def evv(e, fg=fg, tc=tc, bank=bank):
                    src = pf[bank][:, :].rearrange("p (j e) -> p j e", j=4)
                    dst = vtm[:, :].rearrange("p (h c e) -> p h c e", h=H, c=NCH)[
                        :, fg * 4:fg * 4 + 4, tc, :]
                    return e.tensor_copy(dst, src)
                v_ve.append(opn("vector", evv, wait=[("MM", v_mm[-1])], inc="VE"))
                gidx += 1
            if fg + 2 < 4:
                reg_v_ld(fg + 2)

        ktr_ve = []
        n = 0
        for h in range(H):
            for c in range(NCH):
                half = n % 2
                w = []
                if n >= 2:
                    w = [("VE", ktr_ve[n - 2])]
                mmv = opn("tensor", lambda e, h=h, c=c, half=half: e.transpose(
                    (ptr if half == 0 else ptr2)[:, 0:128],
                    kfm[:, h * T + c * 128: h * T + (c + 1) * 128], idb[:, :]),
                    wait=w, inc="MM")
                if n % 8 == 7:
                    P.raw("tensor", lambda e: e.drain())
                ktr_ve.append(opn("vector", lambda e, h=h, c=c, half=half:
                                  e.tensor_scalar(
                                      kdt[:, (h * NCH + c) * 128:(h * NCH + c + 1) * 128],
                                      (ptr if half == 0 else ptr2)[:, 0:128],
                                      kd_c[:, h:h + 1], None, ALU.mult),
                                  wait=[("MM", mmv)], inc="VE"))
                n += 1
        kdt_ve = P.ctr["VE"]
        last_v_mm = v_mm[-1]
        dbg("qfm", lambda: qfm[:, :], [128, H * 1024], BF16)
        dbg("kfm", lambda: kfm[:, :], [128, H * 1024], BF16)
        dbg("vtm", lambda: vtm[:, :], [128, H * 1024], BF16)
        dbg("kdt", lambda: kdt[:, :], [128, H * 1024], BF16)

    if upto >= 2:
        sz_ve = opn("vector", lambda e: e.memset(sbf[:, :], 0.0), inc="VE")
        p1_mm = []
        p1_ve = []
        nidx = 0
        p1_order = [(hp + dh, c) for hp in range(0, H, 2) for c in range(NCH)
                    for dh in (0, 1)]
        for (h, c) in p1_order:
            if True:
                bank = 4 + (nidx % 2)
                w = [("VE", kdt_ve)] if nidx == 0 else []
                if nidx >= 2:
                    w.append(("VE", p1_ve[nidx - 2]))

                def smm(e, h=h, c=c, bank=bank):
                    i = h * NCH + c
                    return e.matmul(pf[bank][:, 0:128],
                                    kdt[:, i * 128:(i + 1) * 128],
                                    vtm[:, i * 128:(i + 1) * 128],
                                    start=True, stop=True)
                p1_mm.append(opn("tensor", smm, wait=w, inc="MM"))
                if nidx % 4 == 3:
                    P.raw("tensor", lambda e: e.drain())

                def sstt(e, h=h, bank=bank):
                    return e.scalar_tensor_tensor(
                        sbf[:, h * 128:(h + 1) * 128], sbf[:, h * 128:(h + 1) * 128],
                        bd_c[:, h:h + 1], pf[bank][:, 0:128], ALU.mult, ALU.add)
                p1_ve.append(opn("vector", sstt,
                                 wait=[("MM", p1_mm[-1]), ("VE", sz_ve)], inc="VE"))
                nidx += 1
        p1_done_ve = P.ctr["VE"]

        if subph < 2:
            upto = 0
        texp = None
        if subph >= 2:
            texp = opn("sync", lambda e: e.dma_start(out=agt_in[:, :], in_=sbf[:, :]),
                       wait=[("VE", p1_done_ve)], inc="LD", incn=16)
        agT = None
        if subph >= 2:
            agT = opn("gpsimd", lambda e: e.collective_compute(
                "AllGather", ALU.bypass, replica_groups=RG,
                ins=[agt_in.opt()], outs=[agt_out.opt()]),
                wait=[("LD", texp)], inc="CC")

        if subph < 3:
            upto = 0
        wsh_ld = []
        if subph >= 3:
            wsh_ve = []
            for i in range(24):
                if i < 2:
                    w = [("MM", last_v_mm)]
                else:
                    w = [("VE", wsh_ve[i - 2])]

                def lsh(e, i=i, buf=(wsh_f32, wsh_f32b)[i % 2]):
                    if i < 8:
                        return e.dma_start(out=buf[:, :],
                                           in_=wfc_sh[i * 128:(i + 1) * 128, :])
                    j = i - 8
                    return e.dma_start(out=buf[:, 0:1024],
                                       in_=wpr_sh[j * 128:(j + 1) * 128, :])
                wsh_ld.append(opn("scalar", lsh, wait=w, inc="LS", incn=16))

                def msh(e, i=i, buf=(wsh_f32, wsh_f32b)[i % 2]):
                    width = D if i < 8 else 1024
                    col = 8 if i < 8 else 9
                    e.tensor_reduce(mcol[:, :], buf[:, 0:width], AX.X, ALU.max,
                                    apply_absolute_value=True)
                    e.drain()
                    if i == 0 or i == 8:
                        e.tensor_copy(scal[:, col:col + 1], mcol[:, :])
                    else:
                        r = e.tensor_tensor(scal[:, col:col + 1],
                                            scal[:, col:col + 1], mcol[:, :],
                                            ALU.max)
                    e.drain()
                    return e.tensor_copy(dmy[:, :], bnag[:, 0:1])
                wsh_ve.append(opn("vector", msh, wait=[("LS", wsh_ld[-1])], inc="VE"))

            def gmax_f(e):
                e.tensor_reduce(scal[0:1, 8:9], scal[:, 8:9], AX.C, ALU.max)
                return e.tensor_reduce(scal[0:1, 9:10], scal[:, 9:10], AX.C, ALU.max)
            gmax = opn("gpsimd", gmax_f, wait=[("VE", wsh_ve[-1])], inc="GE")

            def stgw_f(e):
                e.dma_start(out=arw_in[0:1, 0:1], in_=scal[0:1, 8:9]).then_inc(
                    P.sems["LG"], 16)
                return e.dma_start(out=arw_in[0:1, 1:2], in_=scal[0:1, 9:10])
            opn("gpsimd", stgw_f, wait=[("GE", gmax)], inc="LG", incn=16)
            P.ctr["LG"] += 16
            stgw = P.ctr["LG"]
            arw = opn("gpsimd", lambda e: e.collective_compute(
                "AllReduce", ALU.max, replica_groups=RG,
                ins=[arw_in.opt()], outs=[arw_out.opt()]),
                wait=[("LG", stgw)], inc="CC")

            def ldw_f(e):
                e.dma_start(out=scal[:, 14:15],
                            in_=arw_out[0:1, 0:1].broadcast_to((128, 1))).then_inc(
                    P.sems["LG"], 16)
                return e.dma_start(out=scal[:, 15:16],
                                   in_=arw_out[0:1, 1:2].broadcast_to((128, 1)))
            opn("gpsimd", ldw_f, wait=[("CC", arw)], inc="LG", incn=16)
            P.ctr["LG"] += 16
            ldw = P.ctr["LG"]

            def wscal(e):
                e.tensor_scalar(scal[:, 5:7], scal[:, 14:16], 1.0 / 448.0, 1e-6,
                                ALU.mult, ALU.max)
                e.drain()
                e.reciprocal(scal[:, 0:2], scal[:, 5:7])
                e.drain()
                e.tensor_scalar(scal[:, 0:2], scal[:, 0:2], 0.5, None, ALU.mult)
                e.drain()
                return e.tensor_copy(dmy[:, :], bnag[:, 0:1])
            bc_ve = opn("vector", wscal, wait=[("LG", ldw)], inc="VE")


            if subph < 6:
                upto = 0
            if subph >= 6:
                comb_last = None
                for r in range(NCORES):
                    wl = [("CC", agT)]
                    if r == 0:
                        wl.append(("VE", wsh_ve[-1]))
                    else:
                        wl.append(("VE", comb_last))
                    lt = opn("sync", lambda e, r=r: e.dma_start(
                        out=tjin[:, :], in_=agt_out[r * 128:(r + 1) * 128, :]),
                        wait=wl, inc="LD", incn=16)
                    for h in range(H):
                        def cmb(e, r=r, h=h):
                            if r == 0:
                                return e.tensor_scalar(
                                    sini[:, h * 128:(h + 1) * 128],
                                    tjin[:, h * 128:(h + 1) * 128],
                                    bcw[:, h * 8 + r:h * 8 + r + 1], None, ALU.mult)
                            return e.scalar_tensor_tensor(
                                sini[:, h * 128:(h + 1) * 128],
                                tjin[:, h * 128:(h + 1) * 128],
                                bcw[:, h * 8 + r:h * 8 + r + 1],
                                sini[:, h * 128:(h + 1) * 128], ALU.mult, ALU.add)
                        comb_last = opn("vector", cmb,
                                        wait=[("LD", lt)] if h == 0 else [], inc="VE")
                sinit_ve = opn("vector", lambda e: e.tensor_copy(sbf[:, :], sini[:, :]),
                               inc="VE")
                dbg("sinit", lambda: sbf[:, :], [128, H * 128], BF16)

                half_ev = {0: None, 1: None}
                first_q = True
                for i in range(8):
                    for hh in range(2):
                        w = []
                        if first_q:
                            w = [("VE", sinit_ve)]
                            first_q = False
                        elif half_ev[hh] is not None:
                            w = [("VE", half_ev[hh])]
                        ldq = opn("scalar", lambda e, i=i, hh=hh: e.dma_start(
                            out=wsh_f32[:, hh * 1024:(hh + 1) * 1024],
                            in_=wfc_sh[i * 128:(i + 1) * 128,
                                       hh * 1024:(hh + 1) * 1024]),
                            wait=w, inc="LS", incn=16)
                        qv = opn("vector", lambda e, hh=hh: e.tensor_scalar(
                            wsh_f8[:, hh * 1024:(hh + 1) * 1024],
                            wsh_f32[:, hh * 1024:(hh + 1) * 1024],
                            scal[:, 0:1], None, ALU.mult),
                            wait=[("LS", ldq)], inc="VE")
                        prev_tr = []
                        for kk in range(8):
                            k = hh * 8 + kk
                            halfp = kk % 2
                            w2 = [("VE", qv)]
                            if kk >= 2:
                                w2.append(("VE", prev_tr[kk - 2]))
                            tmm = opn("tensor", lambda e, k=k, halfp=halfp: e.transpose(
                                (ptr8 if halfp == 0 else ptr28)[:, 0:256:2],
                                wsh_f8[:, k * 128:(k + 1) * 128], id8[:, :]),
                                wait=w2, inc="MM")
                            if kk == 7:
                                P.raw("tensor", lambda e: e.drain())
                            prev_tr.append(opn("vector",
                                               lambda e, k=k, halfp=halfp, i=i:
                                               e.tensor_copy(
                                                   wsh_stg[:, k * 1024 + i * 128:
                                                           k * 1024 + (i + 1) * 128],
                                                   (ptr8 if halfp == 0 else ptr28)[:, 0:256:2]),
                                               wait=[("MM", tmm)], inc="VE"))
                        half_ev[hh] = prev_tr[-1]
                stg_f = opn("sync", lambda e: e.dma_start(
                    out=w8f_in.rearrange("(kc p) u -> p kc u", p=128),
                    in_=wsh_stg[:, :].rearrange("p (kc u) -> p kc u", kc=KC)),
                    wait=[("VE", P.ctr["VE"])], inc="LD", incn=16)
                agF = opn("gpsimd", lambda e: e.collective_compute(
                    "AllGather", ALU.bypass, replica_groups=RG,
                    ins=[w8f_in.opt()], outs=[w8f_out.opt()]),
                    wait=[("LD", stg_f), ("CC", agT)], inc="CC")

                for i in range(16):
                    w = [("LD", stg_f)] if i == 0 else [("VE", P.ctr["VE"])]
                    ldq = opn("scalar", lambda e, i=i: e.dma_start(
                        out=wsh_f32[:, 0:1024], in_=wpr_sh[i * 128:(i + 1) * 128, :]),
                        wait=w, inc="LS", incn=16)
                    qv = opn("vector", lambda e: e.tensor_scalar(
                        wsh_f8[:, 0:1024], wsh_f32[:, 0:1024], scal[:, 1:2], None, ALU.mult),
                        wait=[("LS", ldq)], inc="VE")
                    prev_tr = []
                    for k in range(8):
                        half = k % 2
                        w2 = [("VE", qv)]
                        if k >= 2:
                            w2.append(("VE", prev_tr[k - 2]))
                        tmm = opn("tensor", lambda e, k=k, half=half: e.transpose(
                            (ptr8 if half == 0 else ptr28)[:, 0:256:2],
                            wsh_f8[:, k * 128:(k + 1) * 128], id8[:, :]),
                            wait=w2, inc="MM")
                        prev_tr.append(opn("vector", lambda e, k=k, half=half, i=i:
                                           e.tensor_copy(
                                               wsh_stg[:, k * 2048 + i * 128:
                                                       k * 2048 + (i + 1) * 128],
                                               (ptr8 if half == 0 else ptr28)[:, 0:256:2]),
                                           wait=[("MM", tmm)], inc="VE"))
                    P.raw("tensor", lambda e: e.drain())
                stg_p = opn("sync", lambda e: e.dma_start(
                    out=w8p_in.rearrange("(kc p) o -> p kc o", p=128),
                    in_=wsh_stg[:, :].rearrange("p (kc o) -> p kc o", kc=8)),
                    wait=[("VE", P.ctr["VE"])], inc="LD", incn=16)
                agP = opn("gpsimd", lambda e: e.collective_compute(
                    "AllGather", ALU.bypass, replica_groups=RG,
                    ins=[w8p_in.opt()], outs=[w8p_out.opt()]),
                    wait=[("LD", stg_p), ("CC", agF)], inc="CC")
    dbg("bcw", lambda: bcw[:, :], [128, 128], F32)
    dbg("kdc", lambda: kd_c[:, :], [128, H], F32)

    if upto >= 3:
        p2_sco, p2_stv, p2_qdv, p2_omm, p2_umm, p2_oev, p2_stt = ([] for _ in range(7))
        nidx = 0
        p2_order = [(hp + dh, c) for hp in range(0, H, 2) for c in range(NCH)
                    for dh in (0, 1)]
        for (h, c) in p2_order:
            if True:
                p = nidx % 2
                A, B, Cb = pf[0 + p], pf[2 + p], pf[4 + p]
                i = h * NCH + c
                w = [("VE", sinit_ve)] if nidx == 0 else []
                if nidx >= 2:
                    w.append(("VE", p2_stv[nidx - 2]))
                p2_sco.append(opn("tensor", lambda e, i=i, A=A: e.matmul(
                    A[:, 0:128], kfm[:, i * 128:(i + 1) * 128],
                    qfm[:, i * 128:(i + 1) * 128], start=True, stop=True),
                    wait=w, inc="MM"))

                w2 = [("MM", p2_sco[-1])]
                if nidx >= 2:
                    w2.append(("MM", p2_omm[nidx - 2]))
                p2_stv.append(opn("vector", lambda e, h=h, p=p, A=A:
                                  e.tensor_tensor(st_s[:, p * 128:(p + 1) * 128],
                                                  A[:, 0:128],
                                                  din_t[:, h * 128:(h + 1) * 128],
                                                  ALU.mult), wait=w2, inc="VE"))
                w3 = []
                if nidx >= 2:
                    w3.append(("MM", p2_omm[nidx - 2]))
                p2_qdv.append(opn("vector", lambda e, i=i, h=h, p=p:
                                  e.tensor_tensor(qd_s[:, p * 128:(p + 1) * 128],
                                                  qfm[:, i * 128:(i + 1) * 128],
                                                  dq_t[:, h * 128:(h + 1) * 128],
                                                  ALU.mult), wait=w3, inc="VE"))
                w4 = [("VE", p2_qdv[-1])]
                if nidx >= 2:
                    w4.append(("SE", p2_oev[nidx - 2]))
                    w4.append(("VE", p2_stt[nidx - 2]))

                def mo(e, i=i, h=h, p=p, B=B):
                    e.matmul(B[:, 0:128], vtm[:, i * 128:(i + 1) * 128],
                             st_s[:, p * 128:(p + 1) * 128], start=True, stop=False)
                    return e.matmul(B[:, 0:128], sbf[:, h * 128:(h + 1) * 128],
                                    qd_s[:, p * 128:(p + 1) * 128],
                                    start=False, stop=True)
                p2_omm.append(opn("tensor", mo, wait=w4, inc="MM"))

                w5 = []
                if nidx >= 2:
                    w5.append(("VE", p2_stt[nidx - 2]))
                p2_umm.append(opn("tensor", lambda e, i=i, Cb=Cb: e.matmul(
                    Cb[:, 0:128], kdt[:, i * 128:(i + 1) * 128],
                    vtm[:, i * 128:(i + 1) * 128], start=True, stop=True),
                    wait=w5, inc="MM"))
                if nidx % 2 == 1:
                    P.raw("tensor", lambda e: e.drain())
                w_oev = [("MM", p2_omm[-1])]
                if nidx == 32:
                    w_oev.append(("LD", stg_p))
                p2_oev.append(opn("scalar", lambda e, i=i, B=B: e.activation(
                    attnt[:, i * 128:(i + 1) * 128], B[:, 0:128], AF.Copy),
                    wait=w_oev, inc="SE"))
                p2_stt.append(opn("vector", lambda e, h=h, Cb=Cb:
                                  e.scalar_tensor_tensor(
                                      sbf[:, h * 128:(h + 1) * 128],
                                      sbf[:, h * 128:(h + 1) * 128],
                                      bd_c[:, h:h + 1], Cb[:, 0:128],
                                      ALU.mult, ALU.add),
                                  wait=[("MM", p2_umm[-1])], inc="VE"))
                nidx += 1
        pass2_se = P.ctr["SE"]
        pass2_ve = P.ctr["VE"]
        dbg("attnt", lambda: attnt[:, :], [128, H * 1024], BF16)

    if upto >= 4:
        op_mm = []
        op_ve = []
        wo_ld = []
        xr_ld = []

        def reg_wo_ld(og):
            w = [("VE", pass2_ve), ("SE", pass2_se)]
            if og == 0:
                w = [("MM", p2_sco[63]), ("VE", p2_qdv[63])]
            if og >= 2:
                w = [("MM", op_mm[(og - 2) * TC * 2 + TC * 2 - 1])]
            evs = []
            for hh in range(2):
                evs.append(opn("gpsimd", lambda e, og=og, hh=hh: e.dma_start(
                    out=woslab[og % 2][:, hh * 8 * 512:(hh + 1) * 8 * 512].rearrange(
                        "p (kc oc) -> p kc oc", kc=8),
                    in_=wout.rearrange("(kc p) o -> p kc o", p=128)[
                        :, hh * 8:(hh + 1) * 8, og * 512:(og + 1) * 512]),
                    wait=w if hh == 0 else [], inc="LG", incn=16))
            wo_ld.append(evs)

        reg_wo_ld(0)
        reg_wo_ld(1)
        gidx = 0
        ln_last = None
        for og in range(4):
            for tc in range(TC):
                bank = gidx % 4
                wxl = []
                if gidx >= 2:
                    wxl = [("VE", op_ve[gidx - 2])]
                elif og == 0 and tc == 0:
                    wxl = [("SE", pass2_se)]
                xr_ld.append(opn("sync", lambda e, og=og, tc=tc, par=gidx % 2:
                                 e.dma_start(out=xres[par][:, :],
                                             in_=xin[tc * 128:(tc + 1) * 128,
                                                     og * 512:(og + 1) * 512]),
                                 wait=wxl, inc="LD", incn=16))

                w = [("LG", wo_ld[og][0])]
                if gidx == 0:
                    w += [("VE", pass2_ve), ("SE", pass2_se)]
                if gidx >= 4:
                    w.append(("VE", op_ve[gidx - 4]))

                def mop0(e, tc=tc, og=og, bank=bank):
                    r = None
                    for k in range(8):
                        r = e.matmul(pf[bank][:, :],
                                     attnt[:, k * T + tc * 128:k * T + (tc + 1) * 128],
                                     woslab[og % 2][:, k * 512:(k + 1) * 512],
                                     start=(k == 0), stop=False)
                    return r
                op_mm.append(opn("tensor", mop0, wait=w, inc="MM"))

                def mop1(e, tc=tc, og=og, bank=bank):
                    for k in range(8, KC):
                        e.matmul(pf[bank][:, :],
                                 attnt[:, k * T + tc * 128:k * T + (tc + 1) * 128],
                                 woslab[og % 2][:, k * 512:(k + 1) * 512],
                                 start=False, stop=False)
                    return e.matmul(pf[bank][:, :], onesb[0:1, 0:128],
                                    bo_r[0:1, og * 512:(og + 1) * 512],
                                    start=False, stop=True)
                op_mm.append(opn("tensor", mop1,
                                 wait=[("LG", wo_ld[og][1]), ("LD", xr_ld[-1])],
                                 inc="MM"))

                def vev(e, tc=tc, og=og, bank=bank, par=gidx % 2):
                    return e.tensor_tensor(
                        x1[:, tc * D + og * 512: tc * D + (og + 1) * 512],
                        pf[bank][:, :], xres[par][:, :], ALU.add)
                op_ve.append(opn("vector", vev, wait=[("MM", op_mm[-1])], inc="VE"))
                gidx += 1

                if og == 3:
                    def ln1(e, tc=tc):
                        xt = x1[:, tc * D:(tc + 1) * D]
                        e.tensor_reduce(bnag[:, 0:1], xt, AX.X, ALU.add)
                        e.drain()
                        e.tensor_scalar(bnag[:, 0:1], bnag[:, 0:1], 1.0 / D, None,
                                        ALU.mult)
                        e.drain()
                        e.tensor_scalar(x2t[0][:, :], xt, bnag[:, 0:1], None,
                                        ALU.subtract)
                        e.tensor_tensor(x2t[0][:, :], x2t[0][:, :], x2t[0][:, :],
                                        ALU.mult)
                        e.tensor_reduce(bnag[:, 1:2], x2t[0][:, :], AX.X, ALU.add)
                        e.drain()
                        r = e.tensor_scalar(rstd[:, :], bnag[:, 1:2], 1.0 / D, EPS,
                                            ALU.mult, ALU.add)
                        e.drain()
                        return e.tensor_copy(dmy[:, :], bnag[:, 0:1])
                    lv = opn("vector", ln1, wait=[("VE", op_ve[-1])], inc="VE")
                    sq = opn("scalar", lambda e: e.activation(rstd[:, :], rstd[:, :],
                                                              AF.Sqrt),
                             wait=[("VE", lv)], inc="SE")
                    rv = opn("vector", lambda e: (e.reciprocal(rstd[:, :], rstd[:, :]),
                                                  e.drain())[0],
                             wait=[("SE", sq)], inc="VE")

                    def nrm(e, tc=tc):
                        e.drain()
                        e.tensor_scalar(x1[:, tc * D:(tc + 1) * D],
                                        x1[:, tc * D:(tc + 1) * D],
                                        bnag[:, 0:1], rstd[:, :], ALU.subtract,
                                        ALU.mult)
                        e.tensor_reduce(mcol[:, :], x1[:, tc * D:(tc + 1) * D], AX.X,
                                        ALU.max, apply_absolute_value=True)
                        e.drain()
                        if tc == 0:
                            e.tensor_copy(scal[:, 10:11], mcol[:, :])
                        else:
                            r = e.tensor_tensor(scal[:, 10:11], scal[:, 10:11],
                                                mcol[:, :], ALU.max)
                        e.drain()
                        return e.tensor_copy(dmy[:, :], bnag[:, 0:1])
                    ln_last = opn("vector", nrm, wait=[("VE", rv)], inc="VE")
            if og + 2 < 4:
                reg_wo_ld(og + 2)
        opj_ve = P.ctr["VE"]
        ln1_ve = P.ctr["VE"]

        dbg("x1", lambda: x1[:, :], [128, TC * D], F32)

        gx = opn("gpsimd", lambda e: e.tensor_reduce(
            scal[0:1, 10:11], scal[:, 10:11], AX.C, ALU.max),
            wait=[("VE", ln1_ve)], inc="GE")
        sgx = opn("gpsimd", lambda e: e.dma_start(out=arx_in[0:1, 0:1],
                                                  in_=scal[0:1, 10:11]),
                  wait=[("GE", gx)], inc="LG", incn=16)
        arx = opn("gpsimd", lambda e: e.collective_compute(
            "AllReduce", ALU.max, replica_groups=RG,
            ins=[arx_in.opt()], outs=[arx_out.opt()]),
            wait=[("LG", sgx), ("CC", agP)], inc="CC")
        ldx = opn("gpsimd", lambda e: e.dma_start(
            out=scal[:, 14:15], in_=arx_out[0:1, 0:1].broadcast_to((128, 1))),
                  wait=[("CC", arx)], inc="LG", incn=16)

        def xscal(e):
            e.tensor_scalar(scal[:, 7:8], scal[:, 14:15], 1.0 / 448.0, 1e-6,
                            ALU.mult, ALU.max)
            e.drain()
            e.reciprocal(scal[:, 2:3], scal[:, 7:8])
            e.drain()
            e.tensor_scalar(scal[:, 2:3], scal[:, 2:3], 0.5, None, ALU.mult)
            e.tensor_tensor(scal[:, 13:14], scal[:, 7:8], scal[:, 5:6], ALU.mult)
            e.drain()
            e.tensor_scalar(scal[:, 13:14], scal[:, 13:14], 4.0, None,
                                ALU.mult)
            e.drain()
            return e.tensor_copy(dmy[:, :], bnag[:, 0:1])
        bcx_ve = opn("vector", xscal, wait=[("LG", ldx)], inc="VE")

    if upto >= 5:
        x8_ve = []
        n = 0
        for tc in range(TC):
            wq = [("VE", bcx_ve)] if tc == 0 else [("VE", P.ctr["VE"])]
            qv = opn("vector", lambda e, tc=tc: e.tensor_scalar(
                qtm8[:, :], x1[:, tc * D:(tc + 1) * D], scal[:, 2:3], None, ALU.mult),
                wait=wq, inc="VE")
            for k in range(KC):
                half = n % 2
                w = [("VE", qv)]
                if n >= 2:
                    w.append(("VE", x8_ve[n - 2]))
                tmm = opn("tensor", lambda e, k=k, half=half: e.transpose(
                    (ptr8 if half == 0 else ptr28)[:, 0:256:2],
                    qtm8[:, k * 128:(k + 1) * 128], id8[:, :]),
                    wait=w, inc="MM")
                if n % 8 == 7:
                    P.raw("tensor", lambda e: e.drain())
                x8_ve.append(opn("vector", lambda e, k=k, tc=tc, half=half:
                                 e.tensor_copy(
                                     x8fm[:, k * T + tc * 128:k * T + (tc + 1) * 128],
                                     (ptr8 if half == 0 else ptr28)[:, 0:256:2]),
                                 wait=[("MM", tmm)], inc="VE"))
                n += 1
        x8_done = P.ctr["VE"]
        dbg("x8fm", lambda: x8fm[:, :], [128, KC * T], FP8)

    if upto >= 6:
        fc_ld, fc_mm, fc_se, fc_st, fc_vmax = [], [], [], [], []

        def reg_fc_ld(ug):
            w = [("CC", agF)]
            if ug == 0:
                w += [("MM", op_mm[-1]), ("SE", pass2_se)]
            if ug >= 2:
                w.append(("MM", fc_mm[(ug - 2) * 8 + 7]))

            def lfc(e, ug=ug):
                r, lh = ug // 2, (ug % 2) * 512
                src = w8f_out.rearrange("(r kc p) u -> p r kc u", r=NCORES, kc=KC)[
                    :, r, :, lh:lh + 512]
                return e.dma_start(
                    out=w8slab[ug % 2][:, :].rearrange("p (kc u) -> p kc u", kc=KC),
                    in_=src)
            fc_ld.append((("sync", "LD") if ug % 2 == 0 else ("scalar", "LS"),
                          opn("sync" if ug % 2 == 0 else "scalar", lfc, wait=w,
                              inc="LD" if ug % 2 == 0 else "LS", incn=16)))

        reg_fc_ld(0)
        reg_fc_ld(1)
        for ug in range(16):
            for uc4 in range(4):
                for th in range(2):
                    gi = len(fc_mm)
                    bank = gi % 4
                    w2 = [(fc_ld[ug][0][1], fc_ld[ug][1]), ("VE", x8_done)]
                    if gi >= 4:
                        w2.append(("SE", fc_se[gi - 4]))

                    def mfc(e, ug=ug, uc4=uc4, th=th, bank=bank):
                        r = None
                        for k in range(KC):
                            r = e.matmul(pf[bank][:, :],
                                         w8slab[ug % 2][:, k * 512 + uc4 * 128:
                                                        k * 512 + (uc4 + 1) * 128],
                                         x8fm[:, k * T + th * 512:
                                              k * T + th * 512 + 512],
                                         start=(k == 0), stop=(k == KC - 1))
                        return r
                    fc_mm.append(opn("tensor", mfc, wait=w2, inc="MM"))

                    w3 = [("MM", fc_mm[-1])]
                    if gi >= 2:
                        w3 += [("ST", fc_st[gi - 2]), ("VE", fc_vmax[gi - 2])]
                    fc_se.append(opn("scalar", lambda e, bank=bank, gi=gi:
                                     e.activation(rstg[gi % 2][:, :], pf[bank][:, :],
                                                  AF.Relu), wait=w3, inc="SE"))

                    def vmx(e, gi=gi):
                        e.tensor_reduce(mcol[:, :], rstg[gi % 2][:, :], AX.X,
                                        ALU.max, apply_absolute_value=True)
                        e.drain()
                        if gi == 0:
                            r = e.tensor_copy(scal[:, 11:12], mcol[:, :])
                        else:
                            r = e.tensor_tensor(scal[:, 11:12], scal[:, 11:12],
                                                mcol[:, :], ALU.max)
                        e.drain()
                        return e.tensor_copy(dmy[:, :], bnag[:, 0:1])
                    fc_vmax.append(opn("vector", vmx, wait=[("SE", fc_se[-1])],
                                       inc="VE"))
                    uc = ug * 4 + uc4
                    fc_st.append(opn("sync", lambda e, uc=uc, th=th, gi=gi:
                                     e.dma_start(out=gdram[uc, :, th * 512:(th + 1) * 512],
                                                 in_=rstg[gi % 2][:, :]),
                                     wait=[("SE", fc_se[-1])], inc="ST", incn=16))
            if ug + 2 < 16:
                reg_fc_ld(ug + 2)
        fc_done_st = P.ctr["ST"]
        fc_done_mm = fc_mm[-1]

        gh = opn("gpsimd", lambda e: e.tensor_reduce(
            scal[0:1, 11:12], scal[:, 11:12], AX.C, ALU.max),
            wait=[("VE", P.ctr["VE"])], inc="GE")
        sgh = opn("gpsimd", lambda e: e.dma_start(out=arh_in[0:1, 0:1],
                                                  in_=scal[0:1, 11:12]),
                  wait=[("GE", gh)], inc="LG", incn=16)
        arh = opn("gpsimd", lambda e: e.collective_compute(
            "AllReduce", ALU.max, replica_groups=RG,
            ins=[arh_in.opt()], outs=[arh_out.opt()]),
            wait=[("LG", sgh)], inc="CC")
        ldh = opn("gpsimd", lambda e: e.dma_start(
            out=scal[:, 14:15], in_=arh_out[0:1, 0:1].broadcast_to((128, 1))),
            wait=[("CC", arh)], inc="LG", incn=16)

        def hscal(e):
            e.tensor_tensor(scal[:, 15:16], scal[:, 13:14], scal[:, 14:15], ALU.mult)
            e.drain()
            e.tensor_tensor(scal[:, 15:16], scal[:, 15:16], scal[:, 15:16], ALU.mult)
            e.drain()
            e.tensor_scalar(scal[:, 12:13], scal[:, 15:16], 1.0 / 448.0, 1e-6,
                            ALU.mult, ALU.max)
            e.tensor_tensor(scal[:, 15:16], scal[:, 13:14], scal[:, 13:14], ALU.mult)
            e.drain()
            e.reciprocal(scal[:, 3:4], scal[:, 12:13])
            e.drain()
            e.tensor_tensor(scal[:, 3:4], scal[:, 3:4], scal[:, 15:16], ALU.mult)
            e.drain()
            e.tensor_scalar(scal[:, 3:4], scal[:, 3:4], 0.5, None, ALU.mult)
            e.tensor_tensor(scal[:, 4:5], scal[:, 12:13], scal[:, 6:7], ALU.mult)
            e.drain()
            e.tensor_scalar(scal[:, 4:5], scal[:, 4:5], 4.0, None, ALU.mult)
            e.drain()
            return e.tensor_copy(dmy[:, :], bnag[:, 0:1])
        bch_ve = opn("vector", hscal, wait=[("LG", ldh)], inc="VE")
        if "rg" in debug:
            rg_out = nc.dram_tensor("dbg_rg", [4 * 128, T], BF16,
                                    kind="ExternalOutput").ap()
            for uu, uc in enumerate((0, 1, 31, 63)):
                opn("sync", lambda e, uu=uu, uc=uc: e.dma_start(
                    out=rg_out[uu * 128:(uu + 1) * 128, :], in_=gdram[uc, :, :]),
                    wait=[("ST", fc_done_st)], inc="ST", incn=16)
        dbg("scal", lambda: scal[:, :], [128, 16], F32)

    if upto >= 7:
        prev_pair_mm = None
        prev2_pair_mm = None
        prev_gq_last = None
        prev_out_st = [None, None]
        gq_of = {}

        def prep(pr):
            gcur = g8ab[pr % 2]
            gq_ve = []
            for hfl in range(2):
                lds = []
                for blk in range(4):
                    w = []
                    if pr == 0 and hfl == 0 and blk == 0:
                        w = [("ST", fc_done_st), ("VE", bch_ve), ("MM", fc_done_mm)]
                    elif blk == 0:
                        if hfl == 1:
                            w = [("VE", gq_ve[-1])]
                        elif prev_gq_last is not None:
                            w = [("VE", prev_gq_last)]

                    def ldr(e, pr=pr, hfl=hfl, blk=blk):
                        u0 = hfl * 32 + blk * 8
                        src = gdram[u0:u0 + 8, :, pr * 256:(pr + 1) * 256].rearrange(
                            "u p c -> p u c")
                        dst = rload[:, blk * 8 * 256:(blk + 1) * 8 * 256].rearrange(
                            "p (u c) -> p u c", u=8)
                        return e.dma_start(out=dst, in_=src)
                    lds.append(opn("sync", ldr, wait=w, inc="LD", incn=16))

                def gq(e, hfl=hfl, gcur=gcur):
                    r = None
                    for uo in range(32):
                        uc = hfl * 32 + uo
                        r = e.scalar_tensor_tensor(
                            gcur[:, uc * 256:(uc + 1) * 256],
                            rload[:, uo * 256:(uo + 1) * 256], scal[:, 3:4],
                            rload[:, uo * 256:(uo + 1) * 256], ALU.mult, ALU.mult)
                    return r
                wq = [("LD", lds[-1])]
                if hfl == 0 and pr >= 2:
                    wq.append(("MM", pair_last_mm[pr - 2]))
                gq_ve.append(opn("vector", gq, wait=wq, inc="VE"))
            gq_of[pr] = gq_ve[-1]
            return gq_ve[-1]

        pair_last_mm = {}
        prev_gq_last = prep(0)
        for pr in range(4):
            gcur = g8ab[pr % 2]
            gq_done = gq_of[pr]
            pr_mm = []
            pr_ev = []
            for og in range(4):
                for us in range(8):
                    if len(pr_mm) >= 2:
                        wl = [("MM", pr_mm[-2])]
                    elif pr > 0:
                        wl = [("MM", pair_last_mm[pr - 1])]
                    else:
                        wl = [("VE", gq_done)]
                    qeng = "sync" if us % 2 == 0 else "scalar"
                    qsem = "LD" if us % 2 == 0 else "LS"
                    sli = opn(qeng, lambda e, og=og, us=us: e.dma_start(
                        out=w8psl[us % 2][:, :].rearrange("p (u o) -> p u o", u=8),
                        in_=w8p_out.rearrange("(u p) o -> p u o", p=128)[
                            :, us * 8:(us + 1) * 8, og * 512:(og + 1) * 512]),
                        wait=wl, inc=qsem, incn=16)

                    wm = [(qsem, sli), ("VE", gq_done)]
                    if og >= 2 and us == 0:
                        wm.append(("VE", pr_ev[(og - 2) * 2 + 1]))

                    def mpr(e, og=og, us=us, gcur=gcur):
                        r = None
                        for j in range(8):
                            uc = us * 8 + j
                            for half in range(2):
                                r = e.matmul(pf[(og % 2) * 2 + half][:, :],
                                             gcur[:, uc * 256 + half * 128:
                                                  uc * 256 + half * 128 + 128],
                                             w8psl[us % 2][:, j * 512:(j + 1) * 512],
                                             start=(us == 0 and j == 0),
                                             stop=(us == 7 and j == 7))
                        return r
                    pr_mm.append(opn("tensor", mpr, wait=wm, inc="MM"))
                for half in range(2):
                    tc = pr * 2 + half

                    def pev(e, og=og, half=half, tc=tc):
                        return e.scalar_tensor_tensor(
                            x2t[half][:, og * 512:(og + 1) * 512],
                            pf[(og % 2) * 2 + half][:, :], scal[:, 4:5],
                            x1[:, tc * D + og * 512: tc * D + (og + 1) * 512],
                            ALU.mult, ALU.add)
                    wpe = [("MM", pr_mm[-1])]
                    if og == 0 and prev_out_st[half] is not None:
                        wpe.append(("ST", prev_out_st[half]))
                    pr_ev.append(opn("vector", pev, wait=wpe, inc="VE"))
                if og == 1 and pr + 1 < 4:
                    prev_gq_last = prep(pr + 1)
            pair_last_mm[pr] = pr_mm[-1]
            prev2_pair_mm = prev_pair_mm
            prev_pair_mm = pr_mm[-1]

            for half in range(2):
                tc = pr * 2 + half

                def ln2(e, half=half):
                    xt = x2t[half][:, :]
                    e.tensor_reduce(bnag[:, 0:1], xt, AX.X, ALU.add)
                    e.drain()
                    e.tensor_scalar(bnag[:, 0:1], bnag[:, 0:1], 1.0 / D, None,
                                    ALU.mult)
                    e.drain()
                    e.tensor_scalar(lnscr2[:, :], xt, bnag[:, 0:1], None,
                                    ALU.subtract)
                    e.tensor_tensor(lnscr2[:, :], lnscr2[:, :], lnscr2[:, :],
                                    ALU.mult)
                    e.tensor_reduce(bnag[:, 1:2], lnscr2[:, :], AX.X, ALU.add)
                    e.drain()
                    r = e.tensor_scalar(rstd[:, :], bnag[:, 1:2], 1.0 / D, EPS,
                                        ALU.mult, ALU.add)
                    e.drain()
                    return e.tensor_copy(dmy[:, :], bnag[:, 0:1])
                lv = opn("vector", ln2, wait=[("VE", pr_ev[-1])], inc="VE")
                sq2 = opn("scalar", lambda e: e.activation(rstd[:, :], rstd[:, :],
                                                           AF.Sqrt),
                          wait=[("VE", lv)], inc="SE")
                rv2 = opn("vector", lambda e: (e.reciprocal(rstd[:, :], rstd[:, :]),
                                               e.drain())[0],
                          wait=[("SE", sq2)], inc="VE")
                nv = opn("vector", lambda e, half=half: e.tensor_scalar(
                    x2t[half][:, :], x2t[half][:, :], bnag[:, 0:1], rstd[:, :],
                    ALU.subtract, ALU.mult), wait=[("VE", rv2)], inc="VE")
                prev_out_st[half] = opn("sync", lambda e, tc=tc, half=half:
                    e.dma_start(out=yout[tc * 128:(tc + 1) * 128, :],
                                in_=x2t[half][:, :]),
                    wait=[("VE", nv)], inc="ST", incn=16)
    P.raw("sync", lambda e, v=P.ctr["VE"]: e.wait_ge(P.sems["VE"], v))
    P.raw("sync", lambda e, v=P.ctr["SE"]: e.wait_ge(P.sems["SE"], v))
    P.raw("sync", lambda e, v=P.ctr["MM"]: e.wait_ge(P.sems["MM"], v))
    P.raw("sync", lambda e, v=P.ctr["GE"]: e.wait_ge(P.sems["GE"], v))

    def fin(e):
        for (dap, apget) in dbg_list:
            e.dma_start(out=dap[:, :], in_=apget()).then_inc(P.sems["ST"], 16)
            P.ctr["ST"] += 16
        if upto < 7:
            e.dma_start(out=yout[0:128, :], in_=x1[:, 0:D]).then_inc(P.sems["ST"], 16)
            P.ctr["ST"] += 16
        e.wait_ge(P.sems["ST"], P.ctr["ST"])
    P.raw("sync", fin)

    with nc.Block() as block:
        @block.sync
        def _(e):
            for f in P.streams["sync"]:
                f(e)

        @block.gpsimd
        def _(e):
            for f in P.streams["gpsimd"]:
                f(e)

        @block.vector
        def _(e):
            for f in P.streams["vector"]:
                f(e)

        @block.scalar
        def _(e):
            for f in P.streams["scalar"]:
                f(e)

        @block.tensor
        def _(e):
            for f in P.streams["tensor"]:
                f(e)

    es.close()
    return nc


def make_consts(core_id):
    import ml_dtypes
    i = np.arange(128, dtype=np.float32)
    return {
        "identb": np.eye(128).astype(ml_dtypes.bfloat16),
        "ident8": np.eye(128).astype(ml_dtypes.float8_e4m3fn),
        "bciota": np.ascontiguousarray(np.tile(i[None, :], (128, 1))),
        "difft": np.ascontiguousarray(i[None, :] - i[:, None]),
        "iotac": np.ascontiguousarray((i - 128.0)[:, None]),
        "dvecbc": np.ascontiguousarray(np.tile(
            np.where(np.arange(8) < core_id,
                     1024.0 * (core_id - 1 - np.arange(8)),
                     1e9).astype(np.float32)[None, :], (128, 1))),
    }


def kernel(L0_attn_x, L0_attn_Wqkv, L0_attn_bqkv, L0_attn_Wout, L0_attn_bout,
           L0_attn_slopes, L0_ffn_w_fc, L0_ffn_w_proj):
    x = np.ascontiguousarray(np.asarray(L0_attn_x, dtype=np.float32).reshape(8192, 2048))
    wqkv = np.ascontiguousarray(np.asarray(L0_attn_Wqkv, dtype=np.float32))
    bq = np.asarray(L0_attn_bqkv, dtype=np.float32)
    wout_a = np.ascontiguousarray(np.asarray(L0_attn_Wout, dtype=np.float32))
    bo = np.asarray(L0_attn_bout, dtype=np.float32)
    wfc = np.ascontiguousarray(np.asarray(L0_ffn_w_fc, dtype=np.float32))
    wpr = np.ascontiguousarray(np.asarray(L0_ffn_w_proj, dtype=np.float32))
    slopes_t = np.ascontiguousarray(
        np.tile(np.asarray(L0_attn_slopes, dtype=np.float32)[None, :], (128, 1)))
    bq_cols = np.ascontiguousarray(bq[:4096].reshape(32, 128).T)
    nc = build_program(upto=7)
    in_maps = []
    for c in range(NCORES):
        m = dict(
            xin=np.ascontiguousarray(x[c * T:(c + 1) * T]),
            wqkv=wqkv,
            bq_cols=bq_cols,
            bq_vrow=np.ascontiguousarray(bq[4096:].reshape(1, 2048)),
            wout=wout_a,
            bo_row=np.ascontiguousarray(bo.reshape(1, 2048)),
            slopes=slopes_t,
            wfc_sh=np.ascontiguousarray(wfc[c * (FF // 8):(c + 1) * (FF // 8)]),
            wpr_sh=np.ascontiguousarray(wpr[:, c * (FF // 8):(c + 1) * (FF // 8)]),
        )
        m.update(make_consts(c))
        in_maps.append(m)
    res = run_bass_kernel_spmd(nc, in_maps, core_ids=list(range(NCORES)))
    out = np.concatenate([r["yout"] for r in res.results], axis=0)
    return np.ascontiguousarray(out.reshape(1, 8192, 2048))
